# version 05
# baseline (speedup 1.0000x reference)
import math
import contextlib
import numpy as np
import ml_dtypes
import concourse.bass as bass
import concourse.mybir as mybir
from concourse.bass_utils import run_bass_kernel_spmd

F32 = mybir.dt.float32
BF16 = mybir.dt.bfloat16
AF = mybir.ActivationFunctionType
ALU = mybir.AluOpType
AX = mybir.AxisListType
BF = ml_dtypes.bfloat16

ENGINES = ("sync", "tensor", "vector", "scalar", "gpsimd")

B, S, D = 2, 8192, 1024
NTOK = B * S
NCORES = 8
TPC = NTOK // NCORES


class Prog:

    def __init__(self, nc):
        self.nc = nc
        self.ops = {e: [] for e in ENGINES}
        self.cnt = {e: 0 for e in ENGINES}
        self.ndma = {e: 0 for e in ENGINES}
        self.markers = {e: 0 for e in ENGINES}
        self.last_w = {}
        self.readers = {}
        self.hw = {e: {} for e in ENGINES}

    def _marker(self, q, ndma_needed):
        if self.markers[q] >= ndma_needed:
            return 16 * self.markers[q]
        cur = self.ndma[q]
        self.ops[q].append(([(("D", q), 16 * cur)], None, 0))
        self.markers[q] = cur
        self.hw[q][("D", q)] = max(self.hw[q].get(("D", q), 0), 16 * cur)
        return 16 * cur

    def emit(self, eng, fn, reads=(), writes=(), dma=False):
        need = {}

        def add(e2, v, isdma):
            if isdma:
                sem = ("D", e2)
                if e2 != eng:
                    v = self._marker(e2, v // 16)
                else:
                    v = self._marker(e2, v // 16)
            else:
                if e2 == eng and eng == "tensor" and not dma:
                    return
                sem = ("C", e2)
            if need.get(sem, 0) < v:
                need[sem] = v

        for k in reads:
            lw = self.last_w.get(k)
            if lw is not None:
                add(*lw)
        for k in writes:
            lw = self.last_w.get(k)
            if lw is not None:
                add(*lw)
            for (e2, isdma), v in self.readers.get(k, {}).items():
                add(e2, v, isdma)
        waits = []
        for sem in sorted(need):
            v = need[sem]
            if v > self.hw[eng].get(sem, 0):
                waits.append((sem, v))
                self.hw[eng][sem] = v
        if dma:
            self.ndma[eng] += 1
            my = (eng, 16 * self.ndma[eng], True)
        else:
            self.cnt[eng] += 1
            my = (eng, self.cnt[eng], False)
        for k in writes:
            self.last_w[k] = my
            self.readers[k] = {}
        for k in reads:
            self.readers.setdefault(k, {})[(eng, dma)] = my[1]
        self.ops[eng].append((waits, fn, 16 if dma else 1))
        return my

    def barrier(self):
        vals = {}
        for q in ENGINES:
            if self.ndma[q] > 0:
                vals[("D", q)] = self._marker(q, self.ndma[q])
            if self.cnt[q] > 0:
                vals[("C", q)] = self.cnt[q]
        for e in ENGINES:
            waits = []
            for sem, v in sorted(vals.items()):
                if sem == ("C", e):
                    continue
                if v > self.hw[e].get(sem, 0):
                    waits.append((sem, v))
                    self.hw[e][sem] = v
            if waits:
                self.ops[e].append((waits, None, 0))

    def dma(self, eng, out, in_, reads=(), writes=()):
        return self.emit(eng, lambda e, o=out, i=in_: e.dma_start(out=o, in_=i),
                         reads=reads, writes=writes, dma=True)

    def finalize(self, es):
        sems = {}
        for e in ENGINES:
            if self.cnt[e] > 0:
                sems[("C", e)] = es.enter_context(self.nc.semaphore(f"c_{e}"))
            if self.ndma[e] > 0:
                sems[("D", e)] = es.enter_context(self.nc.semaphore(f"d_{e}"))
        final = [(sem, self.cnt[e] if sem[0] == "C" else 16 * self.ndma[e])
                 for e in ENGINES for sem in [("C", e), ("D", e)] if sem in sems]
        block = es.enter_context(self.nc.Block())
        for e in ENGINES:
            ops = self.ops[e]
            if not ops and e != "gpsimd":
                continue

            def mk(e, ops):
                def body(engobj):
                    for waits, fn, amt in ops:
                        for (sem, v) in waits:
                            engobj.wait_ge(sems[sem], v)
                        if fn is None:
                            continue
                        inst = fn(engobj)
                        inst.then_inc(sems[("D" if amt == 16 else "C", e)], amt)
                    if e == "gpsimd":
                        for sem, v in final:
                            if v > 0 and self.hw["gpsimd"].get(sem, 0) < v:
                                engobj.wait_ge(sems[sem], v)
                return body

            getattr(block, e)(mk(e, ops))


def bias_host(b):
    return np.ascontiguousarray(np.asarray(b, dtype=np.float32).reshape(-1, 128).T)


def run_prog(progfn, in_maps, sim=False, **kwargs):
    if sim:
        from concourse import bacc
        from concourse.bass_interp import CoreSim
        nc = bacc.Bacc("TRN2", target_bir_lowering=False, debug=True)
    else:
        nc = bass.Bass("TRN2", target_bir_lowering=False)
    with contextlib.ExitStack() as es:
        p = Prog(nc)
        progfn(nc, p, es)
        p.finalize(es)
    if sim:
        from concourse.bass_interp import CoreSim
        nc.compile()
        s = CoreSim(nc)
        outs = [a.memorylocations[0].name for a in nc.m.functions[0].allocations
                if getattr(a, "kind", None) == "ExternalOutput"]
        for name, arr in in_maps[0].items():
            s.tensor(name)[:] = arr
        s.simulate(check_with_hw=False)
        return [{k: np.asarray(s.tensor(k)) for k in outs}]
    res = run_bass_kernel_spmd(nc, in_maps, core_ids=list(range(len(in_maps))), **kwargs)
    return res.results



class TL:
    def __init__(self, nc, p, es, T=TPC, tt=512):
        self.nc, self.p, self.es, self.T, self.tt = nc, p, es, T, tt
        self.njt = T // tt
        sb = self.sb = lambda n, s, d: es.enter_context(nc.sbuf_tensor(n, s, d))
        ps = self.ps = lambda n, s, d: es.enter_context(nc.psum_tensor(n, s, d))
        self.ones_col = sb("ones_col", [128, 1], F32)
        self.ones_row = sb("ones_row", [1, 128], F32)
        p.emit("vector", lambda e: e.memset(self.ones_col[:], 1.0), writes=[("ones_col", 0)])
        p.emit("vector", lambda e: e.memset(self.ones_row[:], 1.0), writes=[("ones_row", 0)])
        self.sq0 = sb("sq0", [128, tt], F32)
        self.sq1 = sb("sq1", [128, tt], F32)
        self.r0 = sb("r0", [1, tt], F32)
        self.r1 = sb("r1", [1, tt], F32)
        self.r2 = sb("r2", [1, tt], F32)
        self.prx = ps("prx", [1, 512], F32)
        self.prq = ps("prq", [1, 512], F32)
        self.pbS = ps("pbS", [128, 512], F32)
        self.pbC = ps("pbC", [128, 512], F32)
        self.pmm = [ps(f"pmm{i}", [128, 512], F32) for i in range(3)]
        self._mm_rot = 0
        self._uid = 0
        self._consts = {}
        self._cst = sb("consts", [128, 16], F32)

    def const(self, value):
        if value not in self._consts:
            assert len(self._consts) < 16
            k = len(self._consts)
            self.p.emit("vector", lambda e, k=k, v=value: e.memset(self._cst[:, k:k + 1], v),
                        writes=[("const", value)])
            self._consts[value] = k
        k = self._consts[value]
        return self._cst[:, k:k + 1]

    def uid(self):
        self._uid += 1
        return self._uid

    def sbuf_fm(self, name, rows, T, dtype):
        assert rows % 128 == 0
        return self.sb(name, [128, rows // 128, T], dtype)

    def load_fm(self, name, hbm_ap, rows, T, dtype, eng="sync"):
        t = self.sbuf_fm(name, rows, T, dtype)
        nchunk = rows // 128
        src = hbm_ap.rearrange("(c p) t -> p c t", p=128)
        self.p.dma(eng, t[:, :, :], src, writes=[(name, c) for c in range(nchunk)])
        return t

    def load_weight_bf(self, name, hbm_ap, kin, nfo, stage_cols=None, eng="sync"):
        nchunk = kin // 128
        wb = self.sb(name + "_bf", [128, nchunk, nfo], BF16)
        stg = [self.sb(f"{name}_stg{i}", [128, nfo], F32) for i in range(2)]
        src = hbm_ap.rearrange("(c p) f -> c p f", p=128)
        for c in range(nchunk):
            s = stg[c % 2]
            skey = (name + "_stg", c % 2)
            self.p.dma(eng, s[:, :], src[c], reads=[], writes=[skey])
            self.p.emit("vector", lambda e, s=s, c=c: e.tensor_copy(out=wb[:, c, :], in_=s[:, :]),
                        reads=[skey], writes=[(name, c)])
        return wb

    def load_rows_f32(self, name, hbm_ap, n):
        assert n % 128 == 0
        t = self.sb(name, [128, n // 128], F32)
        self.p.dma("sync", t[:, :], hbm_ap, writes=[(name, 0)])
        return t

    def emit_ln(self, xsb, xname, nchunk, out_bf, oname, out_f32=None, o32name=None,
               y32_tile=None, y32name=None, tile_cb=None, rms_w=None, eps=1e-5):
        p, tt = self.p, self.tt
        nd = nchunk * 128
        for j in range(self.njt):
            js = slice(j * tt, (j + 1) * tt)
            u = self.uid()
            for c in range(nchunk):
                p.emit("tensor", lambda e, c=c, js=js: e.matmul(
                    out=self.prx[:, :], lhsT=self.ones_col[:], rhs=xsb[:, c, js],
                    start=(c == 0), stop=(c == nchunk - 1)),
                    reads=[("ones_col", 0), (xname, c)], writes=[("prx",)])
            for c in range(nchunk):
                sq = self.sq0 if c % 2 == 0 else self.sq1
                sqk = ("sq", c % 2)
                p.emit("scalar", lambda e, c=c, sq=sq, js=js: e.activation(
                    out=sq[:, :], in_=xsb[:, c, js], func=AF.Square),
                    reads=[(xname, c)], writes=[sqk])
                p.emit("tensor", lambda e, c=c, sq=sq: e.matmul(
                    out=self.prq[:, :], lhsT=self.ones_col[:], rhs=sq[:, :],
                    start=(c == 0), stop=(c == nchunk - 1)),
                    reads=[("ones_col", 0), sqk], writes=[("prq",)])
            p.emit("vector", lambda e: e.tensor_scalar_mul(out=self.r0[:, :], in0=self.prx[:, :], scalar1=-1.0 / nd),
                   reads=[("prx",)], writes=[("r0",)])
            p.emit("vector", lambda e: e.tensor_scalar_mul(out=self.r1[:, :], in0=self.prq[:, :], scalar1=1.0 / nd),
                   reads=[("prq",)], writes=[("r1",)])
            p.emit("vector", lambda e: e.tensor_mul(out=self.r2[:, :], in0=self.r0[:, :], in1=self.r0[:, :]),
                   reads=[("r0",)], writes=[("r2",)])
            p.emit("vector", lambda e: e.tensor_sub(out=self.r1[:, :], in0=self.r1[:, :], in1=self.r2[:, :]),
                   reads=[("r1",), ("r2",)], writes=[("r1",)])
            epsap = self.const(eps)
            p.emit("scalar", lambda e: e.activation(out=self.r1[:, :], in_=self.r1[:, :], func=AF.Sqrt,
                                                    bias=epsap[0:1, :]),
                   reads=[("r1",), ("const", eps)], writes=[("r1",)])
            p.emit("vector", lambda e: e.reciprocal(out=self.r1[:, :], in_=self.r1[:, :]),
                   reads=[("r1",)], writes=[("r1",)])
            p.emit("vector", lambda e: e.tensor_mul(out=self.r2[:, :], in0=self.r0[:, :], in1=self.r1[:, :]),
                   reads=[("r0",), ("r1",)], writes=[("r2",)])
            p.emit("tensor", lambda e: e.matmul(out=self.pbS[:, :], lhsT=self.ones_row[:], rhs=self.r1[:, :],
                                                start=True, stop=True),
                   reads=[("ones_row", 0), ("r1",)], writes=[("pbS",)])
            p.emit("tensor", lambda e: e.matmul(out=self.pbC[:, :], lhsT=self.ones_row[:], rhs=self.r2[:, :],
                                                start=True, stop=True),
                   reads=[("ones_row", 0), ("r2",)], writes=[("pbC",)])
            for c in range(nchunk):
                sq = self.sq0 if c % 2 == 0 else self.sq1
                sqk = ("sq", c % 2)
                p.emit("vector", lambda e, c=c, sq=sq, js=js: e.tensor_mul(out=sq[:, :], in0=xsb[:, c, js], in1=self.pbS[:, :]),
                       reads=[(xname, c), ("pbS",)], writes=[sqk])
                if out_bf is not None:
                    p.emit("vector", lambda e, c=c, sq=sq, js=js: e.tensor_add(out=out_bf[:, c, js], in0=sq[:, :], in1=self.pbC[:, :]),
                           reads=[sqk, ("pbC",)], writes=[(oname, c)])
                if out_f32 is not None:
                    p.emit("vector", lambda e, c=c, sq=sq, js=js: e.tensor_add(out=out_f32[:, c, js], in0=sq[:, :], in1=self.pbC[:, :]),
                           reads=[sqk, ("pbC",)], writes=[(o32name, c)])
                if y32_tile is not None:
                    p.emit("vector", lambda e, c=c, sq=sq: e.tensor_add(out=y32_tile[:, c, :], in0=sq[:, :], in1=self.pbC[:, :]),
                           reads=[sqk, ("pbC",)], writes=[(y32name, c)])
            if tile_cb is not None:
                tile_cb(j)

    def emit_mm(self, act, aname, ncin, w, wname, nfo, epilogue, jlist=None):
        p, tt = self.p, self.tt
        nfob = nfo // 128
        for j in (range(self.njt) if jlist is None else jlist):
            js = slice(j * tt, (j + 1) * tt)
            for fo in range(nfob):
                pm = self.pmm[self._mm_rot % len(self.pmm)]
                pmkey = ("pmm", self._mm_rot % len(self.pmm))
                self._mm_rot += 1
                for c in range(ncin):
                    p.emit("tensor", lambda e, c=c, fo=fo, pm=pm, js=js: e.matmul(
                        out=pm[:, :tt], lhsT=w[:, c, fo * 128:(fo + 1) * 128], rhs=act[:, c, js],
                        start=(c == 0), stop=(c == ncin - 1)),
                        reads=[(wname, c), (aname, c)], writes=[pmkey])
                epilogue(fo, j, pm[:, :tt], pmkey)



def mk_l0a_qkv(T=TPC, dbg=False):
    def prog(nc, p, es):
        X = nc.dram_tensor("x_fm", [D, T], F32, kind="ExternalInput").ap()
        W = nc.dram_tensor("w_qkv", [D, 3 * D], F32, kind="ExternalInput").ap()
        Bq = nc.dram_tensor("b_qkv", [128, (3 * D) // 128], F32, kind="ExternalInput").ap()
        O = nc.dram_tensor("qkv_fm", [3 * D, T], BF16, kind="ExternalOutput").ap()
        tl = TL(nc, p, es, T=T)
        xsb = tl.load_fm("x", X, D, T, F32)
        wsb = tl.load_weight_bf("w", W, D, 3 * D)
        bsb = tl.load_rows_f32("b", Bq, 3 * D)
        ybf = tl.sbuf_fm("y", D, T, BF16)
        tl.emit_ln(xsb, "x", 8, ybf, "y")
        if dbg:
            Yo = nc.dram_tensor("y_dbg", [D, T], BF16, kind="ExternalOutput").ap()
            p.dma("gpsimd", Yo.rearrange("(c p) t -> p c t", p=128), ybf[:, :, :],
                  reads=[("y", c) for c in range(8)])
        eo = [tl.sb(f"eo{i}", [128, 512], BF16) for i in range(3)]
        Ore = O.rearrange("(fo p) t -> fo p t", p=128)
        ecnt = [0]

        def epi(fo, j, pmap, pkey):
            js = slice(j * tl.tt, (j + 1) * tl.tt)
            i = ecnt[0] % 3
            ecnt[0] += 1
            ek = ("eo", i)
            p.emit("vector", lambda e: e.tensor_scalar_add(
                out=eo[i][:, :], in0=pmap, scalar1=bsb[:, fo:fo + 1]),
                reads=[pkey, ("b", 0)], writes=[ek])
            p.dma("gpsimd", Ore[fo][:, js], eo[i][:, :], reads=[ek])

        tl.emit_mm(ybf, "y", 8, wsb, "w", 3 * D, epi)
    return prog





def mk_l0_ffn(T=TPC, dbg=False):
    E, HH = 16, 512

    def prog(nc, p, es):
        X = nc.dram_tensor("x_fm", [D, T], F32, kind="ExternalInput").ap()
        WGT = nc.dram_tensor("wg_t", [D, E], F32, kind="ExternalInput").ap()
        W1 = nc.dram_tensor("w1", [E, D, 2 * HH], F32, kind="ExternalInput").ap()
        W2 = nc.dram_tensor("w2", [E, HH, D], F32, kind="ExternalInput").ap()
        WS1 = nc.dram_tensor("ws1", [D, 2048], F32, kind="ExternalInput").ap()
        WS2 = nc.dram_tensor("ws2", [1024, D], F32, kind="ExternalInput").ap()
        IDF = nc.dram_tensor("ident_f32", [128, 128], F32, kind="ExternalInput").ap()
        O = nc.dram_tensor("xout_fm", [D, T], F32, kind="ExternalOutput").ap()
        if dbg:
            OCW = nc.dram_tensor("cw_dbg", [128, T // 128, E], F32, kind="ExternalOutput").ap()

        tl = TL(nc, p, es, T=T)
        sb, ps = tl.sb, tl.ps
        njt = tl.njt
        nsub = T // 128

        idsb = sb("idsb", [128, 128], F32)
        p.dma("sync", idsb[:, :], IDF, writes=[("id",)])
        wgt = sb("wgt", [128, 8, E], F32)
        p.dma("sync", wgt[:, :, :], WGT.rearrange("(c p) e -> p c e", p=128),
              writes=[("wgt",)])
        ybf = tl.sbuf_fm("y", D, T, BF16)
        zbf = tl.sbuf_fm("z", D, T, BF16)
        cw_all = sb("cw_all", [128, nsub, E], F32)
        lg = sb("lg", [16, 512], F32)
        gs = [sb(f"gs{i}", [128, E], F32) for i in range(5)]
        gr = [sb(f"gr{i}", [128, 1], F32) for i in range(4)]

        def gate_tile(j, y32t):
            plg = tl.pmm[0]
            for c in range(8):
                p.emit("tensor", lambda e, c=c: e.matmul(
                    out=plg[0:E, :], lhsT=wgt[:, c, :], rhs=y32t[:, c, :],
                    start=(c == 0), stop=(c == 7)),
                    reads=[("wgt",), ("y32", c)], writes=[("pmm", 0)])
            p.emit("vector", lambda e: e.tensor_copy(out=lg[:, :], in_=plg[0:E, :]),
                   reads=[("pmm", 0)], writes=[("lg",)])
            for i in range(4):
                ti = j * 4 + i
                ltm, ptm, g1, l2, cwtm = gs
                m1, nm1, ssum, m2 = gr
                ptf = tl.pmm[1]
                p.emit("tensor", lambda e, i=i: e.transpose(
                    out=ptf[0:128, 0:E], in_=lg[:, i * 128:(i + 1) * 128], identity=idsb[0:E, 0:E]),
                    reads=[("lg",), ("id",)], writes=[("pmm", 1)])
                p.emit("vector", lambda e: e.tensor_copy(out=ltm[:, :], in_=ptf[0:128, 0:E]),
                       reads=[("pmm", 1)], writes=[("ltm",)])
                p.emit("vector", lambda e: e.reduce_max(out=m1[:, :], in_=ltm[:, :], axis=AX.X),
                       reads=[("ltm",)], writes=[("m1",)])
                p.emit("vector", lambda e: e.tensor_scalar_mul(out=nm1[:, :], in0=m1[:, :], scalar1=-1.0),
                       reads=[("m1",)], writes=[("nm1",)])
                p.emit("scalar", lambda e: e.activation(out=ptm[:, :], in_=ltm[:, :], func=AF.Exp,
                                                        bias=nm1[:, 0:1], accum_out=ssum[:, 0:1]),
                       reads=[("ltm",), ("nm1",)], writes=[("ptm",), ("ssum",)])
                p.emit("vector", lambda e: e.reciprocal(out=ssum[:, :], in_=ssum[:, :]),
                       reads=[("ssum",)], writes=[("ssum",)])
                p.emit("vector", lambda e: e.tensor_scalar_mul(out=ptm[:, :], in0=ptm[:, :], scalar1=ssum[:, 0:1]),
                       reads=[("ptm",), ("ssum",)], writes=[("ptm",)])
                p.emit("vector", lambda e: e.tensor_scalar(out=g1[:, :], in0=ltm[:, :], scalar1=m1[:, 0:1],
                                                           scalar2=None, op0=ALU.is_ge),
                       reads=[("ltm",), ("m1",)], writes=[("g1",)])
                p.emit("vector", lambda e: e.scalar_tensor_tensor(out=l2[:, :], in0=g1[:, :], scalar=-1e30,
                                                                  in1=ltm[:, :], op0=ALU.mult, op1=ALU.add),
                       reads=[("g1",), ("ltm",)], writes=[("l2",)])
                p.emit("vector", lambda e: e.reduce_max(out=m2[:, :], in_=l2[:, :], axis=AX.X),
                       reads=[("l2",)], writes=[("m2",)])
                p.emit("vector", lambda e: e.tensor_scalar(out=g1[:, :], in0=ltm[:, :], scalar1=m2[:, 0:1],
                                                           scalar2=None, op0=ALU.is_ge),
                       reads=[("ltm",), ("m2",)], writes=[("g1",)])
                p.emit("vector", lambda e, ti=ti: e.tensor_mul(out=cw_all[:, ti, :], in0=ptm[:, :], in1=g1[:, :]),
                       reads=[("ptm",), ("g1",)], writes=[("cw", ti)])

        with contextlib.ExitStack() as es2:
            tl2 = contextlib.nullcontext()
            xsb = es2.enter_context(nc.sbuf_tensor("x", [128, 8, T], F32))
            p.dma("sync", xsb[:, :, :], X.rearrange("(c p) t -> p c t", p=128),
                  writes=[("x", c) for c in range(8)])
            y32t = es2.enter_context(nc.sbuf_tensor("y32t", [128, 8, 512], F32))
            tl.emit_ln(xsb, "x", 8, ybf, "y", y32_tile=y32t, y32name="y32",
                       tile_cb=lambda j: gate_tile(j, y32t))
        p.barrier()
        if dbg:
            p.dma("gpsimd", OCW, cw_all[:, :, :], reads=[("cw", ti) for ti in range(nsub)])

        h = [tl.sbuf_fm(f"h{i}", HH, T, BF16) for i in range(2)]
        stg = [sb(f"stgw{i}", [128, 2048], F32) for i in range(2)]
        w1b = [sb(f"w1b{i}", [128, 8, 2 * HH], BF16) for i in range(2)]
        w2b = [sb(f"w2b{i}", [128, 4, D], BF16) for i in range(2)]
        hs = sb("hs", [128, 8, 512], BF16)
        cwb = [sb(f"cwb{i}", [128, 512], BF16) for i in range(2)]
        sg = [sb(f"sg{i}", [128, 512], F32) for i in range(2)]
        hwt = [sb(f"hwt{i}", [128, 512], BF16) for i in range(2)]
        nst = [0]

        def load_cast(dram_src, dst, dstkey, nchunk, ncols):
            for c in range(nchunk):
                s = stg[nst[0] % 2]
                skey = ("stgw", nst[0] % 2)
                nst[0] += 1
                p.dma("sync", s[:, 0:ncols], dram_src[c], writes=[skey])
                p.emit("vector", lambda e, s=s, c=c, ncols=ncols, dst=dst: e.tensor_copy(
                    out=dst[:, c, 0:ncols], in_=s[:, 0:ncols]),
                    reads=[skey], writes=[(dstkey, c)])

        def up_and_down(eidx, w1cur, w1key, w2cur, w2key, hcur, hkey, weighted, nuc, first):
            for j in range(njt):
                js = slice(j * 512, (j + 1) * 512)
                if weighted:
                    pcw = tl.pmm[2]
                    for i in range(4):
                        ti = j * 4 + i
                        p.emit("tensor", lambda e, ti=ti, i=i, eidx=eidx: e.transpose(
                            out=pcw[:, i * 128:(i + 1) * 128],
                            in_=cw_all[:, ti, eidx:eidx + 1].broadcast_to([128, 128]),
                            identity=idsb[:, :]),
                            reads=[("cw", ti), ("id",)], writes=[("pmm", 2)])
                    cb = cwb[j % 2]
                    cbkey = ("cwb", j % 2)
                    p.emit("vector", lambda e, cb=cb: e.tensor_copy(out=cb[:, :], in_=pcw[:, :]),
                           reads=[("pmm", 2)], writes=[cbkey])
                for blk in range(nuc):
                    pu, pukey = tl.pmm[0], ("pmm", 0)
                    pg, pgkey = tl.pmm[1], ("pmm", 1)
                    for c in range(8):
                        p.emit("tensor", lambda e, c=c, blk=blk, js=js, w1cur=w1cur, pu=pu: e.matmul(
                            out=pu[:, :], lhsT=w1cur[:, c, blk * 128:(blk + 1) * 128],
                            rhs=ybf[:, c, js], start=(c == 0), stop=(c == 7)),
                            reads=[(w1key, c), ("y", c)], writes=[pukey])
                    for c in range(8):
                        p.emit("tensor", lambda e, c=c, blk=blk, js=js, w1cur=w1cur, pg=pg, nuc=nuc: e.matmul(
                            out=pg[:, :], lhsT=w1cur[:, c, (nuc + blk) * 128:(nuc + blk + 1) * 128],
                            rhs=ybf[:, c, js], start=(c == 0), stop=(c == 7)),
                            reads=[(w1key, c), ("y", c)], writes=[pgkey])
                    sgc, sgkey = sg[blk % 2], ("sg", blk % 2)
                    p.emit("scalar", lambda e, sgc=sgc, pg=pg: e.activation(out=sgc[:, :], in_=pg[:, :], func=AF.Silu),
                           reads=[pgkey], writes=[sgkey])
                    if weighted:
                        ht, htkey = hwt[blk % 2], ("hwt", blk % 2)
                        p.emit("vector", lambda e, ht=ht, sgc=sgc, pu=pu: e.tensor_mul(out=ht[:, :], in0=pu[:, :], in1=sgc[:, :]),
                               reads=[pukey, sgkey], writes=[htkey])
                        p.emit("gpsimd", lambda e, ht=ht, blk=blk, js=js, hcur=hcur, cb=cb: e.tensor_mul(
                            out=hcur[:, blk, js], in0=ht[:, :], in1=cb[:, :]),
                            reads=[htkey, cbkey], writes=[(hkey, blk, j)])
                    else:
                        p.emit("vector", lambda e, blk=blk, js=js, hcur=hcur, sgc=sgc, pu=pu: e.tensor_mul(
                            out=hcur[:, blk, js], in0=pu[:, :], in1=sgc[:, :]),
                               reads=[pukey, sgkey], writes=[(hkey, blk, j)])
                for fo in range(8):
                    pz, pzkey = tl.pmm[(fo % 2)], ("pmm", fo % 2)
                    for blk in range(nuc):
                        p.emit("tensor", lambda e, blk=blk, fo=fo, js=js, w2cur=w2cur, hcur=hcur, pz=pz, j=j: e.matmul(
                            out=pz[:, :], lhsT=w2cur[:, blk, fo * 128:(fo + 1) * 128],
                            rhs=hcur[:, blk, js], start=(blk == 0), stop=(blk == nuc - 1)),
                            reads=[(w2key, blk), (hkey, blk, j)], writes=[pzkey])
                    if first:
                        p.emit("vector", lambda e, fo=fo, js=js, pz=pz: e.tensor_copy(out=zbf[:, fo, js], in_=pz[:, :]),
                               reads=[pzkey], writes=[("z", fo, j)])
                    else:
                        p.emit("vector", lambda e, fo=fo, js=js, pz=pz: e.tensor_add(out=zbf[:, fo, js], in0=zbf[:, fo, js], in1=pz[:, :]),
                               reads=[pzkey, ("z", fo, j)], writes=[("z", fo, j)])

        for eidx in range(E):
            b = eidx % 2
            load_cast(W1[eidx].rearrange("(c p) f -> c p f", p=128), w1b[b], f"w1b{b}", 8, 2 * HH)
            load_cast(W2[eidx].rearrange("(c p) f -> c p f", p=128), w2b[b], f"w2b{b}", 4, D)
            up_and_down(eidx, w1b[b], f"w1b{b}", w2b[b], f"w2b{b}", h[b], f"h{b}",
                        weighted=True, nuc=4, first=(eidx == 0))

        load_cast(WS1[:, 0:1024].rearrange("(c p) f -> c p f", p=128), w1b[0], "w1b0", 8, 1024)
        load_cast(WS1[:, 1024:2048].rearrange("(c p) f -> c p f", p=128), w1b[1], "w1b1", 8, 1024)
        load_cast(WS2[0:512].rearrange("(c p) f -> c p f", p=128), w2b[0], "w2b0", 4, D)
        load_cast(WS2[512:1024].rearrange("(c p) f -> c p f", p=128), w2b[1], "w2b1", 4, D)
        for j in range(njt):
            js = slice(j * 512, (j + 1) * 512)
            for blk in range(8):
                pu, pukey = tl.pmm[0], ("pmm", 0)
                pg, pgkey = tl.pmm[1], ("pmm", 1)
                for c in range(8):
                    p.emit("tensor", lambda e, c=c, blk=blk, js=js, pu=pu: e.matmul(
                        out=pu[:, :], lhsT=w1b[0][:, c, blk * 128:(blk + 1) * 128],
                        rhs=ybf[:, c, js], start=(c == 0), stop=(c == 7)),
                        reads=[("w1b0", c), ("y", c)], writes=[pukey])
                for c in range(8):
                    p.emit("tensor", lambda e, c=c, blk=blk, js=js, pg=pg: e.matmul(
                        out=pg[:, :], lhsT=w1b[1][:, c, blk * 128:(blk + 1) * 128],
                        rhs=ybf[:, c, js], start=(c == 0), stop=(c == 7)),
                        reads=[("w1b1", c), ("y", c)], writes=[pgkey])
                sgc, sgkey = sg[blk % 2], ("sg", blk % 2)
                p.emit("scalar", lambda e, sgc=sgc, pg=pg: e.activation(out=sgc[:, :], in_=pg[:, :], func=AF.Silu),
                       reads=[pgkey], writes=[sgkey])
                p.emit("vector", lambda e, blk=blk, sgc=sgc, pu=pu: e.tensor_mul(out=hs[:, blk, :], in0=pu[:, :], in1=sgc[:, :]),
                       reads=[pukey, sgkey], writes=[("hs", blk)])
            for fo in range(8):
                pz, pzkey = tl.pmm[fo % 2], ("pmm", fo % 2)
                for blk in range(8):
                    wsel, wselkey = (w2b[0], "w2b0") if blk < 4 else (w2b[1], "w2b1")
                    p.emit("tensor", lambda e, blk=blk, fo=fo, wsel=wsel, pz=pz: e.matmul(
                        out=pz[:, :], lhsT=wsel[:, blk % 4, fo * 128:(fo + 1) * 128],
                        rhs=hs[:, blk, :], start=(blk == 0), stop=(blk == 7)),
                        reads=[(wselkey, blk % 4), ("hs", blk)], writes=[pzkey])
                p.emit("vector", lambda e, fo=fo, js=js, pz=pz: e.tensor_add(out=zbf[:, fo, js], in0=zbf[:, fo, js], in1=pz[:, :]),
                       reads=[pzkey, ("z", fo, j)], writes=[("z", fo, j)])

        xc = [sb(f"xc{i}", [128, T], F32) for i in range(2)]
        Xre = X.rearrange("(c p) t -> c p t", p=128)
        Ore = O.rearrange("(c p) t -> c p t", p=128)
        for fo in range(8):
            b = fo % 2
            xck = ("xc", b)
            p.dma("sync", xc[b][:, :], Xre[fo], writes=[xck])
            p.emit("vector", lambda e, fo=fo, b=b: e.tensor_add(
                out=xc[b][:, :], in0=xc[b][:, :], in1=zbf[:, fo, :]),
                reads=[xck] + [("z", fo, j) for j in range(njt)], writes=[xck])
            p.dma("gpsimd", Ore[fo], xc[b][:, :], reads=[xck])
    return prog


def mk_flash(S_=8192, npair=4, dv=64, alibi=True, krange=None, maskid=None, nmasks=4,
             Sk=None, dbg=False):
    if Sk is None:
        Sk = S_
    NKT = Sk // 128
    NQB = S_ // 512
    NKB = Sk // 512
    if krange is None:
        krange = lambda qt: range(0, qt * 4 + 4)
    if maskid is None:
        maskid = lambda qt, kt: (kt - 4 * qt) if kt - 4 * qt >= 0 else None

    def prog(nc, p, es):
        Q = nc.dram_tensor("qp", [npair, 64, S_], BF16, kind="ExternalInput").ap()
        K_ = nc.dram_tensor("kp", [npair, 64, Sk], BF16, kind="ExternalInput").ap()
        V = nc.dram_tensor("vp", [npair, 128, Sk // 128, dv], BF16, kind="ExternalInput").ap()
        RAMP = nc.dram_tensor("ramp", [npair, 128, NKT], F32, kind="ExternalInput").ap()
        AUGQ = nc.dram_tensor("augq", [npair, 1, S_], F32, kind="ExternalInput").ap()
        MSK = nc.dram_tensor("masks", [nmasks, 128, 512], F32, kind="ExternalInput").ap()
        O = nc.dram_tensor("attnout", [npair, dv, S_], BF16, kind="ExternalOutput").ap()

        sb = lambda n, s, d: es.enter_context(nc.sbuf_tensor(n, s, d))
        ps = lambda n, s, d: es.enter_context(nc.psum_tensor(n, s, d))
        onesr = sb("onesr", [1, 128], F32)
        onesc = sb("onesc", [128, 1], F32)
        p.emit("vector", lambda e: e.memset(onesr[:, :], 1.0), writes=[("onesr",)])
        p.emit("vector", lambda e: e.memset(onesc[:, :], 1.0), writes=[("onesc",)])
        msk = sb("msk", [128, nmasks, 512], F32)
        p.dma("sync", msk[:, :, :], MSK.rearrange("m p q -> p m q"), writes=[("msk",)])

        qh = [sb(f"qh{i}", [65, S_], BF16) for i in range(2)]
        kh = [sb(f"kh{i}", [65, Sk], BF16) for i in range(2)]
        vt = [sb(f"vt{i}", [128, NKT, dv + 1], BF16) for i in range(2)]
        rmp = [sb(f"rmp{i}", [128, NKT], F32) for i in range(2)]
        qr = sb("qr", [1, 512], F32)
        kr = sb("kr", [1, 1], F32)
        sqt = sb("sqt", [64, 512], F32)
        kms = sb("kms", [1, 1], F32)
        outsb = [sb(f"outsb{i}", [dv, 512], BF16) for i in range(2)]
        rr = [sb(f"rr{i}", [1, 512], F32) for i in range(2)]
        rrb = [sb(f"rrb{i}", [dv, 512], F32) for i in range(2)]
        aqt = sb("aqt", [1, 512], F32)
        NPA = 4
        psA = [ps(f"psA{i}", [128, 512], F32) for i in range(NPA)]
        psO = [ps(f"psO{i}", [dv + 1, 512], F32) for i in range(2)]
        psR = ps("psR", [1, 512], F32)
        psB = ps("psB", [dv, 512], F32)
        pT = [sb(f"pT{i}", [128, 512], BF16) for i in range(NPA)]

        def load_pair(pi, b):
            kq = ("qh", b)
            kk = ("kh", b)
            kv = ("vt", b)
            p.dma("sync", qh[b][0:64, :], Q[pi], writes=[kq])
            p.dma("sync", kh[b][0:64, :], K_[pi], writes=[kk])
            p.dma("sync", vt[b][:, :, 0:dv], V[pi], writes=[kv])
            p.dma("sync", rmp[b][:, :], RAMP[pi], writes=[("rmp", b)])
            p.emit("vector", lambda e, b=b: e.memset(vt[b][:, :, dv:dv + 1], 1.0),
                   reads=[kv], writes=[kv])
            p.emit("vector", lambda e, b=b: e.memset(kh[b][64:65, :], 1.0),
                   reads=[kk], writes=[kk])
            p.emit("vector", lambda e, b=b: e.tensor_scalar_mul(out=qh[b][0:64, :], in0=qh[b][0:64, :], scalar1=0.125),
                   reads=[kq], writes=[kq])
            for t in range(NKB):
                js = slice(t * 512, (t + 1) * 512)
                p.emit("scalar", lambda e, b=b, js=js: e.activation(out=sqt[:, :], in_=kh[b][0:64, js], func=AF.Square),
                       reads=[kk], writes=[("sqt",)])
                p.emit("tensor", lambda e: e.matmul(out=psR[:, :], lhsT=onesc[0:64, :], rhs=sqt[:, :],
                                                    start=True, stop=True),
                       reads=[("sqt",), ("onesc",)], writes=[("psR",)])
                p.emit("vector", lambda e: e.reduce_max(out=kr[:, :], in_=psR[:, :], axis=AX.X),
                       reads=[("psR",)], writes=[("kr",)])
                if t == 0:
                    p.emit("vector", lambda e: e.tensor_copy(out=kms[:, :], in_=kr[:, :]),
                           reads=[("kr",)], writes=[("kms",)])
                else:
                    p.emit("vector", lambda e: e.tensor_max(out=kms[:, :], in0=kms[:, :], in1=kr[:, :]),
                           reads=[("kr",), ("kms",)], writes=[("kms",)])
            p.emit("scalar", lambda e: e.activation(out=kms[:, :], in_=kms[:, :], func=AF.Sqrt),
                   reads=[("kms",)], writes=[("kms",)])
            for t in range(NQB):
                js = slice(t * 512, (t + 1) * 512)
                p.emit("scalar", lambda e, b=b, js=js: e.activation(out=sqt[:, :], in_=qh[b][0:64, js], func=AF.Square),
                       reads=[kq], writes=[("sqt",)])
                p.emit("tensor", lambda e: e.matmul(out=psR[:, :], lhsT=onesc[0:64, :], rhs=sqt[:, :],
                                                    start=True, stop=True),
                       reads=[("sqt",), ("onesc",)], writes=[("psR",)])
                p.emit("scalar", lambda e: e.activation(out=qr[:, :], in_=psR[:, :], func=AF.Sqrt),
                       reads=[("psR",)], writes=[("qr",)])
                p.emit("vector", lambda e: e.tensor_scalar_mul(out=qr[:, :], in0=qr[:, :], scalar1=kms[0:1, 0:1]),
                       reads=[("qr",), ("kms",)], writes=[("qr",)])
                p.dma("sync", aqt[:, :], AUGQ[pi][:, js], writes=[("aqt",)])
                p.emit("vector", lambda e, b=b, js=js: e.tensor_sub(out=qh[b][64:65, js], in0=aqt[:, :], in1=qr[:, :]),
                       reads=[("aqt",), ("qr",)], writes=[kq])

        pending = [None]

        def do_pair(pi, b):
            kq, kk, kv = ("qh", b), ("kh", b), ("vt", b)
            LA = 2
            for qt in range(NQB):
                js = slice(qt * 512, (qt + 1) * 512)
                kts = list(krange(qt))
                n = len(kts)
                ob = qt % 2

                def emit_qk(i, qt=qt, js=js, kts=kts):
                    kt = kts[i]
                    ab = i % NPA
                    pa, pakey = psA[ab], ("psA", ab)
                    p.emit("tensor", lambda e, kt=kt, js=js, b=b, pa=pa: e.matmul(
                        out=pa[:, :], lhsT=kh[b][:, kt * 128:(kt + 1) * 128], rhs=qh[b][:, js],
                        start=True, stop=True),
                        reads=[kk, kq], writes=[pakey])
                    mid = maskid(qt, kt)
                    if mid is not None:
                        p.emit("vector", lambda e, mid=mid, pa=pa: e.tensor_add(
                            out=pa[:, :], in0=pa[:, :], in1=msk[:, mid, :]),
                            reads=[pakey, ("msk",)], writes=[pakey])
                    pt, ptkey = pT[ab], ("pT", ab)
                    p.emit("scalar", lambda e, kt=kt, b=b, pa=pa, pt=pt: e.activation(
                        out=pt[:, :], in_=pa[:, :], func=AF.Exp, bias=rmp[b][:, kt:kt + 1]),
                        reads=[pakey, ("rmp", b)], writes=[ptkey])

                def emit_av(i, n=n, ob=ob, kts=kts):
                    kt = kts[i]
                    ab = i % NPA
                    pt, ptkey = pT[ab], ("pT", ab)
                    p.emit("tensor", lambda e, kt=kt, b=b, pt=pt, first=(i == 0), last=(i == n - 1), ob=ob: e.matmul(
                        out=psO[ob][:, :], lhsT=vt[b][:, kt, :], rhs=pt[:, :],
                        start=first, stop=last),
                        reads=[kv, ptkey], writes=[("psO", ob)])

                for i in range(n):
                    emit_qk(i)
                    if i == LA and pending[0] is not None:
                        pending[0]()
                        pending[0] = None
                    if i >= LA:
                        emit_av(i - LA)
                if pending[0] is not None:
                    pending[0]()
                    pending[0] = None
                for i in range(max(0, n - LA), n):
                    emit_av(i)

                def epi(pi=pi, qt=qt, ob=ob):
                    p.emit("scalar", lambda e: e.activation(out=rr[ob][:, :], in_=psO[ob][dv:dv + 1, :],
                                                            func=AF.Copy),
                           reads=[("psO", ob)], writes=[("rr", ob)])
                    p.emit("tensor", lambda e: e.matmul(out=psB[:, :], lhsT=onesr[:, 0:dv], rhs=rr[ob][:, :],
                                                        start=True, stop=True),
                           reads=[("rr", ob), ("onesr",)], writes=[("psB",)])
                    p.emit("vector", lambda e: e.reciprocal(out=rrb[ob][:, :], in_=psB[:, :]),
                           reads=[("psB",)], writes=[("rrb", ob)])
                    p.emit("vector", lambda e: e.tensor_mul(
                        out=outsb[ob][:, :], in0=psO[ob][0:dv, :], in1=rrb[ob][:, :]),
                        reads=[("psO", ob), ("rrb", ob)], writes=[("outsb", ob)])
                    p.dma("gpsimd", O[pi][:, qt * 512:(qt + 1) * 512], outsb[ob][:, :],
                          reads=[("outsb", ob)])
                pending[0] = epi

        load_pair(0, 0)
        if npair > 1:
            load_pair(1, 1)
        for pi in range(npair):
            do_pair(pi, pi % 2)
            if pi + 2 < npair:
                load_pair(pi + 2, (pi + 2) % 2)
        if pending[0] is not None:
            pending[0]()
            pending[0] = None
    return prog


def mk_resid_proj(T=TPC, din=1024, use_bias=True):
    nck = din // 128

    def prog(nc, p, es):
        X = nc.dram_tensor("x_fm", [D, T], F32, kind="ExternalInput").ap()
        A = nc.dram_tensor("a_fm", [din, T], BF16, kind="ExternalInput").ap()
        W = nc.dram_tensor("w", [din, D], F32, kind="ExternalInput").ap()
        if use_bias:
            Bv = nc.dram_tensor("b", [128, D // 128], F32, kind="ExternalInput").ap()
        O = nc.dram_tensor("xout_fm", [D, T], F32, kind="ExternalOutput").ap()
        tl = TL(nc, p, es, T=T)
        xsb = tl.load_fm("x", X, D, T, F32)
        asb = tl.load_fm("a", A, din, T, BF16)
        wsb = tl.load_weight_bf("w", W, din, D)
        bsb = tl.load_rows_f32("bo", Bv, D) if use_bias else None
        ost = [tl.sb(f"ost{i}", [128, 512], F32) for i in range(3)]
        cnt = [0]
        Ore = O.rearrange("(c p) t -> c p t", p=128)

        def epi(fo, j, pmap, pkey):
            js = slice(j * tl.tt, (j + 1) * tl.tt)
            i = cnt[0] % 3
            cnt[0] += 1
            ok = ("ost", i)
            if use_bias:
                p.emit("vector", lambda e, i=i, fo=fo: e.tensor_scalar_add(
                    out=ost[i][:, :], in0=pmap, scalar1=bsb[:, fo:fo + 1]),
                    reads=[pkey, ("bo", 0)], writes=[ok])
                p.emit("vector", lambda e, i=i, fo=fo, js=js: e.tensor_add(
                    out=ost[i][:, :], in0=ost[i][:, :], in1=xsb[:, fo, js]),
                    reads=[ok, ("x", fo)], writes=[ok])
            else:
                p.emit("vector", lambda e, i=i, fo=fo, js=js: e.tensor_add(
                    out=ost[i][:, :], in0=pmap, in1=xsb[:, fo, js]),
                    reads=[pkey, ("x", fo)], writes=[ok])
            p.dma("gpsimd", Ore[fo][:, js], ost[i][:, :], reads=[ok])

        tl.emit_mm(asb, "a", nck, wsb, "w", D, epi)
    return prog


def mk_l0b_scan(S_=8192, npair=4, dbg=False):
    C = 128
    NCH = S_ // C
    P4 = npair

    def prog(nc, p, es):
        Q = nc.dram_tensor("qs", [P4, 64, S_], F32, kind="ExternalInput").ap()
        K_ = nc.dram_tensor("ks", [P4, 64, S_], F32, kind="ExternalInput").ap()
        V = nc.dram_tensor("vs", [P4, 64, S_], F32, kind="ExternalInput").ap()
        AZ = nc.dram_tensor("az", [P4, 1, S_], F32, kind="ExternalInput").ap()
        BZ = nc.dram_tensor("bz", [P4, 1, S_], F32, kind="ExternalInput").ap()
        MTS = nc.dram_tensor("mtris", [128, 512], F32, kind="ExternalInput").ap()
        MTU = nc.dram_tensor("mtriui", [128, 512], F32, kind="ExternalInput").ap()
        IDF = nc.dram_tensor("ident_f32", [128, 128], F32, kind="ExternalInput").ap()
        O = nc.dram_tensor("out_tm", [P4, NCH, 128, 64], BF16, kind="ExternalOutput").ap()

        sb = lambda n, s, d: es.enter_context(nc.sbuf_tensor(n, s, d))
        ps = lambda n, s, d: es.enter_context(nc.psum_tensor(n, s, d))
        dbl = lambda n, s, d: [sb(n + "0", s, d), sb(n + "1", s, d)]
        onesr = sb("onesr", [1, 128], F32)
        p.emit("vector", lambda e: e.memset(onesr[:, :], 1.0), writes=[("onesr",)])
        onesc = sb("onesc", [128, 1], F32)
        p.emit("vector", lambda e: e.memset(onesc[:, :], 1.0), writes=[("onesc",)])
        zrow = sb("zrow", [P4, 128], F32)
        p.emit("vector", lambda e: e.memset(zrow[:, :], 0.0), writes=[("zrow",)])
        idsb = sb("idsb", [128, 128], F32)
        p.dma("sync", idsb[:, :], IDF, writes=[("id",)])
        mts = sb("mts", [128, 512], F32)
        p.dma("sync", mts[:, :], MTS, writes=[("mts",)])
        mtu = sb("mtu", [128, 512], F32)
        p.dma("sync", mtu[:, :], MTU, writes=[("mtu",)])
        azc = dbl("azc", [P4, 128], F32)
        bzc = dbl("bzc", [P4, 128], F32)
        AZr = AZ.rearrange("p one s -> (p one) s")
        BZr = BZ.rearrange("p one s -> (p one) s")

        kc = dbl("kc", [64, P4, 128], F32)
        qc = dbl("qc", [64, P4, 128], F32)
        vc = dbl("vc", [64, P4, 128], F32)
        kcn = dbl("kcn", [64, P4, 128], F32)
        qcn = dbl("qcn", [64, P4, 128], F32)
        sqs = dbl("sqs", [64, P4 * 128], F32)
        nrm = dbl("nrm", [1, P4 * 128], F32)
        rbc = dbl("rbc", [64, P4 * 128], F32)
        l4 = dbl("l4", [P4, 128], F32)
        lna = dbl("lna", [P4, 128], F32)
        ldc = dbl("ldc", [P4, 128], F32)
        lcb = dbl("lcb", [P4, 128], F32)
        cols = dbl("cols", [128, 16], F32)
        colse = dbl("colse", [128, 16], F32)
        kk = dbl("kk", [128, P4, 128], F32)
        kq = dbl("kq", [128, P4, 128], F32)
        lb = dbl("lb", [128, P4, 128], F32)
        ldif = dbl("ldif", [128, P4, 128], F32)
        dm = dbl("dm", [128, P4, 128], F32)
        dmx = dbl("dmx", [128, P4, 128], F32)
        lw = dbl("lw", [128, P4, 128], F32)
        lu = dbl("lu", [128, P4, 128], F32)
        m2t = dbl("m2t", [128, P4, 128], F32)
        m3t = dbl("m3t", [128, P4, 128], F32)
        ktm = dbl("ktm", [128, P4, 64], F32)
        vtm = dbl("vtm", [128, P4, 64], F32)
        kbar = dbl("kbar", [128, P4, 64], F32)
        Pp = dbl("Pp", [128, 2 * P4, 128], F32)
        Tt = dbl("Tt", [128, 2 * P4, 128], F32)
        Xs = dbl("Xs", [128, 2 * P4, 64], F32)
        wts = dbl("wts", [64, P4, 128], F32)
        wg = dbl("wg", [128, P4, 64], F32)
        o1 = dbl("o1", [128, P4, 64], F32)
        g1t = dbl("g1t", [64, P4, 64], F32)
        gsb = sb("gsb", [64, P4, 64], F32)
        och = dbl("och", [128, P4, 64], BF16)
        T2_ = sb("T2", [128, 2 * P4, 128], F32)
        P2_ = sb("P2", [128, 2 * P4, 128], F32)
        T4_ = sb("T4", [128, 2 * P4, 128], F32)
        P4b_ = sb("P4b", [128, 2 * P4, 128], F32)
        T8_ = sb("T8", [128, 2 * P4, 128], F32)
        T2 = [T2_, T2_]; P2 = [P2_, P2_]; T4 = [T4_, T4_]
        P4b = [P4b_, P4b_]; T8 = [T8_, T8_]
        pA = [ps(f"pA{i}", [128, 512], F32) for i in range(4)]
        pB = [ps(f"pB{i}", [128, 512], F32) for i in range(2)]
        pN = [ps(f"pN{i}", [1, 512], F32) for i in range(2)]

        p.emit("vector", lambda e: e.memset(gsb[:, :, :], 0.0), writes=[("gsb",)])

        prot = [0]

        def bank():
            b = prot[0] % 4
            prot[0] += 1
            return pA[b], ("pA", b)

        def chunk(ci):
            bi = ci % 2
            cs = slice(ci * 128, (ci + 1) * 128)
            kck, qck, vck = ("kc", bi), ("qc", bi), ("vc", bi)
            kB, kNk = ("pB", bi), ("pN", bi)
            pBb, pNb = pB[bi], pN[bi]
            p.dma("sync", azc[bi][:, :], AZr[:, cs], writes=[("azc", bi)])
            p.dma("sync", bzc[bi][:, :], BZr[:, cs], writes=[("bzc", bi)])
            p.dma("sync", kc[bi][:, :, :], K_[:, :, cs].rearrange("p d c -> d p c"), writes=[kck])
            p.dma("sync", qc[bi][:, :, :], Q[:, :, cs].rearrange("p d c -> d p c"), writes=[qck])
            p.dma("sync", vc[bi][:, :, :], V[:, :, cs].rearrange("p d c -> d p c"), writes=[vck])
            for src, srk, dst, dsk in ((kc[bi], kck, kcn[bi], ("kcn", bi)), (qc[bi], qck, qcn[bi], ("qcn", bi))):
                p.emit("scalar", lambda e, src=src: e.activation(out=sqs[bi][:, :], in_=src[:, :, :], func=AF.Square),
                       reads=[srk], writes=[("sqs", bi)])
                for pi in range(P4):
                    pslc = slice(pi * 128, (pi + 1) * 128)
                    p.emit("tensor", lambda e, pslc=pslc: e.matmul(out=pNb[:, pslc], lhsT=onesc[0:64, :],
                                                                   rhs=sqs[bi][:, pslc], start=True, stop=True),
                           reads=[("sqs", bi), ("onesc",)], writes=[kNk])
                p.emit("scalar", lambda e: e.activation(out=nrm[bi][:, :], in_=pNb[:, :], func=AF.Sqrt),
                       reads=[kNk], writes=[("nrm", bi)])
                p.emit("vector", lambda e: e.tensor_scalar_max(out=nrm[bi][:, :], in0=nrm[bi][:, :], scalar1=1e-12),
                       reads=[("nrm", bi)], writes=[("nrm", bi)])
                p.emit("tensor", lambda e: e.matmul(out=pBb[0:64, :], lhsT=onesr[:, 0:64], rhs=nrm[bi][:, :],
                                                    start=True, stop=True),
                       reads=[("nrm", bi), ("onesr",)], writes=[kB])
                p.emit("vector", lambda e: e.reciprocal(out=rbc[bi][:, :], in_=pBb[0:64, :]),
                       reads=[kB], writes=[("rbc", bi)])
                p.emit("vector", lambda e, src=src, dst=dst: e.tensor_mul(
                    out=dst[:, :, :], in0=src[:, :, :], in1=rbc[bi][:, :].rearrange("d (p c) -> d p c", p=P4)),
                    reads=[srk, ("rbc", bi)], writes=[dsk])
            p.emit("scalar", lambda e: e.activation(out=lna[bi][:, :], in_=azc[bi][:, :], func=AF.Ln),
                   reads=[("azc", bi)], writes=[("lna", bi)])
            p.emit("vector", lambda e: e.tensor_tensor_scan(out=l4[bi][:, :], data0=lna[bi][:, :], data1=zrow[:, :],
                                                            initial=0.0, op0=ALU.add, op1=ALU.add),
                   reads=[("lna", bi), ("zrow",)], writes=[("l4", bi)])
            p.emit("vector", lambda e: e.tensor_scalar(out=ldc[bi][:, :], in0=l4[bi][:, :], scalar1=l4[bi][:, 127:128],
                                                       scalar2=-1.0, op0=ALU.subtract, op1=ALU.mult),
                   reads=[("l4", bi)], writes=[("ldc", bi)])
            p.emit("vector", lambda e: e.tensor_scalar(out=lcb[bi][:, :], in0=l4[bi][:, :], scalar1=0.0,
                                                       scalar2=l4[bi][:, 127:128], op0=ALU.mult, op1=ALU.add),
                   reads=[("l4", bi)], writes=[("lcb", bi)])
            for i, (rows, rk) in enumerate(((bzc[bi], ("bzc", bi)), (l4[bi], ("l4", bi)),
                                            (ldc[bi], ("ldc", bi)), (lcb[bi], ("lcb", bi)))):
                inap = rows[:, :]
                p.emit("tensor", lambda e, i=i, inap=inap: e.transpose(
                    out=pBb[:, i * 4:(i + 1) * 4], in_=inap, identity=idsb[0:P4, 0:P4]),
                    reads=[rk, ("id",)], writes=[kB])
            p.emit("vector", lambda e: e.tensor_copy(out=cols[bi][:, :], in_=pBb[:, 0:16]),
                   reads=[kB], writes=[("cols", bi)])
            p.emit("scalar", lambda e: e.activation(out=colse[bi][:, 4:16], in_=cols[bi][:, 4:16], func=AF.Exp),
                   reads=[("cols", bi)], writes=[("colse", bi)])
            for dst, dk_, lh, lk, rh, rk in ((kk[bi], ("kk", bi), kcn[bi], ("kcn", bi), kcn[bi], ("kcn", bi)),
                                             (kq[bi], ("kq", bi), kcn[bi], ("kcn", bi), qcn[bi], ("qcn", bi))):
                pa, pak = bank()
                for pi in range(P4):
                    p.emit("tensor", lambda e, pi=pi, lh=lh, rh=rh, pa=pa: e.matmul(
                        out=pa[:, pi * 128:(pi + 1) * 128], lhsT=lh[:, pi, :], rhs=rh[:, pi, :],
                        start=True, stop=True),
                        reads=[lk, rk], writes=[pak])
                p.emit("vector", lambda e, dst=dst, pa=pa: e.tensor_copy(out=dst[:, :, :], in_=pa[:, :]),
                       reads=[pak], writes=[dk_])
            pa, pak = bank()
            for pi in range(P4):
                p.emit("tensor", lambda e, pi=pi, pa=pa: e.transpose(
                    out=pa[:, pi * 128:(pi + 1) * 128],
                    in_=cols[bi][:, 4 + pi:5 + pi].broadcast_to([128, 128]), identity=idsb[:, :]),
                    reads=[("cols", bi), ("id",)], writes=[pak])
            p.emit("vector", lambda e, pa=pa: e.tensor_copy(out=lb[bi][:, :, :], in_=pa[:, :]),
                   reads=[pak], writes=[("lb", bi)])
            for pi in range(P4):
                p.emit("vector", lambda e, pi=pi: e.tensor_scalar(
                    out=ldif[bi][:, pi, :], in0=lb[bi][:, pi, :], scalar1=cols[bi][:, 4 + pi:5 + pi],
                    scalar2=None, op0=ALU.subtract),
                    reads=[("lb", bi), ("cols", bi)], writes=[("ldif", bi)])
            p.emit("vector", lambda e: e.tensor_scalar(out=dm[bi][:, :, :], in0=ldif[bi][:, :, :], scalar1=-1.0,
                                                       scalar2=0.0, op0=ALU.mult, op1=ALU.min),
                   reads=[("ldif", bi)], writes=[("dm", bi)])
            p.emit("scalar", lambda e: e.activation(out=dm[bi][:, :, :], in_=dm[bi][:, :, :], func=AF.Exp),
                   reads=[("dm", bi)], writes=[("dm", bi)])
            p.emit("vector", lambda e: e.tensor_scalar_min(out=dmx[bi][:, :, :], in0=ldif[bi][:, :, :], scalar1=0.0),
                   reads=[("ldif", bi)], writes=[("dmx", bi)])
            p.emit("scalar", lambda e: e.activation(out=dmx[bi][:, :, :], in_=dmx[bi][:, :, :], func=AF.Exp),
                   reads=[("dmx", bi)], writes=[("dmx", bi)])
            for pi in range(P4):
                p.emit("vector", lambda e, pi=pi: e.tensor_scalar_mul(
                    out=lw[bi][:, pi, :], in0=kk[bi][:, pi, :], scalar1=cols[bi][:, pi:pi + 1]),
                    reads=[("kk", bi), ("cols", bi)], writes=[("lw", bi)])
            p.emit("vector", lambda e: e.tensor_mul(out=lw[bi][:, :, :], in0=lw[bi][:, :, :],
                                                    in1=mts[:, :].rearrange("a (p c) -> a p c", p=P4)),
                   reads=[("lw", bi), ("mts",)], writes=[("lw", bi)])
            p.emit("vector", lambda e: e.tensor_mul(out=lu[bi][:, :, :], in0=lw[bi][:, :, :], in1=dm[bi][:, :, :]),
                   reads=[("lw", bi), ("dm", bi)], writes=[("lu", bi)])
            p.emit("vector", lambda e: e.tensor_mul(out=m2t[bi][:, :, :], in0=kq[bi][:, :, :],
                                                    in1=mtu[:, :].rearrange("a (p c) -> a p c", p=P4)),
                   reads=[("kq", bi), ("mtu",)], writes=[("m2t", bi)])
            p.emit("vector", lambda e: e.tensor_mul(out=m3t[bi][:, :, :], in0=m2t[bi][:, :, :], in1=dmx[bi][:, :, :]),
                   reads=[("m2t", bi), ("dmx", bi)], writes=[("m3t", bi)])
            pa, pak = bank()
            for pi in range(P4):
                p.emit("tensor", lambda e, pi=pi, pa=pa: e.transpose(
                    out=pa[:, pi * 64:(pi + 1) * 64], in_=kcn[bi][:, pi, :], identity=idsb[0:64, 0:64]),
                    reads=[("kcn", bi), ("id",)], writes=[pak])
            p.emit("vector", lambda e, pa=pa: e.tensor_copy(out=ktm[bi][:, :, :], in_=pa[:, 0:256]),
                   reads=[pak], writes=[("ktm", bi)])
            pa, pak = bank()
            for pi in range(P4):
                p.emit("tensor", lambda e, pi=pi, pa=pa: e.transpose(
                    out=pa[:, pi * 64:(pi + 1) * 64], in_=vc[bi][:, pi, :], identity=idsb[0:64, 0:64]),
                    reads=[vck, ("id",)], writes=[pak])
            p.emit("vector", lambda e, pa=pa: e.tensor_copy(out=vtm[bi][:, :, :], in_=pa[:, 0:256]),
                   reads=[pak], writes=[("vtm", bi)])
            for pi in range(P4):
                p.emit("vector", lambda e, pi=pi: e.tensor_scalar_mul(
                    out=Xs[bi][:, pi, :], in0=ktm[bi][:, pi, :], scalar1=cols[bi][:, pi:pi + 1]),
                    reads=[("ktm", bi), ("cols", bi)], writes=[("Xs", bi)])
                p.emit("vector", lambda e, pi=pi: e.tensor_scalar_mul(
                    out=Xs[bi][:, P4 + pi, :], in0=vtm[bi][:, pi, :], scalar1=cols[bi][:, pi:pi + 1]),
                    reads=[("vtm", bi), ("cols", bi)], writes=[("Xs", bi)])
                p.emit("vector", lambda e, pi=pi: e.tensor_scalar_mul(
                    out=kbar[bi][:, pi, :], in0=ktm[bi][:, pi, :], scalar1=colse[bi][:, 8 + pi:9 + pi]),
                    reads=[("ktm", bi), ("colse", bi)], writes=[("kbar", bi)])
            pa, pak = bank()
            for pi in range(P4):
                for si, lsrc, lk in ((0, lw[bi], ("lw", bi)), (1, lu[bi], ("lu", bi))):
                    p.emit("tensor", lambda e, pi=pi, si=si, lsrc=lsrc, pa=pa: e.transpose(
                        out=pa[:, 0:128], in_=lsrc[:, pi, :], identity=idsb[:, :]),
                        reads=[lk, ("id",)], writes=[pak])
                    p.emit("vector", lambda e, pi=pi, si=si, pa=pa: e.tensor_copy(
                        out=Tt[bi][:, si * P4 + pi, :], in_=pa[:, 0:128]),
                        reads=[pak], writes=[("Tt", bi)])
            p.emit("vector", lambda e: e.tensor_copy(out=Pp[bi][:, 0:P4, :], in_=lw[bi][:, :, :]),
                   reads=[("lw", bi)], writes=[("Pp", bi)])
            p.emit("vector", lambda e: e.tensor_copy(out=Pp[bi][:, P4:2 * P4, :], in_=lu[bi][:, :, :]),
                   reads=[("lu", bi)], writes=[("Pp", bi)])
            do_powers(ci, bi)

        def do_powers(ci, bi):
            seq = [((Pp[bi], ("Pp", bi)), (Tt[bi], ("Tt", bi)), (P2[bi], ("P2",)), (T2[bi], ("T2",)), True),
                   ((P2[bi], ("P2",)), (T2[bi], ("T2",)), (P4b[bi], ("P4b",)), (T4[bi], ("T4",)), True),
                   ((P4b[bi], ("P4b",)), (T4[bi], ("T4",)), (None, None), (T8[bi], ("T8",)), False)]
            for (Pn, Pk), (Tn, Tk), (Pd, Pdk), (Td, Tdk), needP in seq:
                if needP:
                    for half in range(2):
                        pa, pak = bank()
                        for i in range(P4):
                            m = half * P4 + i
                            p.emit("tensor", lambda e, m=m, Tn=Tn, Pn=Pn, i=i, pa=pa: e.matmul(
                                out=pa[:, i * 128:(i + 1) * 128], lhsT=Tn[:, m, :], rhs=Pn[:, m, :],
                                start=True, stop=True),
                                reads=[Tk, Pk], writes=[pak])
                        p.emit("vector", lambda e, half=half, pa=pa, Pd=Pd: e.tensor_copy(
                            out=Pd[:, half * P4:(half + 1) * P4, :], in_=pa[:, :]),
                            reads=[pak], writes=[Pdk])
                for half in range(2):
                    pa, pak = bank()
                    for i in range(P4):
                        m = half * P4 + i
                        p.emit("tensor", lambda e, m=m, Tn=Tn, Pn=Pn, i=i, pa=pa: e.matmul(
                            out=pa[:, i * 128:(i + 1) * 128], lhsT=Pn[:, m, :], rhs=Tn[:, m, :],
                            start=True, stop=True),
                            reads=[Tk, Pk], writes=[pak])
                    p.emit("vector", lambda e, half=half, pa=pa, Td=Td: e.tensor_copy(
                        out=Td[:, half * P4:(half + 1) * P4, :], in_=pa[:, :]),
                        reads=[pak], writes=[Tdk])
            for Tn, Tk, sign in ((T8[bi], ("T8",), 1.0), (T4[bi], ("T4",), 1.0),
                                 (T2[bi], ("T2",), 1.0), (Tt[bi], ("Tt", bi), -1.0)):
                pa, pak = bank()
                for m in range(2 * P4):
                    p.emit("tensor", lambda e, m=m, Tn=Tn, pa=pa: e.matmul(
                        out=pa[:, m * 64:(m + 1) * 64], lhsT=Tn[:, m, :], rhs=Xs[bi][:, m, :],
                        start=True, stop=True),
                        reads=[Tk, ("Xs", bi)], writes=[pak])
                pav = pa[:, :].rearrange("a (m c) -> a m c", m=2 * P4)
                if sign > 0:
                    p.emit("vector", lambda e, pav=pav: e.tensor_add(out=Xs[bi][:, :, :], in0=Xs[bi][:, :, :], in1=pav),
                           reads=[pak, ("Xs", bi)], writes=[("Xs", bi)])
                else:
                    p.emit("vector", lambda e, pav=pav: e.tensor_sub(out=Xs[bi][:, :, :], in0=Xs[bi][:, :, :], in1=pav),
                           reads=[pak, ("Xs", bi)], writes=[("Xs", bi)])
            finish_chunk(ci, bi)

        def finish_chunk(ci, bi):
            pa, pak = bank()
            for pi in range(P4):
                p.emit("tensor", lambda e, pi=pi, pa=pa: e.transpose(
                    out=pa[:, pi * 128:(pi + 1) * 128][0:64, :], in_=Xs[bi][:, pi, :], identity=idsb[:, :]),
                    reads=[("Xs", bi), ("id",)], writes=[pak])
            p.emit("vector", lambda e, pa=pa: e.tensor_copy(
                out=wts[bi][:, :, :], in_=pa[0:64, :].rearrange("d (p c) -> d p c", p=P4)),
                reads=[pak], writes=[("wts", bi)])
            pa, pak = bank()
            for pi in range(P4):
                p.emit("tensor", lambda e, pi=pi, pa=pa: e.matmul(
                    out=pa[:, pi * 64:(pi + 1) * 64], lhsT=wts[bi][:, pi, :], rhs=gsb[:, pi, :],
                    start=True, stop=True),
                    reads=[("wts", bi), ("gsb",)], writes=[pak])
            p.emit("vector", lambda e, pa=pa: e.tensor_scalar_mul(
                out=wg[bi][:, :, :], in0=pa[:, 0:256].rearrange("a (p c) -> a p c", p=P4), scalar1=-1.0),
                reads=[pak], writes=[("wg", bi)])
            pa, pak = bank()
            for pi in range(P4):
                p.emit("tensor", lambda e, pi=pi, pa=pa: e.matmul(
                    out=pa[:, pi * 64:(pi + 1) * 64], lhsT=qcn[bi][:, pi, :], rhs=gsb[:, pi, :],
                    start=True, stop=False),
                    reads=[("qcn", bi), ("gsb",)], writes=[pak])
                p.emit("tensor", lambda e, pi=pi, pa=pa: e.matmul(
                    out=pa[:, pi * 64:(pi + 1) * 64], lhsT=m2t[bi][:, pi, :], rhs=wg[bi][:, pi, :],
                    start=False, stop=True),
                    reads=[("m2t", bi), ("wg", bi)], writes=[pak])
            for pi in range(P4):
                p.emit("vector", lambda e, pi=pi, pa=pa: e.tensor_scalar_mul(
                    out=o1[bi][:, pi, :], in0=pa[:, pi * 64:(pi + 1) * 64], scalar1=colse[bi][:, 4 + pi:5 + pi]),
                    reads=[pak, ("colse", bi)], writes=[("o1", bi)])
            pa2, pak2 = bank()
            for pi in range(P4):
                p.emit("tensor", lambda e, pi=pi, pa2=pa2: e.matmul(
                    out=pa2[:, pi * 64:(pi + 1) * 64], lhsT=m3t[bi][:, pi, :], rhs=Xs[bi][:, P4 + pi, :],
                    start=True, stop=True),
                    reads=[("m3t", bi), ("Xs", bi)], writes=[pak2])
            p.emit("vector", lambda e, pa2=pa2: e.tensor_add(
                out=och[bi][:, :, :], in0=o1[bi][:, :, :], in1=pa2[:, 0:256].rearrange("a (p c) -> a p c", p=P4)),
                reads=[pak2, ("o1", bi)], writes=[("och", bi)])
            p.dma("gpsimd", O[:, ci, :, :].rearrange("p a d -> a p d"), och[bi][:, :, :], reads=[("och", bi)])
            pa, pak = bank()
            for pi in range(P4):
                p.emit("tensor", lambda e, pi=pi, pa=pa: e.matmul(
                    out=pa[0:64, pi * 64:(pi + 1) * 64], lhsT=ktm[bi][:, pi, :], rhs=wg[bi][:, pi, :],
                    start=True, stop=True),
                    reads=[("ktm", bi), ("wg", bi)], writes=[pak])
            p.emit("vector", lambda e, pa=pa: e.tensor_add(
                out=g1t[bi][:, :, :], in0=gsb[:, :, :], in1=pa[0:64, 0:256].rearrange("d (p c) -> d p c", p=P4)),
                reads=[pak, ("gsb",)], writes=[("g1t", bi)])
            pa2, pak2 = bank()
            for pi in range(P4):
                p.emit("tensor", lambda e, pi=pi, pa2=pa2: e.matmul(
                    out=pa2[0:64, pi * 64:(pi + 1) * 64], lhsT=kbar[bi][:, pi, :], rhs=Xs[bi][:, P4 + pi, :],
                    start=True, stop=True),
                    reads=[("kbar", bi), ("Xs", bi)], writes=[pak2])
            for pi in range(P4):
                p.emit("vector", lambda e, pi=pi, pa2=pa2: e.scalar_tensor_tensor(
                    out=gsb[:, pi, :], in0=g1t[bi][:, pi, :], scalar=colse[bi][0:64, 12 + pi:13 + pi],
                    in1=pa2[0:64, pi * 64:(pi + 1) * 64], op0=ALU.mult, op1=ALU.add),
                    reads=[("g1t", bi), ("colse", bi), pak2], writes=[("gsb",)])

        for ci in range(NCH):
            chunk(ci)
    return prog


def mk_lm1(T=TPC, dbg=False):
    def prog(nc, p, es):
        X = nc.dram_tensor("x_fm", [D, T], F32, kind="ExternalInput").ap()
        WQ = nc.dram_tensor("wq", [D, D], F32, kind="ExternalInput").ap()
        WKVA = nc.dram_tensor("wkva", [D, 160], F32, kind="ExternalInput").ap()
        WKVB = nc.dram_tensor("wkvb", [128, 512], F32, kind="ExternalInput").ap()
        WLN1 = nc.dram_tensor("wln1", [128, 8], F32, kind="ExternalInput").ap()
        COS = nc.dram_tensor("cosT", [128, T], F32, kind="ExternalInput").ap()
        SIN = nc.dram_tensor("sinT", [128, T], F32, kind="ExternalInput").ap()
        YO = nc.dram_tensor("y_out", [D, T], F32, kind="ExternalOutput").ap()
        QO = nc.dram_tensor("q_out", [D, T], BF16, kind="ExternalOutput").ap()
        KVO = nc.dram_tensor("kv_out", [512, T], BF16, kind="ExternalOutput").ap()
        KRO = nc.dram_tensor("krope_out", [32, T], BF16, kind="ExternalOutput").ap()

        tl = TL(nc, p, es, T=T)
        sb = tl.sb
        wq = tl.load_weight_bf("wq", WQ, D, D)
        wkva = tl.load_weight_bf("wkva", WKVA, D, 160)
        wkvbs = sb("wkvb_stg", [128, 512], F32)
        p.dma("sync", wkvbs[:, :], WKVB, writes=[("wkvb_stg",)])
        wkvb = sb("wkvb_bf", [128, 512], BF16)
        p.emit("vector", lambda e: e.tensor_copy(out=wkvb[:, :], in_=wkvbs[:, :]),
               reads=[("wkvb_stg",)], writes=[("wkvb",)])
        wln1 = tl.load_rows_f32("wln1c", WLN1, D)
        cosb = sb("cosb", [128, T], F32)
        p.dma("sync", cosb[:, :], COS, writes=[("cos",)])
        sinb = sb("sinb", [128, T], F32)
        p.dma("sync", sinb[:, :], SIN, writes=[("sin",)])

        xsb = tl.load_fm("x", X, D, T, F32)
        ybf = None
        y32t = sb("y32t", [128, 8, 512], F32)
        ttile = sb("ttile", [128, 8, 512], BF16)
        qsb = sb("qsb", [128, 8, 512], F32)
        qstg = sb("qstg", [128, 8, 512], BF16)
        lat = sb("lat", [128, 512], BF16)
        kvt = sb("kvt", [128, 4, 512], BF16)
        krl = sb("krl", [16, 512], BF16)
        krh = sb("krh", [16, 512], BF16)
        rs0 = sb("rs0", [16, 512], F32)
        rs1 = sb("rs1", [16, 512], F32)
        pRL = tl.pmm[0][0:16, :]
        pRH = tl.pmm[1][0:16, :]
        Yre = YO.rearrange("(c p) t -> p c t", p=128)
        Qre = QO.rearrange("(c p) t -> p c t", p=128)
        KVre = KVO.rearrange("(c p) t -> p c t", p=128)

        def tile_cb(j):
            js = slice(j * 512, (j + 1) * 512)
            p.dma("gpsimd", Yre[:, :, js], y32t[:, :, :], reads=[("y32", c) for c in range(8)])
            for c in range(8):
                sq = tl.sq0 if c % 2 == 0 else tl.sq1
                sqk = ("sq", c % 2)
                p.emit("scalar", lambda e, c=c, sq=sq: e.activation(out=sq[:, :], in_=y32t[:, c, :], func=AF.Square),
                       reads=[("y32", c)], writes=[sqk])
                p.emit("tensor", lambda e, c=c, sq=sq: e.matmul(out=tl.prq[:, :], lhsT=tl.ones_col[:], rhs=sq[:, :],
                                                                start=(c == 0), stop=(c == 7)),
                       reads=[sqk, ("ones_col", 0)], writes=[("prq",)])
            epsap = tl.const(1e-5)
            p.emit("vector", lambda e: e.tensor_scalar_mul(out=tl.r1[:, :], in0=tl.prq[:, :], scalar1=1.0 / D),
                   reads=[("prq",)], writes=[("r1",)])
            p.emit("scalar", lambda e, epsap=epsap: e.activation(out=tl.r1[:, :], in_=tl.r1[:, :], func=AF.Sqrt, bias=epsap[0:1, :]),
                   reads=[("r1",), ("const", 1e-5)], writes=[("r1",)])
            p.emit("vector", lambda e: e.reciprocal(out=tl.r1[:, :], in_=tl.r1[:, :]),
                   reads=[("r1",)], writes=[("r1",)])
            p.emit("tensor", lambda e: e.matmul(out=tl.pbS[:, :], lhsT=tl.ones_row[:], rhs=tl.r1[:, :],
                                                start=True, stop=True),
                   reads=[("r1",), ("ones_row", 0)], writes=[("pbS",)])
            for c in range(8):
                p.emit("vector", lambda e, c=c: e.scalar_tensor_tensor(
                    out=ttile[:, c, :], in0=y32t[:, c, :], scalar=wln1[:, c:c + 1],
                    in1=tl.pbS[:, :], op0=ALU.mult, op1=ALU.mult),
                    reads=[("y32", c), ("wln1c", 0), ("pbS",)], writes=[("tt", c)])
            for fo in range(8):
                pm = tl.pmm[fo % 2]
                pmk = ("pmm", fo % 2)
                for c in range(8):
                    p.emit("tensor", lambda e, c=c, fo=fo, pm=pm: e.matmul(
                        out=pm[:, :], lhsT=wq[:, c, fo * 128:(fo + 1) * 128], rhs=ttile[:, c, :],
                        start=(c == 0), stop=(c == 7)),
                        reads=[("wq", c), ("tt", c)], writes=[pmk])
                p.emit("vector", lambda e, fo=fo, pm=pm: e.tensor_copy(out=qsb[:, fo, :], in_=pm[:, :]),
                       reads=[pmk], writes=[("qsb", fo)])
            for ci_ in (0, 1):
                lo, hi = 4 + ci_, 6 + ci_
                p.emit("vector", lambda e, lo=lo, js=js: e.tensor_mul(out=tl.sq0[:, :], in0=qsb[:, lo, :], in1=cosb[:, js]),
                       reads=[("qsb", lo), ("cos",)], writes=[("sq", 0)])
                p.emit("vector", lambda e, hi=hi, js=js: e.tensor_mul(out=tl.sq1[:, :], in0=qsb[:, hi, :], in1=sinb[:, js]),
                       reads=[("qsb", hi), ("sin",)], writes=[("sq", 1)])
                p.emit("vector", lambda e, lo=lo: e.tensor_sub(out=qstg[:, lo, :], in0=tl.sq0[:, :], in1=tl.sq1[:, :]),
                       reads=[("sq", 0), ("sq", 1)], writes=[("qst", lo)])
                p.emit("vector", lambda e, lo=lo, js=js: e.tensor_mul(out=tl.sq0[:, :], in0=qsb[:, lo, :], in1=sinb[:, js]),
                       reads=[("qsb", lo), ("sin",)], writes=[("sq", 0)])
                p.emit("vector", lambda e, hi=hi, js=js: e.tensor_mul(out=tl.sq1[:, :], in0=qsb[:, hi, :], in1=cosb[:, js]),
                       reads=[("qsb", hi), ("cos",)], writes=[("sq", 1)])
                p.emit("vector", lambda e, hi=hi: e.tensor_add(out=qstg[:, hi, :], in0=tl.sq0[:, :], in1=tl.sq1[:, :]),
                       reads=[("sq", 0), ("sq", 1)], writes=[("qst", hi)])
            for fo in range(4):
                p.emit("vector", lambda e, fo=fo: e.tensor_copy(out=qstg[:, fo, :], in_=qsb[:, fo, :]),
                       reads=[("qsb", fo)], writes=[("qst", fo)])
            p.dma("gpsimd", Qre[:, :, js], qstg[:, :, :], reads=[("qst", fo) for fo in range(8)])
            pm = tl.pmm[2]
            pmk = ("pmm", 2)
            for c in range(8):
                p.emit("tensor", lambda e, c=c, pm=pm: e.matmul(
                    out=pm[:, :], lhsT=wkva[:, c, 0:128], rhs=ttile[:, c, :],
                    start=(c == 0), stop=(c == 7)),
                    reads=[("wkva", c), ("tt", c)], writes=[pmk])
            p.emit("vector", lambda e, pm=pm: e.tensor_copy(out=lat[:, :], in_=pm[:, :]),
                   reads=[pmk], writes=[("lat",)])
            for fo in range(4):
                pm2 = tl.pmm[fo % 2]
                pmk2 = ("pmm", fo % 2)
                p.emit("tensor", lambda e, fo=fo, pm2=pm2: e.matmul(
                    out=pm2[:, :], lhsT=wkvb[:, fo * 128:(fo + 1) * 128], rhs=lat[:, :],
                    start=True, stop=True),
                    reads=[("wkvb",), ("lat",)], writes=[pmk2])
                p.emit("vector", lambda e, fo=fo, pm2=pm2: e.tensor_copy(out=kvt[:, fo, :], in_=pm2[:, :]),
                       reads=[pmk2], writes=[("kvt", fo)])
            p.dma("gpsimd", KVre[:, :, js], kvt[:, :, :], reads=[("kvt", fo) for fo in range(4)])
            for c in range(8):
                p.emit("tensor", lambda e, c=c: e.matmul(
                    out=pRL, lhsT=wkva[:, c, 128:144], rhs=ttile[:, c, :],
                    start=(c == 0), stop=(c == 7)),
                    reads=[("wkva", c), ("tt", c)], writes=[("pmm", 0)])
            for c in range(8):
                p.emit("tensor", lambda e, c=c: e.matmul(
                    out=pRH, lhsT=wkva[:, c, 144:160], rhs=ttile[:, c, :],
                    start=(c == 0), stop=(c == 7)),
                    reads=[("wkva", c), ("tt", c)], writes=[("pmm", 1)])
            p.emit("vector", lambda e, js=js: e.tensor_mul(out=rs0[:, :], in0=pRL, in1=cosb[0:16, js]),
                   reads=[("pmm", 0), ("cos",)], writes=[("rs0",)])
            p.emit("vector", lambda e, js=js: e.tensor_mul(out=rs1[:, :], in0=pRH, in1=sinb[0:16, js]),
                   reads=[("pmm", 1), ("sin",)], writes=[("rs1",)])
            p.emit("vector", lambda e: e.tensor_sub(out=krl[:, :], in0=rs0[:, :], in1=rs1[:, :]),
                   reads=[("rs0",), ("rs1",)], writes=[("krl",)])
            p.emit("vector", lambda e, js=js: e.tensor_mul(out=rs0[:, :], in0=pRL, in1=sinb[0:16, js]),
                   reads=[("pmm", 0), ("sin",)], writes=[("rs0",)])
            p.emit("vector", lambda e, js=js: e.tensor_mul(out=rs1[:, :], in0=pRH, in1=cosb[0:16, js]),
                   reads=[("pmm", 1), ("cos",)], writes=[("rs1",)])
            p.emit("vector", lambda e: e.tensor_add(out=krh[:, :], in0=rs0[:, :], in1=rs1[:, :]),
                   reads=[("rs0",), ("rs1",)], writes=[("krh",)])
            p.dma("gpsimd", KRO[0:16, js], krl[:, :], reads=[("krl",)])
            p.dma("gpsimd", KRO[16:32, js], krh[:, :], reads=[("krh",)])

        tl.emit_ln(xsb, "x", 8, ybf, "ybf", y32_tile=y32t, y32name="y32", tile_cb=tile_cb)
    return prog


def mk_lm3(T=TPC, dbg=False):
    E, HH = 8, 512

    def prog(nc, p, es):
        X = nc.dram_tensor("x_fm", [D, T], F32, kind="ExternalInput").ap()
        Y = nc.dram_tensor("y_fm", [D, T], F32, kind="ExternalInput").ap()
        A = nc.dram_tensor("a_fm", [512, T], BF16, kind="ExternalInput").ap()
        WO = nc.dram_tensor("wo", [512, D], F32, kind="ExternalInput").ap()
        WLN2 = nc.dram_tensor("wln2", [128, 8], F32, kind="ExternalInput").ap()
        WGATE = nc.dram_tensor("wgate", [D, E], F32, kind="ExternalInput").ap()
        EB = nc.dram_tensor("ebias", [8, 1], F32, kind="ExternalInput").ap()
        W1 = nc.dram_tensor("w1", [E, D, 2 * HH], F32, kind="ExternalInput").ap()
        W2 = nc.dram_tensor("w2", [E, HH, D], F32, kind="ExternalInput").ap()
        WS1 = nc.dram_tensor("ws1", [D, 2 * HH], F32, kind="ExternalInput").ap()
        WS2 = nc.dram_tensor("ws2", [HH, D], F32, kind="ExternalInput").ap()
        IDF = nc.dram_tensor("ident_f32", [128, 128], F32, kind="ExternalInput").ap()
        O = nc.dram_tensor("xout_fm", [D, T], F32, kind="ExternalOutput").ap()
        UD = nc.dram_tensor("u_scratch", [D, T], F32, kind="Internal").ap()
        if dbg:
            OCW = nc.dram_tensor("cw_dbg", [128, T // 128, E], F32, kind="ExternalOutput").ap()

        tl = TL(nc, p, es, T=T)
        sb, ps = tl.sb, tl.ps
        njt = tl.njt
        nsub = T // 128
        idsb = sb("idsb", [128, 128], F32)
        p.dma("sync", idsb[:, :], IDF, writes=[("id",)])
        wgt = sb("wgt", [128, 8, E], F32)
        p.dma("sync", wgt[:, :, :], WGATE.rearrange("(c p) e -> p c e", p=128), writes=[("wgt",)])
        eb = sb("eb", [8, 1], F32)
        p.dma("sync", eb[:, :], EB, writes=[("eb",)])
        wln2 = tl.load_rows_f32("wln2c", WLN2, D)
        Ure = UD.rearrange("(c p) t -> c p t", p=128)
        Xre = X.rearrange("(c p) t -> c p t", p=128)
        Yre = Y.rearrange("(c p) t -> c p t", p=128)
        Ore = O.rearrange("(c p) t -> c p t", p=128)

        with contextlib.ExitStack() as es2:
            sb2 = lambda n, s, d: es2.enter_context(nc.sbuf_tensor(n, s, d))
            asb = sb2("a", [128, 4, T], BF16)
            p.dma("sync", asb[:, :, :], A.rearrange("(c p) t -> p c t", p=128),
                  writes=[("a", c) for c in range(4)])
            wosb = sb2("wo_bf", [128, 4, D], BF16)
            wstg = [sb2(f"wo_stg{i}", [128, D], F32) for i in range(2)]
            srcw = WO.rearrange("(c p) f -> c p f", p=128)
            for c in range(4):
                s = wstg[c % 2]
                sk = ("wo_stg", c % 2)
                p.dma("sync", s[:, :], srcw[c], writes=[sk])
                p.emit("vector", lambda e, s=s, c=c: e.tensor_copy(out=wosb[:, c, :], in_=s[:, :]),
                       reads=[sk], writes=[("wo", c)])
            yst = [sb2(f"yst{i}", [128, T], F32) for i in range(2)]
            ust = [sb2(f"ust{i}", [128, T], F32) for i in range(2)]
            for fo in range(8):
                bb = fo % 2
                yk = ("yst", bb)
                uk = ("ust", bb)
                p.dma("sync", yst[bb][:, :], Yre[fo], writes=[yk])
                for j in range(njt):
                    js = slice(j * 512, (j + 1) * 512)
                    pm = tl.pmm[j % 2]
                    pmk = ("pmm", j % 2)
                    for c in range(4):
                        p.emit("tensor", lambda e, c=c, fo=fo, js=js, pm=pm: e.matmul(
                            out=pm[:, :], lhsT=wosb[:, c, fo * 128:(fo + 1) * 128], rhs=asb[:, c, js],
                            start=(c == 0), stop=(c == 3)),
                            reads=[("wo", c), ("a", c)], writes=[pmk])
                    p.emit("vector", lambda e, js=js, bb=bb, pm=pm: e.tensor_add(
                        out=ust[bb][:, js], in0=yst[bb][:, js], in1=pm[:, :]),
                        reads=[pmk, yk], writes=[uk])
                p.dma("gpsimd", Ure[fo], ust[bb][:, :], reads=[uk])
        p.barrier()

        xnbf = tl.sbuf_fm("xn", D, T, BF16)
        zbf = tl.sbuf_fm("z", D, T, BF16)
        cw_all = sb("cw_all", [128, nsub, E], F32)
        lg = sb("lg", [16, 512], F32)
        gs = [sb(f"gs{i}", [128, E], F32) for i in range(5)]
        gr = [sb(f"gr{i}", [128, 1], F32) for i in range(4)]
        ut = sb("ut", [128, 8, 512], F32)

        def gate_tile(j):
            plg = tl.pmm[0]
            for c in range(8):
                p.emit("tensor", lambda e, c=c: e.matmul(
                    out=plg[0:E, :], lhsT=wgt[:, c, :], rhs=ut[:, c, :],
                    start=(c == 0), stop=(c == 7)),
                    reads=[("wgt",), ("ut", c)], writes=[("pmm", 0)])
            p.emit("vector", lambda e: e.tensor_scalar_add(out=lg[0:E, :], in0=plg[0:E, :], scalar1=eb[:, 0:1]),
                   reads=[("pmm", 0), ("eb",)], writes=[("lg",)])
            for i in range(4):
                ti = j * 4 + i
                ltm, ptm, g1, l2, cwtm = gs
                m1, nm1, ssum, m2 = gr
                ptf = tl.pmm[1]
                p.emit("tensor", lambda e, i=i: e.transpose(
                    out=ptf[0:128, 0:E], in_=lg[0:E, i * 128:(i + 1) * 128], identity=idsb[0:E, 0:E]),
                    reads=[("lg",), ("id",)], writes=[("pmm", 1)])
                p.emit("vector", lambda e: e.tensor_copy(out=ltm[:, 0:E], in_=ptf[0:128, 0:E]),
                       reads=[("pmm", 1)], writes=[("ltm",)])
                p.emit("vector", lambda e: e.reduce_max(out=m1[:, :], in_=ltm[:, 0:E], axis=AX.X),
                       reads=[("ltm",)], writes=[("m1",)])
                p.emit("vector", lambda e: e.tensor_scalar_mul(out=nm1[:, :], in0=m1[:, :], scalar1=-1.0),
                       reads=[("m1",)], writes=[("nm1",)])
                p.emit("scalar", lambda e: e.activation(out=ptm[:, 0:E], in_=ltm[:, 0:E], func=AF.Exp,
                                                        bias=nm1[:, 0:1], accum_out=ssum[:, 0:1]),
                       reads=[("ltm",), ("nm1",)], writes=[("ptm",), ("ssum",)])
                p.emit("vector", lambda e: e.reciprocal(out=ssum[:, :], in_=ssum[:, :]),
                       reads=[("ssum",)], writes=[("ssum",)])
                p.emit("vector", lambda e: e.tensor_scalar_mul(out=ptm[:, 0:E], in0=ptm[:, 0:E], scalar1=ssum[:, 0:1]),
                       reads=[("ptm",), ("ssum",)], writes=[("ptm",)])
                p.emit("vector", lambda e: e.tensor_scalar(out=g1[:, 0:E], in0=ltm[:, 0:E], scalar1=m1[:, 0:1],
                                                           scalar2=None, op0=ALU.is_ge),
                       reads=[("ltm",), ("m1",)], writes=[("g1",)])
                p.emit("vector", lambda e: e.scalar_tensor_tensor(out=l2[:, 0:E], in0=g1[:, 0:E], scalar=-1e30,
                                                                  in1=ltm[:, 0:E], op0=ALU.mult, op1=ALU.add),
                       reads=[("g1",), ("ltm",)], writes=[("l2",)])
                p.emit("vector", lambda e: e.reduce_max(out=m2[:, :], in_=l2[:, 0:E], axis=AX.X),
                       reads=[("l2",)], writes=[("m2",)])
                p.emit("vector", lambda e: e.tensor_scalar(out=g1[:, 0:E], in0=ltm[:, 0:E], scalar1=m2[:, 0:1],
                                                           scalar2=None, op0=ALU.is_ge),
                       reads=[("ltm",), ("m2",)], writes=[("g1",)])
                p.emit("vector", lambda e: e.tensor_mul(out=cwtm[:, 0:E], in0=ptm[:, 0:E], in1=g1[:, 0:E]),
                       reads=[("ptm",), ("g1",)], writes=[("cwtm",)])
                p.emit("vector", lambda e: e.reduce_sum(out=m2[:, :], in_=cwtm[:, 0:E], axis=AX.X),
                       reads=[("cwtm",)], writes=[("m2",)])
                p.emit("vector", lambda e: e.reciprocal(out=m2[:, :], in_=m2[:, :]),
                       reads=[("m2",)], writes=[("m2",)])
                p.emit("vector", lambda e, ti=ti: e.tensor_scalar_mul(out=cw_all[:, ti, :], in0=cwtm[:, 0:E], scalar1=m2[:, 0:1]),
                       reads=[("cwtm",), ("m2",)], writes=[("cw", ti)])

        epsap = tl.const(1e-5)
        ustr = [sb(f"ustr{i}", [128, 512], F32) for i in range(2)]
        for j in range(njt):
            js = slice(j * 512, (j + 1) * 512)
            for c in range(8):
                p.dma("sync", ut[:, c, :], Ure[c][:, js], writes=[("ut", c)])
            for c in range(8):
                sq = tl.sq0 if c % 2 == 0 else tl.sq1
                sqk = ("sq", c % 2)
                p.emit("scalar", lambda e, c=c, sq=sq: e.activation(out=sq[:, :], in_=ut[:, c, :], func=AF.Square),
                       reads=[("ut", c)], writes=[sqk])
                p.emit("tensor", lambda e, c=c, sq=sq: e.matmul(out=tl.prq[:, :], lhsT=tl.ones_col[:], rhs=sq[:, :],
                                                                start=(c == 0), stop=(c == 7)),
                       reads=[sqk, ("ones_col", 0)], writes=[("prq",)])
            p.emit("vector", lambda e: e.tensor_scalar_mul(out=tl.r1[:, :], in0=tl.prq[:, :], scalar1=1.0 / D),
                   reads=[("prq",)], writes=[("r1",)])
            p.emit("scalar", lambda e: e.activation(out=tl.r1[:, :], in_=tl.r1[:, :], func=AF.Sqrt, bias=epsap[0:1, :]),
                   reads=[("r1",), ("const", 1e-5)], writes=[("r1",)])
            p.emit("vector", lambda e: e.reciprocal(out=tl.r1[:, :], in_=tl.r1[:, :]),
                   reads=[("r1",)], writes=[("r1",)])
            p.emit("tensor", lambda e: e.matmul(out=tl.pbS[:, :], lhsT=tl.ones_row[:], rhs=tl.r1[:, :],
                                                start=True, stop=True),
                   reads=[("r1",), ("ones_row", 0)], writes=[("pbS",)])
            for c in range(8):
                p.emit("vector", lambda e, c=c, js=js: e.scalar_tensor_tensor(
                    out=xnbf[:, c, js], in0=ut[:, c, :], scalar=wln2[:, c:c + 1],
                    in1=tl.pbS[:, :], op0=ALU.mult, op1=ALU.mult),
                    reads=[("ut", c), ("wln2c", 0), ("pbS",)], writes=[("xn", c)])
                p.emit("vector", lambda e, c=c: e.scalar_tensor_tensor(
                    out=ut[:, c, :], in0=ut[:, c, :], scalar=wln2[:, c:c + 1],
                    in1=tl.pbS[:, :], op0=ALU.mult, op1=ALU.mult),
                    reads=[("ut", c), ("wln2c", 0), ("pbS",)], writes=[("ut", c)])
            gate_tile(j)
        if dbg:
            p.dma("gpsimd", OCW, cw_all[:, :, :], reads=[("cw", ti) for ti in range(nsub)])
        p.barrier()

        h = [tl.sbuf_fm(f"h{i}", HH, T, BF16) for i in range(2)]
        stg = [sb(f"stgw{i}", [128, 1024], F32) for i in range(2)]
        w1b = [sb(f"w1b{i}", [128, 8, 2 * HH], BF16) for i in range(2)]
        w2b = [sb(f"w2b{i}", [128, 4, D], BF16) for i in range(2)]
        cwb = [sb(f"cwb{i}", [128, 512], BF16) for i in range(2)]
        sg = [sb(f"sg{i}", [128, 512], F32) for i in range(2)]
        hwt = [sb(f"hwt{i}", [128, 512], BF16) for i in range(2)]
        nst = [0]

        def load_cast(dram_src, dst, dstkey, nchunk, ncols):
            for c in range(nchunk):
                s = stg[nst[0] % 2]
                skey = ("stgw", nst[0] % 2)
                nst[0] += 1
                p.dma("sync", s[:, 0:ncols], dram_src[c], writes=[skey])
                p.emit("vector", lambda e, s=s, c=c, ncols=ncols, dst=dst: e.tensor_copy(
                    out=dst[:, c, 0:ncols], in_=s[:, 0:ncols]),
                    reads=[skey], writes=[(dstkey, c)])

        def up_and_down(eidx, w1cur, w1key, w2cur, w2key, hcur, hkey, weighted, first):
            for j in range(njt):
                js = slice(j * 512, (j + 1) * 512)
                if weighted:
                    pcw = tl.pmm[2]
                    for i in range(4):
                        ti = j * 4 + i
                        p.emit("tensor", lambda e, ti=ti, i=i, eidx=eidx: e.transpose(
                            out=pcw[:, i * 128:(i + 1) * 128],
                            in_=cw_all[:, ti, eidx:eidx + 1].broadcast_to([128, 128]),
                            identity=idsb[:, :]),
                            reads=[("cw", ti), ("id",)], writes=[("pmm", 2)])
                    cb = cwb[j % 2]
                    cbkey = ("cwb", j % 2)
                    p.emit("vector", lambda e, cb=cb: e.tensor_copy(out=cb[:, :], in_=pcw[:, :]),
                           reads=[("pmm", 2)], writes=[cbkey])
                for blk in range(4):
                    pu, pukey = tl.pmm[0], ("pmm", 0)
                    pg, pgkey = tl.pmm[1], ("pmm", 1)
                    for c in range(8):
                        p.emit("tensor", lambda e, c=c, blk=blk, js=js, w1cur=w1cur, pu=pu: e.matmul(
                            out=pu[:, :], lhsT=w1cur[:, c, blk * 128:(blk + 1) * 128],
                            rhs=xnbf[:, c, js], start=(c == 0), stop=(c == 7)),
                            reads=[(w1key, c), ("xn", c)], writes=[pukey])
                    for c in range(8):
                        p.emit("tensor", lambda e, c=c, blk=blk, js=js, w1cur=w1cur, pg=pg: e.matmul(
                            out=pg[:, :], lhsT=w1cur[:, c, (4 + blk) * 128:(4 + blk + 1) * 128],
                            rhs=xnbf[:, c, js], start=(c == 0), stop=(c == 7)),
                            reads=[(w1key, c), ("xn", c)], writes=[pgkey])
                    sgc, sgkey = sg[blk % 2], ("sg", blk % 2)
                    p.emit("scalar", lambda e, sgc=sgc, pg=pg: e.activation(out=sgc[:, :], in_=pg[:, :], func=AF.Silu),
                           reads=[pgkey], writes=[sgkey])
                    if weighted:
                        ht, htkey = hwt[blk % 2], ("hwt", blk % 2)
                        p.emit("vector", lambda e, ht=ht, sgc=sgc, pu=pu: e.tensor_mul(out=ht[:, :], in0=pu[:, :], in1=sgc[:, :]),
                               reads=[pukey, sgkey], writes=[htkey])
                        p.emit("gpsimd", lambda e, ht=ht, blk=blk, js=js, hcur=hcur, cb=cb: e.tensor_mul(
                            out=hcur[:, blk, js], in0=ht[:, :], in1=cb[:, :]),
                            reads=[htkey, cbkey], writes=[(hkey, blk, j)])
                    else:
                        p.emit("vector", lambda e, blk=blk, js=js, hcur=hcur, sgc=sgc, pu=pu: e.tensor_mul(
                            out=hcur[:, blk, js], in0=pu[:, :], in1=sgc[:, :]),
                               reads=[pukey, sgkey], writes=[(hkey, blk, j)])
                for fo in range(8):
                    pz, pzkey = tl.pmm[(fo % 2)], ("pmm", fo % 2)
                    for blk in range(4):
                        p.emit("tensor", lambda e, blk=blk, fo=fo, js=js, w2cur=w2cur, hcur=hcur, pz=pz, j=j: e.matmul(
                            out=pz[:, :], lhsT=w2cur[:, blk, fo * 128:(fo + 1) * 128],
                            rhs=hcur[:, blk, js], start=(blk == 0), stop=(blk == 3)),
                            reads=[(w2key, blk), (hkey, blk, j)], writes=[pzkey])
                    if first:
                        p.emit("vector", lambda e, fo=fo, js=js, pz=pz: e.tensor_copy(out=zbf[:, fo, js], in_=pz[:, :]),
                               reads=[pzkey], writes=[("z", fo, j)])
                    else:
                        p.emit("vector", lambda e, fo=fo, js=js, pz=pz: e.tensor_add(out=zbf[:, fo, js], in0=zbf[:, fo, js], in1=pz[:, :]),
                               reads=[pzkey, ("z", fo, j)], writes=[("z", fo, j)])

        for eidx in range(E):
            b = eidx % 2
            load_cast(W1[eidx].rearrange("(c p) f -> c p f", p=128), w1b[b], f"w1b{b}", 8, 2 * HH)
            load_cast(W2[eidx].rearrange("(c p) f -> c p f", p=128), w2b[b], f"w2b{b}", 4, D)
            up_and_down(eidx, w1b[b], f"w1b{b}", w2b[b], f"w2b{b}", h[b], f"h{b}",
                        weighted=True, first=(eidx == 0))
        load_cast(WS1.rearrange("(c p) f -> c p f", p=128), w1b[0], "w1b0", 8, 2 * HH)
        load_cast(WS2.rearrange("(c p) f -> c p f", p=128), w2b[0], "w2b0", 4, D)
        up_and_down(0, w1b[0], "w1b0", w2b[0], "w2b0", h[1], "h1", weighted=False, first=False)

        xc = [sb(f"xc{i}", [128, 512], F32) for i in range(2)]
        uc = [sb(f"uc{i}", [128, 512], F32) for i in range(2)]
        for fo in range(8):
            for j in range(njt):
                js = slice(j * 512, (j + 1) * 512)
                b = j % 2
                xck, uck = ("xc", b), ("uc", b)
                p.dma("sync", xc[b][:, :], Xre[fo][:, js], writes=[xck])
                p.dma("sync", uc[b][:, :], Ure[fo][:, js], writes=[uck])
                p.emit("vector", lambda e, b=b: e.tensor_add(
                    out=xc[b][:, :], in0=xc[b][:, :], in1=uc[b][:, :]),
                    reads=[xck, uck], writes=[xck])
                p.emit("vector", lambda e, fo=fo, b=b, js=js, j=j: e.tensor_add(
                    out=xc[b][:, :], in0=xc[b][:, :], in1=zbf[:, fo, js]),
                    reads=[xck, ("z", fo, j)], writes=[xck])
                p.dma("gpsimd", Ore[fo][:, js], xc[b][:, :], reads=[xck])
    return prog


def mk_lb1(T=TPC):
    def prog(nc, p, es):
        X = nc.dram_tensor("x_fm", [D, T], F32, kind="ExternalInput").ap()
        W = nc.dram_tensor("w_qkv", [D, 3 * D], F32, kind="ExternalInput").ap()
        WGB = nc.dram_tensor("w_gb", [D, 32], F32, kind="ExternalInput").ap()
        BGB = nc.dram_tensor("b_gb", [32, 1], F32, kind="ExternalInput").ap()
        O = nc.dram_tensor("qkv_fm", [3 * D, T], F32, kind="ExternalOutput").ap()
        OG = nc.dram_tensor("gates_fm", [32, T], F32, kind="ExternalOutput").ap()
        tl = TL(nc, p, es, T=T)
        xsb = tl.load_fm("x", X, D, T, F32)
        wsb = tl.load_weight_bf("w", W, D, 3 * D)
        wgb = tl.sb("wgb", [128, 8, 32], F32)
        p.dma("sync", wgb[:, :, :], WGB.rearrange("(c p) g -> p c g", p=128), writes=[("wgb",)])
        bgb = tl.sb("bgb", [32, 1], F32)
        p.dma("sync", bgb[:, :], BGB, writes=[("bgb",)])
        ybf = tl.sbuf_fm("y", D, T, BF16)
        y32t = tl.sb("y32t", [128, 8, 512], F32)
        gt = tl.sb("gt", [32, 512], F32)

        def tile_cb(j):
            js = slice(j * 512, (j + 1) * 512)
            pg = tl.pmm[2]
            for c in range(8):
                p.emit("tensor", lambda e, c=c, pg=pg: e.matmul(
                    out=pg[0:32, :], lhsT=wgb[:, c, :], rhs=y32t[:, c, :],
                    start=(c == 0), stop=(c == 7)),
                    reads=[("wgb",), ("y32", c)], writes=[("pmm", 2)])
            p.emit("scalar", lambda e: e.activation(out=gt[:, :], in_=pg[0:32, :], func=AF.Sigmoid,
                                                    bias=bgb[:, 0:1]),
                   reads=[("pmm", 2), ("bgb",)], writes=[("gt",)])
            p.dma("gpsimd", OG[:, js], gt[:, :], reads=[("gt",)])

        tl.emit_ln(xsb, "x", 8, ybf, "y", y32_tile=y32t, y32name="y32", tile_cb=tile_cb)
        eo = [tl.sb(f"eo{i}", [128, 512], F32) for i in range(3)]
        Ore = O.rearrange("(fo p) t -> fo p t", p=128)
        ecnt = [0]

        def epi(fo, j, pmap, pkey):
            js = slice(j * tl.tt, (j + 1) * tl.tt)
            i = ecnt[0] % 3
            ecnt[0] += 1
            ek = ("eo", i)
            p.emit("vector", lambda e, i=i: e.tensor_copy(out=eo[i][:, :], in_=pmap),
                   reads=[pkey], writes=[ek])
            p.dma("gpsimd", Ore[fo][:, js], eo[i][:, :], reads=[ek])

        tl.emit_mm(ybf, "y", 8, wsb, "w", 3 * D, epi)
    return prog


def _slices(x):
    return [np.ascontiguousarray(x[c * TPC:(c + 1) * TPC].T) for c in range(NCORES)]


def _gather_x(res, key="xout_fm"):
    return np.concatenate([np.asarray(res[c][key]).T for c in range(NCORES)], axis=0)


_ID128 = np.eye(128, dtype=np.float32)
_PP, _QQ = np.meshgrid(np.arange(128), np.arange(512), indexing="ij")
_CAUSAL_MASKS = np.stack([np.where(_PP - _QQ > -128 * m, -1e30, 0.0) for m in range(4)]).astype(np.float32)


def _slopes16():
    start = 2.0 ** (-0.5)
    return np.array([start ** (i + 1) for i in range(16)], np.float32)


def _vtile(v, dv):
    S_ = v.shape[1]
    return np.ascontiguousarray(v.T.reshape(S_ // 128, 128, dv).transpose(1, 0, 2))


def _attn_pair_inputs(qkv_cores, masks, slopes=None, dv=64, vofs=2048, kofs=1024):
    H = 16
    qkv_b = [np.concatenate([qkv_cores[4 * b + c] for c in range(4)], axis=1) for b in range(B)]
    NKT = S // 128
    pos = np.arange(S, dtype=np.float32)
    ims = []
    for core in range(NCORES):
        qs, ks, vs, ramps, augs = [], [], [], [], []
        for pi in range(4 * core, 4 * core + 4):
            b, h = divmod(pi, H)
            qs.append(qkv_b[b][h * 64:(h + 1) * 64])
            ks.append(qkv_b[b][kofs + h * 64:kofs + h * 64 + 64])
            vs.append(_vtile(qkv_b[b][vofs + h * 64:vofs + h * 64 + dv].astype(BF), dv))
            if slopes is not None:
                ramps.append((slopes[h] * pos).reshape(NKT, 128).T.astype(np.float32))
                augs.append((-slopes[h] * pos)[None, :].astype(np.float32))
            else:
                ramps.append(np.zeros((128, NKT), np.float32))
                augs.append(np.zeros((1, S), np.float32))
        ims.append({"qp": np.ascontiguousarray(np.stack(qs)), "kp": np.ascontiguousarray(np.stack(ks)),
                    "vp": np.ascontiguousarray(np.stack(vs)),
                    "ramp": np.stack(ramps), "augq": np.stack(augs), "masks": masks})
    return ims


def _scatter_attnout(res, dv=64):
    H = 16
    ao = np.zeros((B, H * dv, S), dtype=BF)
    for core in range(NCORES):
        r = np.asarray(res[core]["attnout"])
        for i, pi in enumerate(range(4 * core, 4 * core + 4)):
            b, h = divmod(pi, H)
            ao[b, h * dv:(h + 1) * dv] = r[i]
    return ao


_PROGS = {}


def _prog(name, fn):
    if name not in _PROGS:
        _PROGS[name] = fn()
    return _PROGS[name]


def sub_l0a(x, inp):
    im1 = [{"x_fm": xs, "w_qkv": inp["L0_attn_a_W_qkv"], "b_qkv": bias_host(inp["L0_attn_a_b_qkv"])}
           for xs in _slices(x)]
    r1 = run_prog(_prog("qkv", mk_l0a_qkv), im1)
    qkv = [np.asarray(r1[c]["qkv_fm"]) for c in range(NCORES)]
    im2 = _attn_pair_inputs(qkv, _CAUSAL_MASKS, slopes=_slopes16())
    r2 = run_prog(_prog("flash64", lambda: mk_flash(S_=S, npair=4, dv=64)), im2)
    ao = _scatter_attnout(r2)
    aof = ao.transpose(0, 2, 1).reshape(NTOK, D)
    im3 = [{"x_fm": xs, "a_fm": np.ascontiguousarray(aof[c * TPC:(c + 1) * TPC].T),
            "w": inp["L0_attn_a_W_out"], "b": bias_host(inp["L0_attn_a_b_out"])}
           for c, xs in enumerate(_slices(x))]
    r3 = run_prog(_prog("residb", lambda: mk_resid_proj(use_bias=True)), im3)
    return _gather_x(r3)


def sub_l0ffn(x, inp):
    im = [{"x_fm": xs, "wg_t": np.ascontiguousarray(inp["L0_ffn_Wg"].T),
           "w1": inp["L0_ffn_W1"], "w2": inp["L0_ffn_W2"],
           "ws1": inp["L0_ffn_Ws1"], "ws2": inp["L0_ffn_Ws2"], "ident_f32": _ID128}
          for xs in _slices(x)]
    r = run_prog(_prog("l0ffn", mk_l0_ffn), im)
    return _gather_x(r)


def sub_l0b(x, inp):
    Wgb = np.concatenate([inp["L0_attn_b_Wg"], inp["L0_attn_b_Wb"]], axis=0)
    bgb = np.concatenate([inp["L0_attn_b_bg"], inp["L0_attn_b_bb"]]).astype(np.float32).reshape(32, 1)
    im1 = [{"x_fm": xs, "w_qkv": np.ascontiguousarray(inp["L0_attn_b_Wqkv"].T),
            "w_gb": np.ascontiguousarray(Wgb.T), "b_gb": bgb} for xs in _slices(x)]
    r1 = run_prog(_prog("lb1", mk_lb1), im1)
    qkv = [np.concatenate([np.asarray(r1[4 * b + c]["qkv_fm"]) for c in range(4)], axis=1) for b in range(B)]
    gts = [np.concatenate([np.asarray(r1[4 * b + c]["gates_fm"]) for c in range(4)], axis=1) for b in range(B)]
    mtris = np.tile(np.tril(np.ones((128, 128), np.float32), -1), (1, 4))
    mtriui = np.tile(np.triu(np.ones((128, 128), np.float32), 0), (1, 4))
    im2 = []
    for core in range(NCORES):
        qs, ks, vs, azs, bzs = [], [], [], [], []
        for pi in range(4 * core, 4 * core + 4):
            b, h = divmod(pi, 16)
            qs.append(qkv[b][h * 64:(h + 1) * 64])
            ks.append(qkv[b][1024 + h * 64:1024 + h * 64 + 64])
            vs.append(qkv[b][2048 + h * 64:2048 + h * 64 + 64])
            azs.append(gts[b][h:h + 1])
            bzs.append(gts[b][16 + h:16 + h + 1])
        im2.append({"qs": np.ascontiguousarray(np.stack(qs)), "ks": np.ascontiguousarray(np.stack(ks)),
                    "vs": np.ascontiguousarray(np.stack(vs)), "az": np.ascontiguousarray(np.stack(azs)),
                    "bz": np.ascontiguousarray(np.stack(bzs)),
                    "mtris": np.ascontiguousarray(mtris), "mtriui": np.ascontiguousarray(mtriui),
                    "ident_f32": _ID128})
    r2 = run_prog(_prog("scan", lambda: mk_l0b_scan(S_=S, npair=4)), im2)
    ao = np.zeros((B, S, D), dtype=BF)
    for core in range(NCORES):
        o = np.asarray(r2[core]["out_tm"])
        for i, pi in enumerate(range(4 * core, 4 * core + 4)):
            b, h = divmod(pi, 16)
            ao[b, :, h * 64:(h + 1) * 64] = o[i].reshape(S, 64)
    aof = ao.reshape(NTOK, D)
    im3 = [{"x_fm": xs, "a_fm": np.ascontiguousarray(aof[c * TPC:(c + 1) * TPC].T),
            "w": np.ascontiguousarray(inp["L0_attn_b_Wout"].T)} for c, xs in enumerate(_slices(x))]
    r3 = run_prog(_prog("resid", lambda: mk_resid_proj(use_bias=False)), im3)
    return _gather_x(r3)


def sub_l1a(x, inp):
    Wc = np.concatenate([inp["L1_attn_a_Wq"], inp["L1_attn_a_Wk"], inp["L1_attn_a_Wv"]], axis=1)
    bc = np.concatenate([inp["L1_attn_a_bq"], inp["L1_attn_a_bk"], inp["L1_attn_a_bv"]])
    im1 = [{"x_fm": xs, "w_qkv": np.ascontiguousarray(Wc), "b_qkv": bias_host(bc)} for xs in _slices(x)]
    r1 = run_prog(_prog("qkv", mk_l0a_qkv), im1)
    qkv = [np.asarray(r1[c]["qkv_fm"]) for c in range(NCORES)]
    masks = np.stack([np.where(np.abs(_PP - _QQ + 128 * (m - 2)) <= 256, 0.0, -1e30)
                      for m in range(8)]).astype(np.float32)
    im2 = _attn_pair_inputs(qkv, masks, slopes=None)
    NKT = S // 128
    krange = lambda qt: range(max(0, 4 * qt - 2), min(NKT - 1, 4 * qt + 5) + 1)
    maskid = lambda qt, kt: kt - 4 * qt + 2
    r2 = run_prog(_prog("band", lambda: mk_flash(S_=S, npair=4, dv=64, alibi=False,
                                                 krange=krange, maskid=maskid, nmasks=8)), im2)
    ao = _scatter_attnout(r2)
    aof = ao.transpose(0, 2, 1).reshape(NTOK, D)
    im3 = [{"x_fm": xs, "a_fm": np.ascontiguousarray(aof[c * TPC:(c + 1) * TPC].T),
            "w": inp["L1_attn_a_Wo"], "b": bias_host(inp["L1_attn_a_bo"])}
           for c, xs in enumerate(_slices(x))]
    r3 = run_prog(_prog("residb", lambda: mk_resid_proj(use_bias=True)), im3)
    return _gather_x(r3)


def _l1ffn_static(inp):
    THETA = 1e5
    permq = np.zeros(D, np.int64)
    for r_ in range(512):
        h, d = r_ // 32, r_ % 32
        permq[r_] = h * 64 + d
    for r_ in range(256):
        h, i = r_ // 16, r_ % 16
        permq[512 + r_] = h * 64 + 32 + 2 * i
        permq[768 + r_] = h * 64 + 33 + 2 * i
    Wq_p = np.ascontiguousarray(np.asarray(inp["L1_ffn_Wq"])[:, permq])
    permk = np.concatenate([np.arange(128), 128 + 2 * np.arange(16), 129 + 2 * np.arange(16)])
    Wkva_p = np.ascontiguousarray(np.asarray(inp["L1_ffn_Wkva"])[:, permk])
    fr = 1.0 / THETA ** (np.arange(0, 32, 2) / 32.0)
    cos_t, sin_t = [], []
    for c in range(NCORES):
        t0 = (c % 4) * TPC
        tpos = t0 + np.arange(TPC, dtype=np.float32)
        angr = np.array([fr[r_ % 16] * tpos for r_ in range(128)], np.float32)
        cos_t.append(np.cos(angr).astype(np.float32))
        sin_t.append(np.sin(angr).astype(np.float32))
    Wo_r = np.asarray(inp["L1_ffn_Wo"]).reshape(16, 64, D)
    Wo_sl = np.ascontiguousarray(Wo_r[:, :32, :].reshape(512, D))
    W1cat = np.ascontiguousarray(np.concatenate([inp["L1_ffn_Wu"], inp["L1_ffn_Wg"]], axis=2))
    WS1cat = np.ascontiguousarray(np.concatenate([inp["L1_ffn_sWu"][0], inp["L1_ffn_sWg"][0]], axis=1))
    return dict(Wq_p=Wq_p, Wkva_p=Wkva_p, cos_t=cos_t, sin_t=sin_t, Wo_sl=Wo_sl,
                W1cat=W1cat, WS1cat=WS1cat)


def sub_l1ffn(x, inp, st):
    im1 = [{"x_fm": xs, "wq": st["Wq_p"], "wkva": st["Wkva_p"], "wkvb": inp["L1_ffn_Wkvb"],
            "wln1": bias_host(inp["L1_ffn_w_ln1"]), "cosT": st["cos_t"][c], "sinT": st["sin_t"][c]}
           for c, xs in enumerate(_slices(x))]
    r1 = run_prog(_prog("lm1", mk_lm1), im1)
    NKT = S // 128
    qf = [np.concatenate([np.asarray(r1[4 * b + c]["q_out"]) for c in range(4)], axis=1) for b in range(B)]
    kvf = [np.concatenate([np.asarray(r1[4 * b + c]["kv_out"]) for c in range(4)], axis=1) for b in range(B)]
    krf = [np.concatenate([np.asarray(r1[4 * b + c]["krope_out"]) for c in range(4)], axis=1) for b in range(B)]
    im2 = []
    for core in range(NCORES):
        qs, ks, vs = [], [], []
        for pi in range(4 * core, 4 * core + 4):
            b, h = divmod(pi, 16)
            qs.append(np.concatenate([qf[b][h * 32:h * 32 + 32], qf[b][512 + h * 16:512 + h * 16 + 16],
                                      qf[b][768 + h * 16:768 + h * 16 + 16]], axis=0))
            ks.append(np.concatenate([kvf[b][h * 32:h * 32 + 32], krf[b]], axis=0))
            vs.append(_vtile(kvf[b][h * 32:h * 32 + 32], 32))
        im2.append({"qp": np.ascontiguousarray(np.stack(qs)), "kp": np.ascontiguousarray(np.stack(ks)),
                    "vp": np.ascontiguousarray(np.stack(vs)),
                    "ramp": np.zeros((4, 128, NKT), np.float32),
                    "augq": np.zeros((4, 1, S), np.float32), "masks": _CAUSAL_MASKS})
    r2 = run_prog(_prog("flash32", lambda: mk_flash(S_=S, npair=4, dv=32, alibi=False)), im2)
    ao = _scatter_attnout(r2, dv=32)
    im3 = []
    for c, xs in enumerate(_slices(x)):
        b = c // 4
        im3.append({"x_fm": xs, "y_fm": np.ascontiguousarray(np.asarray(r1[c]["y_out"])),
                    "a_fm": np.ascontiguousarray(ao[b][:, (c % 4) * TPC:(c % 4 + 1) * TPC]),
                    "wo": st["Wo_sl"], "wln2": bias_host(inp["L1_ffn_w_ln2"]),
                    "wgate": inp["L1_ffn_Wgate"],
                    "ebias": np.asarray(inp["L1_ffn_expert_bias"], np.float32).reshape(8, 1),
                    "w1": st["W1cat"], "w2": inp["L1_ffn_Wd"],
                    "ws1": st["WS1cat"], "ws2": inp["L1_ffn_sWd"][0], "ident_f32": _ID128})
    r3 = run_prog(_prog("lm3", mk_lm3), im3)
    return _gather_x(r3)


def sub_l1b(x, inp):
    Wc = np.concatenate([np.asarray(inp["L1_attn_b_Wq"]).T, np.asarray(inp["L1_attn_b_Wk"]).T,
                         np.asarray(inp["L1_attn_b_Wv"]).T], axis=1)
    im1 = [{"x_fm": xs, "w_qkv": np.ascontiguousarray(Wc),
            "b_qkv": np.zeros((128, 24), np.float32)} for xs in _slices(x)]
    r1 = run_prog(_prog("qkv", mk_l0a_qkv), im1)
    qkv_b = [np.concatenate([np.asarray(r1[4 * b + c]["qkv_fm"]) for c in range(4)], axis=1) for b in range(B)]
    Sq, HALO = TPC, 128
    Sk = Sq + 2 * HALO
    NKTk = Sk // 128
    NQB = Sq // 512
    im2 = []
    for core in range(NCORES):
        b, quad = core // 4, core % 4
        t0 = quad * Sq
        lo, hi = t0 - HALO, t0 + Sq + HALO
        def sl(rows):
            out = np.zeros((rows.shape[0], Sk), dtype=BF)
            a0, a1 = max(lo, 0), min(hi, S)
            out[:, a0 - lo:a1 - lo] = rows[:, a0:a1]
            return out
        qs, ks, vs = [], [], []
        for h in range(16):
            qs.append(qkv_b[b][h * 64:(h + 1) * 64, t0:t0 + Sq])
            ks.append(sl(qkv_b[b][1024 + h * 64:1024 + h * 64 + 64]))
            vs.append(_vtile(sl(qkv_b[b][2048 + h * 64:2048 + h * 64 + 64]), 64))
        masks = np.zeros((NQB * 6, 128, 512), np.float32)
        for qt in range(NQB):
            for m in range(6):
                kt = 4 * qt + m
                dd = 128 * kt - HALO - 512 * qt
                rel = _PP + dd - _QQ
                jg = t0 + 512 * qt + _QQ + rel
                ok = (rel >= -8) & (rel <= 7) & (jg >= 0) & (jg < S)
                masks[qt * 6 + m] = np.where(ok, 0.0, -1e30)
        im2.append({"qp": np.ascontiguousarray(np.stack(qs)), "kp": np.ascontiguousarray(np.stack(ks)),
                    "vp": np.ascontiguousarray(np.stack(vs)),
                    "ramp": np.zeros((16, 128, NKTk), np.float32),
                    "augq": np.zeros((16, 1, Sq), np.float32), "masks": masks})
    krange = lambda qt: range(4 * qt, 4 * qt + 6)
    maskid = lambda qt, kt: qt * 6 + (kt - 4 * qt)
    r2 = run_prog(_prog("win", lambda: mk_flash(S_=Sq, npair=16, dv=64, alibi=False,
                                                krange=krange, maskid=maskid, nmasks=NQB * 6, Sk=Sk)), im2)
    im3 = []
    for c, xs in enumerate(_slices(x)):
        a = np.asarray(r2[c]["attnout"]).reshape(1024, Sq)
        im3.append({"x_fm": xs, "a_fm": np.ascontiguousarray(a),
                    "w": np.ascontiguousarray(np.asarray(inp["L1_attn_b_Wo"]).T)})
    r3 = run_prog(_prog("resid", lambda: mk_resid_proj(use_bias=False)), im3)
    return _gather_x(r3)


def kernel(**inputs):
    x = np.ascontiguousarray(np.asarray(inputs["L0_attn_a_x"], np.float32).reshape(NTOK, D))
    st = _l1ffn_static(inputs)
    x = sub_l0a(x, inputs)
    x = sub_l0ffn(x, inputs)
    x = sub_l0b(x, inputs)
    x = sub_l0ffn(x, inputs)
    x = sub_l1a(x, inputs)
    x = sub_l1ffn(x, inputs, st)
    x = sub_l1b(x, inputs)
    x = sub_l1ffn(x, inputs, st)
    return x.reshape(B, S, D).astype(np.float32)


# version 06 (final)
# speedup vs baseline: 1.0734x; 1.0734x over previous
import math
import contextlib
import numpy as np
import ml_dtypes
import concourse.bass as bass
import concourse.mybir as mybir
from concourse.bass_utils import run_bass_kernel_spmd

F32 = mybir.dt.float32
BF16 = mybir.dt.bfloat16
AF = mybir.ActivationFunctionType
ALU = mybir.AluOpType
AX = mybir.AxisListType
BF = ml_dtypes.bfloat16

ENGINES = ("sync", "tensor", "vector", "scalar", "gpsimd")

B, S, D = 2, 8192, 1024
NTOK = B * S
NCORES = 8
TPC = NTOK // NCORES


class Prog:

    def __init__(self, nc):
        self.nc = nc
        self.ops = {e: [] for e in ENGINES}
        self.cnt = {e: 0 for e in ENGINES}
        self.ndma = {e: 0 for e in ENGINES}
        self.markers = {e: 0 for e in ENGINES}
        self.last_w = {}
        self.readers = {}
        self.hw = {e: {} for e in ENGINES}

    def _marker(self, q, ndma_needed):
        if self.markers[q] >= ndma_needed:
            return 16 * self.markers[q]
        cur = self.ndma[q]
        self.ops[q].append(([(("D", q), 16 * cur)], None, 0))
        self.markers[q] = cur
        self.hw[q][("D", q)] = max(self.hw[q].get(("D", q), 0), 16 * cur)
        return 16 * cur

    def emit(self, eng, fn, reads=(), writes=(), dma=False):
        need = {}

        def add(e2, v, isdma):
            if isdma:
                sem = ("D", e2)
                if e2 != eng:
                    v = self._marker(e2, v // 16)
                else:
                    v = self._marker(e2, v // 16)
            else:
                if e2 == eng and eng == "tensor" and not dma:
                    return
                sem = ("C", e2)
            if need.get(sem, 0) < v:
                need[sem] = v

        for k in reads:
            lw = self.last_w.get(k)
            if lw is not None:
                add(*lw)
        for k in writes:
            lw = self.last_w.get(k)
            if lw is not None:
                add(*lw)
            for (e2, isdma), v in self.readers.get(k, {}).items():
                add(e2, v, isdma)
        waits = []
        for sem in sorted(need):
            v = need[sem]
            if v > self.hw[eng].get(sem, 0):
                waits.append((sem, v))
                self.hw[eng][sem] = v
        if dma:
            self.ndma[eng] += 1
            my = (eng, 16 * self.ndma[eng], True)
        else:
            self.cnt[eng] += 1
            my = (eng, self.cnt[eng], False)
        for k in writes:
            self.last_w[k] = my
            self.readers[k] = {}
        for k in reads:
            self.readers.setdefault(k, {})[(eng, dma)] = my[1]
        self.ops[eng].append((waits, fn, 16 if dma else 1))
        return my

    def barrier(self):
        vals = {}
        for q in ENGINES:
            if self.ndma[q] > 0:
                vals[("D", q)] = self._marker(q, self.ndma[q])
            if self.cnt[q] > 0:
                vals[("C", q)] = self.cnt[q]
        for e in ENGINES:
            waits = []
            for sem, v in sorted(vals.items()):
                if sem == ("C", e):
                    continue
                if v > self.hw[e].get(sem, 0):
                    waits.append((sem, v))
                    self.hw[e][sem] = v
            if waits:
                self.ops[e].append((waits, None, 0))

    def dma(self, eng, out, in_, reads=(), writes=()):
        return self.emit(eng, lambda e, o=out, i=in_: e.dma_start(out=o, in_=i),
                         reads=reads, writes=writes, dma=True)

    def finalize(self, es):
        sems = {}
        for e in ENGINES:
            if self.cnt[e] > 0:
                sems[("C", e)] = es.enter_context(self.nc.semaphore(f"c_{e}"))
            if self.ndma[e] > 0:
                sems[("D", e)] = es.enter_context(self.nc.semaphore(f"d_{e}"))
        final = [(sem, self.cnt[e] if sem[0] == "C" else 16 * self.ndma[e])
                 for e in ENGINES for sem in [("C", e), ("D", e)] if sem in sems]
        block = es.enter_context(self.nc.Block())
        for e in ENGINES:
            ops = self.ops[e]
            if not ops and e != "gpsimd":
                continue

            def mk(e, ops):
                def body(engobj):
                    for waits, fn, amt in ops:
                        for (sem, v) in waits:
                            engobj.wait_ge(sems[sem], v)
                        if fn is None:
                            continue
                        inst = fn(engobj)
                        inst.then_inc(sems[("D" if amt == 16 else "C", e)], amt)
                    if e == "gpsimd":
                        for sem, v in final:
                            if v > 0 and self.hw["gpsimd"].get(sem, 0) < v:
                                engobj.wait_ge(sems[sem], v)
                return body

            getattr(block, e)(mk(e, ops))


def bias_host(b):
    return np.ascontiguousarray(np.asarray(b, dtype=np.float32).reshape(-1, 128).T)


def run_prog(progfn, in_maps, sim=False, **kwargs):
    if sim:
        from concourse import bacc
        from concourse.bass_interp import CoreSim
        nc = bacc.Bacc("TRN2", target_bir_lowering=False, debug=True)
    else:
        nc = bass.Bass("TRN2", target_bir_lowering=False)
    with contextlib.ExitStack() as es:
        p = Prog(nc)
        progfn(nc, p, es)
        p.finalize(es)
    if sim:
        from concourse.bass_interp import CoreSim
        nc.compile()
        s = CoreSim(nc)
        outs = [a.memorylocations[0].name for a in nc.m.functions[0].allocations
                if getattr(a, "kind", None) == "ExternalOutput"]
        for name, arr in in_maps[0].items():
            s.tensor(name)[:] = arr
        s.simulate(check_with_hw=False)
        return [{k: np.asarray(s.tensor(k)) for k in outs}]
    res = run_bass_kernel_spmd(nc, in_maps, core_ids=list(range(len(in_maps))), **kwargs)
    return res.results



class TL:
    def __init__(self, nc, p, es, T=TPC, tt=512):
        self.nc, self.p, self.es, self.T, self.tt = nc, p, es, T, tt
        self.njt = T // tt
        sb = self.sb = lambda n, s, d: es.enter_context(nc.sbuf_tensor(n, s, d))
        ps = self.ps = lambda n, s, d: es.enter_context(nc.psum_tensor(n, s, d))
        self.ones_col = sb("ones_col", [128, 1], F32)
        self.ones_row = sb("ones_row", [1, 128], F32)
        p.emit("vector", lambda e: e.memset(self.ones_col[:], 1.0), writes=[("ones_col", 0)])
        p.emit("vector", lambda e: e.memset(self.ones_row[:], 1.0), writes=[("ones_row", 0)])
        self.sq0 = sb("sq0", [128, tt], F32)
        self.sq1 = sb("sq1", [128, tt], F32)
        self.r0 = sb("r0", [1, tt], F32)
        self.r1 = sb("r1", [1, tt], F32)
        self.r2 = sb("r2", [1, tt], F32)
        self.prx = ps("prx", [1, 512], F32)
        self.prq = ps("prq", [1, 512], F32)
        self.pbS = ps("pbS", [128, 512], F32)
        self.pbC = ps("pbC", [128, 512], F32)
        self.pmm = [ps(f"pmm{i}", [128, 512], F32) for i in range(4)]
        self._mm_rot = 0
        self._uid = 0
        self._consts = {}
        self._cst = sb("consts", [128, 16], F32)

    def const(self, value):
        if value not in self._consts:
            assert len(self._consts) < 16
            k = len(self._consts)
            self.p.emit("vector", lambda e, k=k, v=value: e.memset(self._cst[:, k:k + 1], v),
                        writes=[("const", value)])
            self._consts[value] = k
        k = self._consts[value]
        return self._cst[:, k:k + 1]

    def uid(self):
        self._uid += 1
        return self._uid

    def sbuf_fm(self, name, rows, T, dtype):
        assert rows % 128 == 0
        return self.sb(name, [128, rows // 128, T], dtype)

    def load_fm(self, name, hbm_ap, rows, T, dtype, eng="sync"):
        t = self.sbuf_fm(name, rows, T, dtype)
        nchunk = rows // 128
        src = hbm_ap.rearrange("(c p) t -> p c t", p=128)
        self.p.dma(eng, t[:, :, :], src, writes=[(name, c) for c in range(nchunk)])
        return t

    def load_weight_bf(self, name, hbm_ap, kin, nfo, stage_cols=None, eng="sync"):
        nchunk = kin // 128
        wb = self.sb(name + "_bf", [128, nchunk, nfo], BF16)
        stg = [self.sb(f"{name}_stg{i}", [128, nfo], F32) for i in range(2)]
        src = hbm_ap.rearrange("(c p) f -> c p f", p=128)
        for c in range(nchunk):
            s = stg[c % 2]
            skey = (name + "_stg", c % 2)
            self.p.dma(eng, s[:, :], src[c], reads=[], writes=[skey])
            self.p.emit("vector", lambda e, s=s, c=c: e.tensor_copy(out=wb[:, c, :], in_=s[:, :]),
                        reads=[skey], writes=[(name, c)])
        return wb

    def load_rows_f32(self, name, hbm_ap, n):
        assert n % 128 == 0
        t = self.sb(name, [128, n // 128], F32)
        self.p.dma("sync", t[:, :], hbm_ap, writes=[(name, 0)])
        return t

    def emit_ln(self, xsb, xname, nchunk, out_bf, oname, out_f32=None, o32name=None,
               y32_tile=None, y32name=None, tile_cb=None, rms_w=None, eps=1e-5):
        p, tt = self.p, self.tt
        nd = nchunk * 128
        for j in range(self.njt):
            js = slice(j * tt, (j + 1) * tt)
            u = self.uid()
            for c in range(nchunk):
                p.emit("tensor", lambda e, c=c, js=js: e.matmul(
                    out=self.prx[:, :], lhsT=self.ones_col[:], rhs=xsb[:, c, js],
                    start=(c == 0), stop=(c == nchunk - 1)),
                    reads=[("ones_col", 0), (xname, c)], writes=[("prx",)])
            for c in range(nchunk):
                sq = self.sq0 if c % 2 == 0 else self.sq1
                sqk = ("sq", c % 2)
                p.emit("scalar", lambda e, c=c, sq=sq, js=js: e.activation(
                    out=sq[:, :], in_=xsb[:, c, js], func=AF.Square),
                    reads=[(xname, c)], writes=[sqk])
                p.emit("tensor", lambda e, c=c, sq=sq: e.matmul(
                    out=self.prq[:, :], lhsT=self.ones_col[:], rhs=sq[:, :],
                    start=(c == 0), stop=(c == nchunk - 1)),
                    reads=[("ones_col", 0), sqk], writes=[("prq",)])
            p.emit("vector", lambda e: e.tensor_scalar_mul(out=self.r0[:, :], in0=self.prx[:, :], scalar1=-1.0 / nd),
                   reads=[("prx",)], writes=[("r0",)])
            p.emit("vector", lambda e: e.tensor_scalar_mul(out=self.r1[:, :], in0=self.prq[:, :], scalar1=1.0 / nd),
                   reads=[("prq",)], writes=[("r1",)])
            p.emit("vector", lambda e: e.tensor_mul(out=self.r2[:, :], in0=self.r0[:, :], in1=self.r0[:, :]),
                   reads=[("r0",)], writes=[("r2",)])
            p.emit("vector", lambda e: e.tensor_sub(out=self.r1[:, :], in0=self.r1[:, :], in1=self.r2[:, :]),
                   reads=[("r1",), ("r2",)], writes=[("r1",)])
            epsap = self.const(eps)
            p.emit("scalar", lambda e: e.activation(out=self.r1[:, :], in_=self.r1[:, :], func=AF.Sqrt,
                                                    bias=epsap[0:1, :]),
                   reads=[("r1",), ("const", eps)], writes=[("r1",)])
            p.emit("vector", lambda e: e.reciprocal(out=self.r1[:, :], in_=self.r1[:, :]),
                   reads=[("r1",)], writes=[("r1",)])
            p.emit("vector", lambda e: e.tensor_mul(out=self.r2[:, :], in0=self.r0[:, :], in1=self.r1[:, :]),
                   reads=[("r0",), ("r1",)], writes=[("r2",)])
            p.emit("tensor", lambda e: e.matmul(out=self.pbS[:, :], lhsT=self.ones_row[:], rhs=self.r1[:, :],
                                                start=True, stop=True),
                   reads=[("ones_row", 0), ("r1",)], writes=[("pbS",)])
            p.emit("tensor", lambda e: e.matmul(out=self.pbC[:, :], lhsT=self.ones_row[:], rhs=self.r2[:, :],
                                                start=True, stop=True),
                   reads=[("ones_row", 0), ("r2",)], writes=[("pbC",)])
            for c in range(nchunk):
                sq = self.sq0 if c % 2 == 0 else self.sq1
                sqk = ("sq", c % 2)
                p.emit("vector", lambda e, c=c, sq=sq, js=js: e.tensor_mul(out=sq[:, :], in0=xsb[:, c, js], in1=self.pbS[:, :]),
                       reads=[(xname, c), ("pbS",)], writes=[sqk])
                if out_bf is not None:
                    p.emit("vector", lambda e, c=c, sq=sq, js=js: e.tensor_add(out=out_bf[:, c, js], in0=sq[:, :], in1=self.pbC[:, :]),
                           reads=[sqk, ("pbC",)], writes=[(oname, c)])
                if out_f32 is not None:
                    p.emit("vector", lambda e, c=c, sq=sq, js=js: e.tensor_add(out=out_f32[:, c, js], in0=sq[:, :], in1=self.pbC[:, :]),
                           reads=[sqk, ("pbC",)], writes=[(o32name, c)])
                if y32_tile is not None:
                    p.emit("vector", lambda e, c=c, sq=sq: e.tensor_add(out=y32_tile[:, c, :], in0=sq[:, :], in1=self.pbC[:, :]),
                           reads=[sqk, ("pbC",)], writes=[(y32name, c)])
            if tile_cb is not None:
                tile_cb(j)

    def emit_mm(self, act, aname, ncin, w, wname, nfo, epilogue, jlist=None):
        p, tt = self.p, self.tt
        nfob = nfo // 128
        for j in (range(self.njt) if jlist is None else jlist):
            js = slice(j * tt, (j + 1) * tt)
            for fo in range(nfob):
                pm = self.pmm[self._mm_rot % len(self.pmm)]
                pmkey = ("pmm", self._mm_rot % len(self.pmm))
                self._mm_rot += 1
                for c in range(ncin):
                    p.emit("tensor", lambda e, c=c, fo=fo, pm=pm, js=js: e.matmul(
                        out=pm[:, :tt], lhsT=w[:, c, fo * 128:(fo + 1) * 128], rhs=act[:, c, js],
                        start=(c == 0), stop=(c == ncin - 1)),
                        reads=[(wname, c), (aname, c)], writes=[pmkey])
                epilogue(fo, j, pm[:, :tt], pmkey)



def mk_l0a_qkv(T=TPC, dbg=False):
    def prog(nc, p, es):
        X = nc.dram_tensor("x_fm", [D, T], F32, kind="ExternalInput").ap()
        W = nc.dram_tensor("w_qkv", [D, 3 * D], F32, kind="ExternalInput").ap()
        Bq = nc.dram_tensor("b_qkv", [128, (3 * D) // 128], F32, kind="ExternalInput").ap()
        O = nc.dram_tensor("qkv_fm", [3 * D, T], BF16, kind="ExternalOutput").ap()
        tl = TL(nc, p, es, T=T)
        xsb = tl.load_fm("x", X, D, T, F32)
        wsb = tl.load_weight_bf("w", W, D, 3 * D)
        bsb = tl.load_rows_f32("b", Bq, 3 * D)
        ybf = tl.sbuf_fm("y", D, T, BF16)
        tl.emit_ln(xsb, "x", 8, ybf, "y")
        if dbg:
            Yo = nc.dram_tensor("y_dbg", [D, T], BF16, kind="ExternalOutput").ap()
            p.dma("gpsimd", Yo.rearrange("(c p) t -> p c t", p=128), ybf[:, :, :],
                  reads=[("y", c) for c in range(8)])
        eo = [tl.sb(f"eo{i}", [128, 512], BF16) for i in range(3)]
        Ore = O.rearrange("(fo p) t -> fo p t", p=128)
        ecnt = [0]

        def epi(fo, j, pmap, pkey):
            js = slice(j * tl.tt, (j + 1) * tl.tt)
            i = ecnt[0] % 3
            ecnt[0] += 1
            ek = ("eo", i)
            p.emit("vector", lambda e: e.tensor_scalar_add(
                out=eo[i][:, :], in0=pmap, scalar1=bsb[:, fo:fo + 1]),
                reads=[pkey, ("b", 0)], writes=[ek])
            p.dma("gpsimd", Ore[fo][:, js], eo[i][:, :], reads=[ek])

        tl.emit_mm(ybf, "y", 8, wsb, "w", 3 * D, epi)
    return prog





def mk_l0_ffn(T=TPC, dbg=False):
    E, HH = 16, 512

    def prog(nc, p, es):
        X = nc.dram_tensor("x_fm", [D, T], F32, kind="ExternalInput").ap()
        WGT = nc.dram_tensor("wg_t", [D, E], F32, kind="ExternalInput").ap()
        W1 = nc.dram_tensor("w1", [E, D, 2 * HH], F32, kind="ExternalInput").ap()
        W2 = nc.dram_tensor("w2", [E, HH, D], F32, kind="ExternalInput").ap()
        WS1 = nc.dram_tensor("ws1", [D, 2048], F32, kind="ExternalInput").ap()
        WS2 = nc.dram_tensor("ws2", [1024, D], F32, kind="ExternalInput").ap()
        IDF = nc.dram_tensor("ident_f32", [128, 128], F32, kind="ExternalInput").ap()
        O = nc.dram_tensor("xout_fm", [D, T], F32, kind="ExternalOutput").ap()
        if dbg:
            OCW = nc.dram_tensor("cw_dbg", [128, T // 128, E], F32, kind="ExternalOutput").ap()

        tl = TL(nc, p, es, T=T)
        sb, ps = tl.sb, tl.ps
        njt = tl.njt
        nsub = T // 128

        idsb = sb("idsb", [128, 128], F32)
        p.dma("sync", idsb[:, :], IDF, writes=[("id",)])
        wgt = sb("wgt", [128, 8, E], F32)
        p.dma("sync", wgt[:, :, :], WGT.rearrange("(c p) e -> p c e", p=128),
              writes=[("wgt",)])
        ybf = tl.sbuf_fm("y", D, T, BF16)
        zbf = tl.sbuf_fm("z", D, T, BF16)
        cw_all = sb("cw_all", [128, nsub, E], F32)
        lg = sb("lg", [16, 512], F32)
        gs = [sb(f"gs{i}", [128, E], F32) for i in range(5)]
        gr = [sb(f"gr{i}", [128, 1], F32) for i in range(4)]

        def gate_tile(j, y32t):
            plg = tl.pmm[0]
            for c in range(8):
                p.emit("tensor", lambda e, c=c: e.matmul(
                    out=plg[0:E, :], lhsT=wgt[:, c, :], rhs=y32t[:, c, :],
                    start=(c == 0), stop=(c == 7)),
                    reads=[("wgt",), ("y32", c)], writes=[("pmm", 0)])
            p.emit("vector", lambda e: e.tensor_copy(out=lg[:, :], in_=plg[0:E, :]),
                   reads=[("pmm", 0)], writes=[("lg",)])
            for i in range(4):
                ti = j * 4 + i
                ltm, ptm, g1, l2, cwtm = gs
                m1, nm1, ssum, m2 = gr
                ptf = tl.pmm[1]
                p.emit("tensor", lambda e, i=i: e.transpose(
                    out=ptf[0:128, 0:E], in_=lg[:, i * 128:(i + 1) * 128], identity=idsb[0:E, 0:E]),
                    reads=[("lg",), ("id",)], writes=[("pmm", 1)])
                p.emit("vector", lambda e: e.tensor_copy(out=ltm[:, :], in_=ptf[0:128, 0:E]),
                       reads=[("pmm", 1)], writes=[("ltm",)])
                p.emit("vector", lambda e: e.reduce_max(out=m1[:, :], in_=ltm[:, :], axis=AX.X),
                       reads=[("ltm",)], writes=[("m1",)])
                p.emit("vector", lambda e: e.tensor_scalar_mul(out=nm1[:, :], in0=m1[:, :], scalar1=-1.0),
                       reads=[("m1",)], writes=[("nm1",)])
                p.emit("scalar", lambda e: e.activation(out=ptm[:, :], in_=ltm[:, :], func=AF.Exp,
                                                        bias=nm1[:, 0:1], accum_out=ssum[:, 0:1]),
                       reads=[("ltm",), ("nm1",)], writes=[("ptm",), ("ssum",)])
                p.emit("vector", lambda e: e.reciprocal(out=ssum[:, :], in_=ssum[:, :]),
                       reads=[("ssum",)], writes=[("ssum",)])
                p.emit("vector", lambda e: e.tensor_scalar_mul(out=ptm[:, :], in0=ptm[:, :], scalar1=ssum[:, 0:1]),
                       reads=[("ptm",), ("ssum",)], writes=[("ptm",)])
                p.emit("vector", lambda e: e.tensor_scalar(out=g1[:, :], in0=ltm[:, :], scalar1=m1[:, 0:1],
                                                           scalar2=None, op0=ALU.is_ge),
                       reads=[("ltm",), ("m1",)], writes=[("g1",)])
                p.emit("vector", lambda e: e.scalar_tensor_tensor(out=l2[:, :], in0=g1[:, :], scalar=-1e30,
                                                                  in1=ltm[:, :], op0=ALU.mult, op1=ALU.add),
                       reads=[("g1",), ("ltm",)], writes=[("l2",)])
                p.emit("vector", lambda e: e.reduce_max(out=m2[:, :], in_=l2[:, :], axis=AX.X),
                       reads=[("l2",)], writes=[("m2",)])
                p.emit("vector", lambda e: e.tensor_scalar(out=g1[:, :], in0=ltm[:, :], scalar1=m2[:, 0:1],
                                                           scalar2=None, op0=ALU.is_ge),
                       reads=[("ltm",), ("m2",)], writes=[("g1",)])
                p.emit("vector", lambda e, ti=ti: e.tensor_mul(out=cw_all[:, ti, :], in0=ptm[:, :], in1=g1[:, :]),
                       reads=[("ptm",), ("g1",)], writes=[("cw", ti)])

        with contextlib.ExitStack() as es2:
            tl2 = contextlib.nullcontext()
            xsb = es2.enter_context(nc.sbuf_tensor("x", [128, 8, T], F32))
            p.dma("sync", xsb[:, :, :], X.rearrange("(c p) t -> p c t", p=128),
                  writes=[("x", c) for c in range(8)])
            y32t = es2.enter_context(nc.sbuf_tensor("y32t", [128, 8, 512], F32))
            tl.emit_ln(xsb, "x", 8, ybf, "y", y32_tile=y32t, y32name="y32",
                       tile_cb=lambda j: gate_tile(j, y32t))
        p.barrier()
        if dbg:
            p.dma("gpsimd", OCW, cw_all[:, :, :], reads=[("cw", ti) for ti in range(nsub)])

        h = [tl.sbuf_fm(f"h{i}", HH, T, BF16) for i in range(2)]
        stg = [sb(f"stgw{i}", [128, 2048], F32) for i in range(2)]
        w1b = [sb(f"w1b{i}", [128, 8, 2 * HH], BF16) for i in range(2)]
        w2b = [sb(f"w2b{i}", [128, 4, D], BF16) for i in range(2)]
        hs = sb("hs", [128, 8, 512], BF16)
        cwb = [sb(f"cwb{i}", [128, 512], BF16) for i in range(2)]
        sg = [sb(f"sg{i}", [128, 512], F32) for i in range(2)]
        hwt = [sb(f"hwt{i}", [128, 512], BF16) for i in range(2)]
        nst = [0]

        def load_cast(dram_src, dst, dstkey, nchunk, ncols):
            for c in range(nchunk):
                s = stg[nst[0] % 2]
                skey = ("stgw", nst[0] % 2)
                nst[0] += 1
                p.dma("sync", s[:, 0:ncols], dram_src[c], writes=[skey])
                p.emit("vector", lambda e, s=s, c=c, ncols=ncols, dst=dst: e.tensor_copy(
                    out=dst[:, c, 0:ncols], in_=s[:, 0:ncols]),
                    reads=[skey], writes=[(dstkey, c)])

        def up_and_down(eidx, w1cur, w1key, w2cur, w2key, hcur, hkey, weighted, nuc, first):
            for j in range(njt):
                js = slice(j * 512, (j + 1) * 512)
                if weighted:
                    pcw = tl.pbS
                    for i in range(4):
                        ti = j * 4 + i
                        p.emit("tensor", lambda e, ti=ti, i=i, eidx=eidx: e.transpose(
                            out=pcw[:, i * 128:(i + 1) * 128],
                            in_=cw_all[:, ti, eidx:eidx + 1].broadcast_to([128, 128]),
                            identity=idsb[:, :]),
                            reads=[("cw", ti), ("id",)], writes=[("pbS",)])
                    cb = cwb[j % 2]
                    cbkey = ("cwb", j % 2)
                    p.emit("vector", lambda e, cb=cb: e.tensor_copy(out=cb[:, :], in_=pcw[:, :]),
                           reads=[("pbS",)], writes=[cbkey])
                for blk in range(nuc):
                    pu, pukey = tl.pmm[blk % 2], ("pmm", blk % 2)
                    pg, pgkey = tl.pmm[2 + blk % 2], ("pmm", 2 + blk % 2)
                    for c in range(8):
                        p.emit("tensor", lambda e, c=c, blk=blk, js=js, w1cur=w1cur, pu=pu: e.matmul(
                            out=pu[:, :], lhsT=w1cur[:, c, blk * 128:(blk + 1) * 128],
                            rhs=ybf[:, c, js], start=(c == 0), stop=(c == 7)),
                            reads=[(w1key, c), ("y", c)], writes=[pukey])
                    for c in range(8):
                        p.emit("tensor", lambda e, c=c, blk=blk, js=js, w1cur=w1cur, pg=pg, nuc=nuc: e.matmul(
                            out=pg[:, :], lhsT=w1cur[:, c, (nuc + blk) * 128:(nuc + blk + 1) * 128],
                            rhs=ybf[:, c, js], start=(c == 0), stop=(c == 7)),
                            reads=[(w1key, c), ("y", c)], writes=[pgkey])
                    sgc, sgkey = sg[blk % 2], ("sg", blk % 2)
                    p.emit("scalar", lambda e, sgc=sgc, pg=pg: e.activation(out=sgc[:, :], in_=pg[:, :], func=AF.Silu),
                           reads=[pgkey], writes=[sgkey])
                    if weighted:
                        ht, htkey = hwt[blk % 2], ("hwt", blk % 2)
                        p.emit("vector", lambda e, ht=ht, sgc=sgc, pu=pu: e.tensor_mul(out=ht[:, :], in0=pu[:, :], in1=sgc[:, :]),
                               reads=[pukey, sgkey], writes=[htkey])
                        p.emit("gpsimd", lambda e, ht=ht, blk=blk, js=js, hcur=hcur, cb=cb: e.tensor_mul(
                            out=hcur[:, blk, js], in0=ht[:, :], in1=cb[:, :]),
                            reads=[htkey, cbkey], writes=[(hkey, blk, j)])
                    else:
                        p.emit("vector", lambda e, blk=blk, js=js, hcur=hcur, sgc=sgc, pu=pu: e.tensor_mul(
                            out=hcur[:, blk, js], in0=pu[:, :], in1=sgc[:, :]),
                               reads=[pukey, sgkey], writes=[(hkey, blk, j)])
                for fo in range(8):
                    pz, pzkey = tl.pmm[(fo % 4)], ("pmm", fo % 4)
                    for blk in range(nuc):
                        p.emit("tensor", lambda e, blk=blk, fo=fo, js=js, w2cur=w2cur, hcur=hcur, pz=pz, j=j: e.matmul(
                            out=pz[:, :], lhsT=w2cur[:, blk, fo * 128:(fo + 1) * 128],
                            rhs=hcur[:, blk, js], start=(blk == 0), stop=(blk == nuc - 1)),
                            reads=[(w2key, blk), (hkey, blk, j)], writes=[pzkey])
                    if first:
                        p.emit("vector", lambda e, fo=fo, js=js, pz=pz: e.tensor_copy(out=zbf[:, fo, js], in_=pz[:, :]),
                               reads=[pzkey], writes=[("z", fo, j)])
                    else:
                        p.emit("vector", lambda e, fo=fo, js=js, pz=pz: e.tensor_add(out=zbf[:, fo, js], in0=zbf[:, fo, js], in1=pz[:, :]),
                               reads=[pzkey, ("z", fo, j)], writes=[("z", fo, j)])

        for eidx in range(E):
            b = eidx % 2
            load_cast(W1[eidx].rearrange("(c p) f -> c p f", p=128), w1b[b], f"w1b{b}", 8, 2 * HH)
            load_cast(W2[eidx].rearrange("(c p) f -> c p f", p=128), w2b[b], f"w2b{b}", 4, D)
            up_and_down(eidx, w1b[b], f"w1b{b}", w2b[b], f"w2b{b}", h[b], f"h{b}",
                        weighted=True, nuc=4, first=(eidx == 0))

        load_cast(WS1[:, 0:1024].rearrange("(c p) f -> c p f", p=128), w1b[0], "w1b0", 8, 1024)
        load_cast(WS1[:, 1024:2048].rearrange("(c p) f -> c p f", p=128), w1b[1], "w1b1", 8, 1024)
        load_cast(WS2[0:512].rearrange("(c p) f -> c p f", p=128), w2b[0], "w2b0", 4, D)
        load_cast(WS2[512:1024].rearrange("(c p) f -> c p f", p=128), w2b[1], "w2b1", 4, D)
        for j in range(njt):
            js = slice(j * 512, (j + 1) * 512)
            for blk in range(8):
                pu, pukey = tl.pmm[blk % 2], ("pmm", blk % 2)
                pg, pgkey = tl.pmm[2 + blk % 2], ("pmm", 2 + blk % 2)
                for c in range(8):
                    p.emit("tensor", lambda e, c=c, blk=blk, js=js, pu=pu: e.matmul(
                        out=pu[:, :], lhsT=w1b[0][:, c, blk * 128:(blk + 1) * 128],
                        rhs=ybf[:, c, js], start=(c == 0), stop=(c == 7)),
                        reads=[("w1b0", c), ("y", c)], writes=[pukey])
                for c in range(8):
                    p.emit("tensor", lambda e, c=c, blk=blk, js=js, pg=pg: e.matmul(
                        out=pg[:, :], lhsT=w1b[1][:, c, blk * 128:(blk + 1) * 128],
                        rhs=ybf[:, c, js], start=(c == 0), stop=(c == 7)),
                        reads=[("w1b1", c), ("y", c)], writes=[pgkey])
                sgc, sgkey = sg[blk % 2], ("sg", blk % 2)
                p.emit("scalar", lambda e, sgc=sgc, pg=pg: e.activation(out=sgc[:, :], in_=pg[:, :], func=AF.Silu),
                       reads=[pgkey], writes=[sgkey])
                p.emit("vector", lambda e, blk=blk, sgc=sgc, pu=pu: e.tensor_mul(out=hs[:, blk, :], in0=pu[:, :], in1=sgc[:, :]),
                       reads=[pukey, sgkey], writes=[("hs", blk)])
            for fo in range(8):
                pz, pzkey = tl.pmm[fo % 4], ("pmm", fo % 4)
                for blk in range(8):
                    wsel, wselkey = (w2b[0], "w2b0") if blk < 4 else (w2b[1], "w2b1")
                    p.emit("tensor", lambda e, blk=blk, fo=fo, wsel=wsel, pz=pz: e.matmul(
                        out=pz[:, :], lhsT=wsel[:, blk % 4, fo * 128:(fo + 1) * 128],
                        rhs=hs[:, blk, :], start=(blk == 0), stop=(blk == 7)),
                        reads=[(wselkey, blk % 4), ("hs", blk)], writes=[pzkey])
                p.emit("vector", lambda e, fo=fo, js=js, pz=pz: e.tensor_add(out=zbf[:, fo, js], in0=zbf[:, fo, js], in1=pz[:, :]),
                       reads=[pzkey, ("z", fo, j)], writes=[("z", fo, j)])

        xc = [sb(f"xc{i}", [128, T], F32) for i in range(2)]
        Xre = X.rearrange("(c p) t -> c p t", p=128)
        Ore = O.rearrange("(c p) t -> c p t", p=128)
        for fo in range(8):
            b = fo % 2
            xck = ("xc", b)
            p.dma("sync", xc[b][:, :], Xre[fo], writes=[xck])
            p.emit("vector", lambda e, fo=fo, b=b: e.tensor_add(
                out=xc[b][:, :], in0=xc[b][:, :], in1=zbf[:, fo, :]),
                reads=[xck] + [("z", fo, j) for j in range(njt)], writes=[xck])
            p.dma("gpsimd", Ore[fo], xc[b][:, :], reads=[xck])
    return prog


def mk_flash(S_=8192, npair=4, dv=64, alibi=True, krange=None, maskid=None, nmasks=4,
             Sk=None, dbg=False):
    if Sk is None:
        Sk = S_
    NKT = Sk // 128
    NQB = S_ // 512
    NKB = Sk // 512
    if krange is None:
        krange = lambda qt: range(0, qt * 4 + 4)
    if maskid is None:
        maskid = lambda qt, kt: (kt - 4 * qt) if kt - 4 * qt >= 0 else None

    def prog(nc, p, es):
        Q = nc.dram_tensor("qp", [npair, 64, S_], BF16, kind="ExternalInput").ap()
        K_ = nc.dram_tensor("kp", [npair, 64, Sk], BF16, kind="ExternalInput").ap()
        V = nc.dram_tensor("vp", [npair, 128, Sk // 128, dv], BF16, kind="ExternalInput").ap()
        RAMP = nc.dram_tensor("ramp", [npair, 128, NKT], F32, kind="ExternalInput").ap()
        AUGQ = nc.dram_tensor("augq", [npair, 1, S_], F32, kind="ExternalInput").ap()
        MSK = nc.dram_tensor("masks", [nmasks, 128, 512], F32, kind="ExternalInput").ap()
        O = nc.dram_tensor("attnout", [npair, dv, S_], BF16, kind="ExternalOutput").ap()

        sb = lambda n, s, d: es.enter_context(nc.sbuf_tensor(n, s, d))
        ps = lambda n, s, d: es.enter_context(nc.psum_tensor(n, s, d))
        onesr = sb("onesr", [1, 128], F32)
        onesc = sb("onesc", [128, 1], F32)
        p.emit("vector", lambda e: e.memset(onesr[:, :], 1.0), writes=[("onesr",)])
        p.emit("vector", lambda e: e.memset(onesc[:, :], 1.0), writes=[("onesc",)])
        msk = sb("msk", [128, nmasks, 512], F32)
        p.dma("sync", msk[:, :, :], MSK.rearrange("m p q -> p m q"), writes=[("msk",)])

        qh = [sb(f"qh{i}", [65, S_], BF16) for i in range(2)]
        kh = [sb(f"kh{i}", [65, Sk], BF16) for i in range(2)]
        vt = [sb(f"vt{i}", [128, NKT, dv + 1], BF16) for i in range(2)]
        rmp = [sb(f"rmp{i}", [128, NKT], F32) for i in range(2)]
        qr = sb("qr", [1, 512], F32)
        kr = sb("kr", [1, 1], F32)
        sqt = sb("sqt", [64, 512], F32)
        kms = sb("kms", [1, 1], F32)
        outsb = [sb(f"outsb{i}", [dv, 512], BF16) for i in range(2)]
        rr = [sb(f"rr{i}", [1, 512], F32) for i in range(2)]
        rrb = [sb(f"rrb{i}", [dv, 512], F32) for i in range(2)]
        aqt = sb("aqt", [1, 512], F32)
        NPA = 4
        psA = [ps(f"psA{i}", [128, 512], F32) for i in range(NPA)]
        psO = [ps(f"psO{i}", [dv + 1, 512], F32) for i in range(2)]
        psR = ps("psR", [1, 512], F32)
        psB = ps("psB", [dv, 512], F32)
        pT = [sb(f"pT{i}", [128, 512], BF16) for i in range(NPA)]

        def load_pair(pi, b):
            kq = ("qh", b)
            kk = ("kh", b)
            kv = ("vt", b)
            p.dma("sync", qh[b][0:64, :], Q[pi], writes=[kq])
            p.dma("sync", kh[b][0:64, :], K_[pi], writes=[kk])
            p.dma("sync", vt[b][:, :, 0:dv], V[pi], writes=[kv])
            p.dma("sync", rmp[b][:, :], RAMP[pi], writes=[("rmp", b)])
            p.emit("vector", lambda e, b=b: e.memset(vt[b][:, :, dv:dv + 1], 1.0),
                   reads=[kv], writes=[kv])
            p.emit("vector", lambda e, b=b: e.memset(kh[b][64:65, :], 1.0),
                   reads=[kk], writes=[kk])
            p.emit("vector", lambda e, b=b: e.tensor_scalar_mul(out=qh[b][0:64, :], in0=qh[b][0:64, :], scalar1=0.125),
                   reads=[kq], writes=[kq])
            for t in range(NKB):
                js = slice(t * 512, (t + 1) * 512)
                p.emit("scalar", lambda e, b=b, js=js: e.activation(out=sqt[:, :], in_=kh[b][0:64, js], func=AF.Square),
                       reads=[kk], writes=[("sqt",)])
                p.emit("tensor", lambda e: e.matmul(out=psR[:, :], lhsT=onesc[0:64, :], rhs=sqt[:, :],
                                                    start=True, stop=True),
                       reads=[("sqt",), ("onesc",)], writes=[("psR",)])
                p.emit("vector", lambda e: e.reduce_max(out=kr[:, :], in_=psR[:, :], axis=AX.X),
                       reads=[("psR",)], writes=[("kr",)])
                if t == 0:
                    p.emit("vector", lambda e: e.tensor_copy(out=kms[:, :], in_=kr[:, :]),
                           reads=[("kr",)], writes=[("kms",)])
                else:
                    p.emit("vector", lambda e: e.tensor_max(out=kms[:, :], in0=kms[:, :], in1=kr[:, :]),
                           reads=[("kr",), ("kms",)], writes=[("kms",)])
            p.emit("scalar", lambda e: e.activation(out=kms[:, :], in_=kms[:, :], func=AF.Sqrt),
                   reads=[("kms",)], writes=[("kms",)])
            for t in range(NQB):
                js = slice(t * 512, (t + 1) * 512)
                p.emit("scalar", lambda e, b=b, js=js: e.activation(out=sqt[:, :], in_=qh[b][0:64, js], func=AF.Square),
                       reads=[kq], writes=[("sqt",)])
                p.emit("tensor", lambda e: e.matmul(out=psR[:, :], lhsT=onesc[0:64, :], rhs=sqt[:, :],
                                                    start=True, stop=True),
                       reads=[("sqt",), ("onesc",)], writes=[("psR",)])
                p.emit("scalar", lambda e: e.activation(out=qr[:, :], in_=psR[:, :], func=AF.Sqrt),
                       reads=[("psR",)], writes=[("qr",)])
                p.emit("vector", lambda e: e.tensor_scalar_mul(out=qr[:, :], in0=qr[:, :], scalar1=kms[0:1, 0:1]),
                       reads=[("qr",), ("kms",)], writes=[("qr",)])
                p.dma("sync", aqt[:, :], AUGQ[pi][:, js], writes=[("aqt",)])
                p.emit("vector", lambda e, b=b, js=js: e.tensor_sub(out=qh[b][64:65, js], in0=aqt[:, :], in1=qr[:, :]),
                       reads=[("aqt",), ("qr",)], writes=[kq])

        pending = [None]

        def do_pair(pi, b):
            kq, kk, kv = ("qh", b), ("kh", b), ("vt", b)
            LA = 2
            for qt in range(NQB):
                js = slice(qt * 512, (qt + 1) * 512)
                kts = list(krange(qt))
                n = len(kts)
                ob = qt % 2

                def emit_qk(i, qt=qt, js=js, kts=kts):
                    kt = kts[i]
                    ab = i % NPA
                    pa, pakey = psA[ab], ("psA", ab)
                    p.emit("tensor", lambda e, kt=kt, js=js, b=b, pa=pa: e.matmul(
                        out=pa[:, :], lhsT=kh[b][:, kt * 128:(kt + 1) * 128], rhs=qh[b][:, js],
                        start=True, stop=True),
                        reads=[kk, kq], writes=[pakey])
                    mid = maskid(qt, kt)
                    if mid is not None:
                        p.emit("vector", lambda e, mid=mid, pa=pa: e.tensor_add(
                            out=pa[:, :], in0=pa[:, :], in1=msk[:, mid, :]),
                            reads=[pakey, ("msk",)], writes=[pakey])
                    pt, ptkey = pT[ab], ("pT", ab)
                    p.emit("scalar", lambda e, kt=kt, b=b, pa=pa, pt=pt: e.activation(
                        out=pt[:, :], in_=pa[:, :], func=AF.Exp, bias=rmp[b][:, kt:kt + 1]),
                        reads=[pakey, ("rmp", b)], writes=[ptkey])

                def emit_av(i, n=n, ob=ob, kts=kts):
                    kt = kts[i]
                    ab = i % NPA
                    pt, ptkey = pT[ab], ("pT", ab)
                    p.emit("tensor", lambda e, kt=kt, b=b, pt=pt, first=(i == 0), last=(i == n - 1), ob=ob: e.matmul(
                        out=psO[ob][:, :], lhsT=vt[b][:, kt, :], rhs=pt[:, :],
                        start=first, stop=last),
                        reads=[kv, ptkey], writes=[("psO", ob)])

                for i in range(n):
                    emit_qk(i)
                    if i == LA and pending[0] is not None:
                        pending[0]()
                        pending[0] = None
                    if i >= LA:
                        emit_av(i - LA)
                if pending[0] is not None:
                    pending[0]()
                    pending[0] = None
                for i in range(max(0, n - LA), n):
                    emit_av(i)

                def epi(pi=pi, qt=qt, ob=ob):
                    p.emit("scalar", lambda e: e.activation(out=rr[ob][:, :], in_=psO[ob][dv:dv + 1, :],
                                                            func=AF.Copy),
                           reads=[("psO", ob)], writes=[("rr", ob)])
                    p.emit("tensor", lambda e: e.matmul(out=psB[:, :], lhsT=onesr[:, 0:dv], rhs=rr[ob][:, :],
                                                        start=True, stop=True),
                           reads=[("rr", ob), ("onesr",)], writes=[("psB",)])
                    p.emit("vector", lambda e: e.reciprocal(out=rrb[ob][:, :], in_=psB[:, :]),
                           reads=[("psB",)], writes=[("rrb", ob)])
                    p.emit("vector", lambda e: e.tensor_mul(
                        out=outsb[ob][:, :], in0=psO[ob][0:dv, :], in1=rrb[ob][:, :]),
                        reads=[("psO", ob), ("rrb", ob)], writes=[("outsb", ob)])
                    p.dma("gpsimd", O[pi][:, qt * 512:(qt + 1) * 512], outsb[ob][:, :],
                          reads=[("outsb", ob)])
                pending[0] = epi

        load_pair(0, 0)
        if npair > 1:
            load_pair(1, 1)
        for pi in range(npair):
            do_pair(pi, pi % 2)
            if pi + 2 < npair:
                load_pair(pi + 2, (pi + 2) % 2)
        if pending[0] is not None:
            pending[0]()
            pending[0] = None
    return prog


def mk_resid_proj(T=TPC, din=1024, use_bias=True):
    nck = din // 128

    def prog(nc, p, es):
        X = nc.dram_tensor("x_fm", [D, T], F32, kind="ExternalInput").ap()
        A = nc.dram_tensor("a_fm", [din, T], BF16, kind="ExternalInput").ap()
        W = nc.dram_tensor("w", [din, D], F32, kind="ExternalInput").ap()
        if use_bias:
            Bv = nc.dram_tensor("b", [128, D // 128], F32, kind="ExternalInput").ap()
        O = nc.dram_tensor("xout_fm", [D, T], F32, kind="ExternalOutput").ap()
        tl = TL(nc, p, es, T=T)
        xsb = tl.load_fm("x", X, D, T, F32)
        asb = tl.load_fm("a", A, din, T, BF16)
        wsb = tl.load_weight_bf("w", W, din, D)
        bsb = tl.load_rows_f32("bo", Bv, D) if use_bias else None
        ost = [tl.sb(f"ost{i}", [128, 512], F32) for i in range(3)]
        cnt = [0]
        Ore = O.rearrange("(c p) t -> c p t", p=128)

        def epi(fo, j, pmap, pkey):
            js = slice(j * tl.tt, (j + 1) * tl.tt)
            i = cnt[0] % 3
            cnt[0] += 1
            ok = ("ost", i)
            if use_bias:
                p.emit("vector", lambda e, i=i, fo=fo: e.tensor_scalar_add(
                    out=ost[i][:, :], in0=pmap, scalar1=bsb[:, fo:fo + 1]),
                    reads=[pkey, ("bo", 0)], writes=[ok])
                p.emit("vector", lambda e, i=i, fo=fo, js=js: e.tensor_add(
                    out=ost[i][:, :], in0=ost[i][:, :], in1=xsb[:, fo, js]),
                    reads=[ok, ("x", fo)], writes=[ok])
            else:
                p.emit("vector", lambda e, i=i, fo=fo, js=js: e.tensor_add(
                    out=ost[i][:, :], in0=pmap, in1=xsb[:, fo, js]),
                    reads=[pkey, ("x", fo)], writes=[ok])
            p.dma("gpsimd", Ore[fo][:, js], ost[i][:, :], reads=[ok])

        tl.emit_mm(asb, "a", nck, wsb, "w", D, epi)
    return prog


def mk_l0b_scan(S_=8192, npair=4, dbg=False):
    C = 128
    NCH = S_ // C
    P4 = npair

    def prog(nc, p, es):
        Q = nc.dram_tensor("qs", [P4, 64, S_], F32, kind="ExternalInput").ap()
        K_ = nc.dram_tensor("ks", [P4, 64, S_], F32, kind="ExternalInput").ap()
        V = nc.dram_tensor("vs", [P4, 64, S_], F32, kind="ExternalInput").ap()
        AZ = nc.dram_tensor("az", [P4, 1, S_], F32, kind="ExternalInput").ap()
        BZ = nc.dram_tensor("bz", [P4, 1, S_], F32, kind="ExternalInput").ap()
        MTS = nc.dram_tensor("mtris", [128, 512], F32, kind="ExternalInput").ap()
        MTU = nc.dram_tensor("mtriui", [128, 512], F32, kind="ExternalInput").ap()
        IDF = nc.dram_tensor("ident_f32", [128, 128], F32, kind="ExternalInput").ap()
        O = nc.dram_tensor("out_tm", [P4, NCH, 128, 64], BF16, kind="ExternalOutput").ap()

        sb = lambda n, s, d: es.enter_context(nc.sbuf_tensor(n, s, d))
        ps = lambda n, s, d: es.enter_context(nc.psum_tensor(n, s, d))
        dbl = lambda n, s, d: [sb(n + "0", s, d), sb(n + "1", s, d)]
        onesr = sb("onesr", [1, 128], F32)
        p.emit("vector", lambda e: e.memset(onesr[:, :], 1.0), writes=[("onesr",)])
        onesc = sb("onesc", [128, 1], F32)
        p.emit("vector", lambda e: e.memset(onesc[:, :], 1.0), writes=[("onesc",)])
        zrow = sb("zrow", [P4, 128], F32)
        p.emit("vector", lambda e: e.memset(zrow[:, :], 0.0), writes=[("zrow",)])
        idsb = sb("idsb", [128, 128], F32)
        p.dma("sync", idsb[:, :], IDF, writes=[("id",)])
        mts = sb("mts", [128, 512], F32)
        p.dma("sync", mts[:, :], MTS, writes=[("mts",)])
        mtu = sb("mtu", [128, 512], F32)
        p.dma("sync", mtu[:, :], MTU, writes=[("mtu",)])
        azc = dbl("azc", [P4, 128], F32)
        bzc = dbl("bzc", [P4, 128], F32)
        AZr = AZ.rearrange("p one s -> (p one) s")
        BZr = BZ.rearrange("p one s -> (p one) s")

        kc = dbl("kc", [64, P4, 128], F32)
        qc = dbl("qc", [64, P4, 128], F32)
        vc = dbl("vc", [64, P4, 128], F32)
        kcn = dbl("kcn", [64, P4, 128], F32)
        qcn = dbl("qcn", [64, P4, 128], F32)
        sqs = dbl("sqs", [64, P4 * 128], F32)
        nrm = dbl("nrm", [1, P4 * 128], F32)
        rbc = dbl("rbc", [64, P4 * 128], F32)
        l4 = dbl("l4", [P4, 128], F32)
        lna = dbl("lna", [P4, 128], F32)
        ldc = dbl("ldc", [P4, 128], F32)
        lcb = dbl("lcb", [P4, 128], F32)
        cols = dbl("cols", [128, 16], F32)
        colse = dbl("colse", [128, 16], F32)
        kk = dbl("kk", [128, P4, 128], F32)
        kq = dbl("kq", [128, P4, 128], F32)
        lb = dbl("lb", [128, P4, 128], F32)
        ldif = dbl("ldif", [128, P4, 128], F32)
        dm = dbl("dm", [128, P4, 128], F32)
        dmx = dbl("dmx", [128, P4, 128], F32)
        lw = dbl("lw", [128, P4, 128], F32)
        lu = dbl("lu", [128, P4, 128], F32)
        m2t = dbl("m2t", [128, P4, 128], F32)
        m3t = dbl("m3t", [128, P4, 128], F32)
        ktm = dbl("ktm", [128, P4, 64], F32)
        vtm = dbl("vtm", [128, P4, 64], F32)
        kbar = dbl("kbar", [128, P4, 64], F32)
        Pp = dbl("Pp", [128, 2 * P4, 128], F32)
        Tt = dbl("Tt", [128, 2 * P4, 128], F32)
        Xs = dbl("Xs", [128, 2 * P4, 64], F32)
        wts = dbl("wts", [64, P4, 128], F32)
        wg = dbl("wg", [128, P4, 64], F32)
        o1 = dbl("o1", [128, P4, 64], F32)
        g1t = dbl("g1t", [64, P4, 64], F32)
        gsb = sb("gsb", [64, P4, 64], F32)
        och = dbl("och", [128, P4, 64], BF16)
        T2_ = sb("T2", [128, 2 * P4, 128], F32)
        P2_ = sb("P2", [128, 2 * P4, 128], F32)
        T4_ = sb("T4", [128, 2 * P4, 128], F32)
        P4b_ = sb("P4b", [128, 2 * P4, 128], F32)
        T8_ = sb("T8", [128, 2 * P4, 128], F32)
        T2 = [T2_, T2_]; P2 = [P2_, P2_]; T4 = [T4_, T4_]
        P4b = [P4b_, P4b_]; T8 = [T8_, T8_]
        pA = [ps(f"pA{i}", [128, 512], F32) for i in range(4)]
        pB = [ps(f"pB{i}", [128, 512], F32) for i in range(2)]
        pN = [ps(f"pN{i}", [1, 512], F32) for i in range(2)]

        p.emit("vector", lambda e: e.memset(gsb[:, :, :], 0.0), writes=[("gsb",)])

        prot = [0]

        def bank():
            b = prot[0] % 4
            prot[0] += 1
            return pA[b], ("pA", b)

        def chunk(ci):
            bi = ci % 2
            cs = slice(ci * 128, (ci + 1) * 128)
            kck, qck, vck = ("kc", bi), ("qc", bi), ("vc", bi)
            kB, kNk = ("pB", bi), ("pN", bi)
            pBb, pNb = pB[bi], pN[bi]
            p.dma("sync", azc[bi][:, :], AZr[:, cs], writes=[("azc", bi)])
            p.dma("sync", bzc[bi][:, :], BZr[:, cs], writes=[("bzc", bi)])
            p.dma("sync", kc[bi][:, :, :], K_[:, :, cs].rearrange("p d c -> d p c"), writes=[kck])
            p.dma("sync", qc[bi][:, :, :], Q[:, :, cs].rearrange("p d c -> d p c"), writes=[qck])
            p.dma("sync", vc[bi][:, :, :], V[:, :, cs].rearrange("p d c -> d p c"), writes=[vck])
            for src, srk, dst, dsk in ((kc[bi], kck, kcn[bi], ("kcn", bi)), (qc[bi], qck, qcn[bi], ("qcn", bi))):
                p.emit("scalar", lambda e, src=src: e.activation(out=sqs[bi][:, :], in_=src[:, :, :], func=AF.Square),
                       reads=[srk], writes=[("sqs", bi)])
                for pi in range(P4):
                    pslc = slice(pi * 128, (pi + 1) * 128)
                    p.emit("tensor", lambda e, pslc=pslc: e.matmul(out=pNb[:, pslc], lhsT=onesc[0:64, :],
                                                                   rhs=sqs[bi][:, pslc], start=True, stop=True),
                           reads=[("sqs", bi), ("onesc",)], writes=[kNk])
                p.emit("scalar", lambda e: e.activation(out=nrm[bi][:, :], in_=pNb[:, :], func=AF.Sqrt),
                       reads=[kNk], writes=[("nrm", bi)])
                p.emit("vector", lambda e: e.tensor_scalar_max(out=nrm[bi][:, :], in0=nrm[bi][:, :], scalar1=1e-12),
                       reads=[("nrm", bi)], writes=[("nrm", bi)])
                p.emit("tensor", lambda e: e.matmul(out=pBb[0:64, :], lhsT=onesr[:, 0:64], rhs=nrm[bi][:, :],
                                                    start=True, stop=True),
                       reads=[("nrm", bi), ("onesr",)], writes=[kB])
                p.emit("vector", lambda e: e.reciprocal(out=rbc[bi][:, :], in_=pBb[0:64, :]),
                       reads=[kB], writes=[("rbc", bi)])
                p.emit("vector", lambda e, src=src, dst=dst: e.tensor_mul(
                    out=dst[:, :, :], in0=src[:, :, :], in1=rbc[bi][:, :].rearrange("d (p c) -> d p c", p=P4)),
                    reads=[srk, ("rbc", bi)], writes=[dsk])
            p.emit("scalar", lambda e: e.activation(out=lna[bi][:, :], in_=azc[bi][:, :], func=AF.Ln),
                   reads=[("azc", bi)], writes=[("lna", bi)])
            p.emit("vector", lambda e: e.tensor_tensor_scan(out=l4[bi][:, :], data0=lna[bi][:, :], data1=zrow[:, :],
                                                            initial=0.0, op0=ALU.add, op1=ALU.add),
                   reads=[("lna", bi), ("zrow",)], writes=[("l4", bi)])
            p.emit("vector", lambda e: e.tensor_scalar(out=ldc[bi][:, :], in0=l4[bi][:, :], scalar1=l4[bi][:, 127:128],
                                                       scalar2=-1.0, op0=ALU.subtract, op1=ALU.mult),
                   reads=[("l4", bi)], writes=[("ldc", bi)])
            p.emit("vector", lambda e: e.tensor_scalar(out=lcb[bi][:, :], in0=l4[bi][:, :], scalar1=0.0,
                                                       scalar2=l4[bi][:, 127:128], op0=ALU.mult, op1=ALU.add),
                   reads=[("l4", bi)], writes=[("lcb", bi)])
            for i, (rows, rk) in enumerate(((bzc[bi], ("bzc", bi)), (l4[bi], ("l4", bi)),
                                            (ldc[bi], ("ldc", bi)), (lcb[bi], ("lcb", bi)))):
                inap = rows[:, :]
                p.emit("tensor", lambda e, i=i, inap=inap: e.transpose(
                    out=pBb[:, i * 4:(i + 1) * 4], in_=inap, identity=idsb[0:P4, 0:P4]),
                    reads=[rk, ("id",)], writes=[kB])
            p.emit("vector", lambda e: e.tensor_copy(out=cols[bi][:, :], in_=pBb[:, 0:16]),
                   reads=[kB], writes=[("cols", bi)])
            p.emit("scalar", lambda e: e.activation(out=colse[bi][:, 4:16], in_=cols[bi][:, 4:16], func=AF.Exp),
                   reads=[("cols", bi)], writes=[("colse", bi)])
            for dst, dk_, lh, lk, rh, rk in ((kk[bi], ("kk", bi), kcn[bi], ("kcn", bi), kcn[bi], ("kcn", bi)),
                                             (kq[bi], ("kq", bi), kcn[bi], ("kcn", bi), qcn[bi], ("qcn", bi))):
                pa, pak = bank()
                for pi in range(P4):
                    p.emit("tensor", lambda e, pi=pi, lh=lh, rh=rh, pa=pa: e.matmul(
                        out=pa[:, pi * 128:(pi + 1) * 128], lhsT=lh[:, pi, :], rhs=rh[:, pi, :],
                        start=True, stop=True),
                        reads=[lk, rk], writes=[pak])
                p.emit("vector", lambda e, dst=dst, pa=pa: e.tensor_copy(out=dst[:, :, :], in_=pa[:, :]),
                       reads=[pak], writes=[dk_])
            pa, pak = bank()
            for pi in range(P4):
                p.emit("tensor", lambda e, pi=pi, pa=pa: e.transpose(
                    out=pa[:, pi * 128:(pi + 1) * 128],
                    in_=cols[bi][:, 4 + pi:5 + pi].broadcast_to([128, 128]), identity=idsb[:, :]),
                    reads=[("cols", bi), ("id",)], writes=[pak])
            p.emit("vector", lambda e, pa=pa: e.tensor_copy(out=lb[bi][:, :, :], in_=pa[:, :]),
                   reads=[pak], writes=[("lb", bi)])
            for pi in range(P4):
                p.emit("vector", lambda e, pi=pi: e.tensor_scalar(
                    out=ldif[bi][:, pi, :], in0=lb[bi][:, pi, :], scalar1=cols[bi][:, 4 + pi:5 + pi],
                    scalar2=None, op0=ALU.subtract),
                    reads=[("lb", bi), ("cols", bi)], writes=[("ldif", bi)])
            p.emit("vector", lambda e: e.tensor_scalar(out=dm[bi][:, :, :], in0=ldif[bi][:, :, :], scalar1=-1.0,
                                                       scalar2=0.0, op0=ALU.mult, op1=ALU.min),
                   reads=[("ldif", bi)], writes=[("dm", bi)])
            p.emit("scalar", lambda e: e.activation(out=dm[bi][:, :, :], in_=dm[bi][:, :, :], func=AF.Exp),
                   reads=[("dm", bi)], writes=[("dm", bi)])
            p.emit("vector", lambda e: e.tensor_scalar_min(out=dmx[bi][:, :, :], in0=ldif[bi][:, :, :], scalar1=0.0),
                   reads=[("ldif", bi)], writes=[("dmx", bi)])
            p.emit("scalar", lambda e: e.activation(out=dmx[bi][:, :, :], in_=dmx[bi][:, :, :], func=AF.Exp),
                   reads=[("dmx", bi)], writes=[("dmx", bi)])
            for pi in range(P4):
                p.emit("vector", lambda e, pi=pi: e.tensor_scalar_mul(
                    out=lw[bi][:, pi, :], in0=kk[bi][:, pi, :], scalar1=cols[bi][:, pi:pi + 1]),
                    reads=[("kk", bi), ("cols", bi)], writes=[("lw", bi)])
            p.emit("vector", lambda e: e.tensor_mul(out=lw[bi][:, :, :], in0=lw[bi][:, :, :],
                                                    in1=mts[:, :].rearrange("a (p c) -> a p c", p=P4)),
                   reads=[("lw", bi), ("mts",)], writes=[("lw", bi)])
            p.emit("vector", lambda e: e.tensor_mul(out=lu[bi][:, :, :], in0=lw[bi][:, :, :], in1=dm[bi][:, :, :]),
                   reads=[("lw", bi), ("dm", bi)], writes=[("lu", bi)])
            p.emit("vector", lambda e: e.tensor_mul(out=m2t[bi][:, :, :], in0=kq[bi][:, :, :],
                                                    in1=mtu[:, :].rearrange("a (p c) -> a p c", p=P4)),
                   reads=[("kq", bi), ("mtu",)], writes=[("m2t", bi)])
            p.emit("vector", lambda e: e.tensor_mul(out=m3t[bi][:, :, :], in0=m2t[bi][:, :, :], in1=dmx[bi][:, :, :]),
                   reads=[("m2t", bi), ("dmx", bi)], writes=[("m3t", bi)])
            pa, pak = bank()
            for pi in range(P4):
                p.emit("tensor", lambda e, pi=pi, pa=pa: e.transpose(
                    out=pa[:, pi * 64:(pi + 1) * 64], in_=kcn[bi][:, pi, :], identity=idsb[0:64, 0:64]),
                    reads=[("kcn", bi), ("id",)], writes=[pak])
            p.emit("vector", lambda e, pa=pa: e.tensor_copy(out=ktm[bi][:, :, :], in_=pa[:, 0:256]),
                   reads=[pak], writes=[("ktm", bi)])
            pa, pak = bank()
            for pi in range(P4):
                p.emit("tensor", lambda e, pi=pi, pa=pa: e.transpose(
                    out=pa[:, pi * 64:(pi + 1) * 64], in_=vc[bi][:, pi, :], identity=idsb[0:64, 0:64]),
                    reads=[vck, ("id",)], writes=[pak])
            p.emit("vector", lambda e, pa=pa: e.tensor_copy(out=vtm[bi][:, :, :], in_=pa[:, 0:256]),
                   reads=[pak], writes=[("vtm", bi)])
            for pi in range(P4):
                p.emit("vector", lambda e, pi=pi: e.tensor_scalar_mul(
                    out=Xs[bi][:, pi, :], in0=ktm[bi][:, pi, :], scalar1=cols[bi][:, pi:pi + 1]),
                    reads=[("ktm", bi), ("cols", bi)], writes=[("Xs", bi)])
                p.emit("vector", lambda e, pi=pi: e.tensor_scalar_mul(
                    out=Xs[bi][:, P4 + pi, :], in0=vtm[bi][:, pi, :], scalar1=cols[bi][:, pi:pi + 1]),
                    reads=[("vtm", bi), ("cols", bi)], writes=[("Xs", bi)])
                p.emit("vector", lambda e, pi=pi: e.tensor_scalar_mul(
                    out=kbar[bi][:, pi, :], in0=ktm[bi][:, pi, :], scalar1=colse[bi][:, 8 + pi:9 + pi]),
                    reads=[("ktm", bi), ("colse", bi)], writes=[("kbar", bi)])
            pa, pak = bank()
            for pi in range(P4):
                for si, lsrc, lk in ((0, lw[bi], ("lw", bi)), (1, lu[bi], ("lu", bi))):
                    p.emit("tensor", lambda e, pi=pi, si=si, lsrc=lsrc, pa=pa: e.transpose(
                        out=pa[:, 0:128], in_=lsrc[:, pi, :], identity=idsb[:, :]),
                        reads=[lk, ("id",)], writes=[pak])
                    p.emit("vector", lambda e, pi=pi, si=si, pa=pa: e.tensor_copy(
                        out=Tt[bi][:, si * P4 + pi, :], in_=pa[:, 0:128]),
                        reads=[pak], writes=[("Tt", bi)])
            p.emit("vector", lambda e: e.tensor_copy(out=Pp[bi][:, 0:P4, :], in_=lw[bi][:, :, :]),
                   reads=[("lw", bi)], writes=[("Pp", bi)])
            p.emit("vector", lambda e: e.tensor_copy(out=Pp[bi][:, P4:2 * P4, :], in_=lu[bi][:, :, :]),
                   reads=[("lu", bi)], writes=[("Pp", bi)])
            do_powers(ci, bi)

        def do_powers(ci, bi):
            seq = [((Pp[bi], ("Pp", bi)), (Tt[bi], ("Tt", bi)), (P2[bi], ("P2",)), (T2[bi], ("T2",)), True),
                   ((P2[bi], ("P2",)), (T2[bi], ("T2",)), (P4b[bi], ("P4b",)), (T4[bi], ("T4",)), True),
                   ((P4b[bi], ("P4b",)), (T4[bi], ("T4",)), (None, None), (T8[bi], ("T8",)), False)]
            for (Pn, Pk), (Tn, Tk), (Pd, Pdk), (Td, Tdk), needP in seq:
                if needP:
                    for half in range(2):
                        pa, pak = bank()
                        for i in range(P4):
                            m = half * P4 + i
                            p.emit("tensor", lambda e, m=m, Tn=Tn, Pn=Pn, i=i, pa=pa: e.matmul(
                                out=pa[:, i * 128:(i + 1) * 128], lhsT=Tn[:, m, :], rhs=Pn[:, m, :],
                                start=True, stop=True),
                                reads=[Tk, Pk], writes=[pak])
                        p.emit("vector", lambda e, half=half, pa=pa, Pd=Pd: e.tensor_copy(
                            out=Pd[:, half * P4:(half + 1) * P4, :], in_=pa[:, :]),
                            reads=[pak], writes=[Pdk])
                for half in range(2):
                    pa, pak = bank()
                    for i in range(P4):
                        m = half * P4 + i
                        p.emit("tensor", lambda e, m=m, Tn=Tn, Pn=Pn, i=i, pa=pa: e.matmul(
                            out=pa[:, i * 128:(i + 1) * 128], lhsT=Pn[:, m, :], rhs=Tn[:, m, :],
                            start=True, stop=True),
                            reads=[Tk, Pk], writes=[pak])
                    p.emit("vector", lambda e, half=half, pa=pa, Td=Td: e.tensor_copy(
                        out=Td[:, half * P4:(half + 1) * P4, :], in_=pa[:, :]),
                        reads=[pak], writes=[Tdk])
            for Tn, Tk, sign in ((T8[bi], ("T8",), 1.0), (T4[bi], ("T4",), 1.0),
                                 (T2[bi], ("T2",), 1.0), (Tt[bi], ("Tt", bi), -1.0)):
                pa, pak = bank()
                for m in range(2 * P4):
                    p.emit("tensor", lambda e, m=m, Tn=Tn, pa=pa: e.matmul(
                        out=pa[:, m * 64:(m + 1) * 64], lhsT=Tn[:, m, :], rhs=Xs[bi][:, m, :],
                        start=True, stop=True),
                        reads=[Tk, ("Xs", bi)], writes=[pak])
                pav = pa[:, :].rearrange("a (m c) -> a m c", m=2 * P4)
                if sign > 0:
                    p.emit("vector", lambda e, pav=pav: e.tensor_add(out=Xs[bi][:, :, :], in0=Xs[bi][:, :, :], in1=pav),
                           reads=[pak, ("Xs", bi)], writes=[("Xs", bi)])
                else:
                    p.emit("vector", lambda e, pav=pav: e.tensor_sub(out=Xs[bi][:, :, :], in0=Xs[bi][:, :, :], in1=pav),
                           reads=[pak, ("Xs", bi)], writes=[("Xs", bi)])
            finish_chunk(ci, bi)

        def finish_chunk(ci, bi):
            pa, pak = bank()
            for pi in range(P4):
                p.emit("tensor", lambda e, pi=pi, pa=pa: e.transpose(
                    out=pa[:, pi * 128:(pi + 1) * 128][0:64, :], in_=Xs[bi][:, pi, :], identity=idsb[:, :]),
                    reads=[("Xs", bi), ("id",)], writes=[pak])
            p.emit("vector", lambda e, pa=pa: e.tensor_copy(
                out=wts[bi][:, :, :], in_=pa[0:64, :].rearrange("d (p c) -> d p c", p=P4)),
                reads=[pak], writes=[("wts", bi)])
            pa, pak = bank()
            for pi in range(P4):
                p.emit("tensor", lambda e, pi=pi, pa=pa: e.matmul(
                    out=pa[:, pi * 64:(pi + 1) * 64], lhsT=wts[bi][:, pi, :], rhs=gsb[:, pi, :],
                    start=True, stop=True),
                    reads=[("wts", bi), ("gsb",)], writes=[pak])
            p.emit("vector", lambda e, pa=pa: e.tensor_scalar_mul(
                out=wg[bi][:, :, :], in0=pa[:, 0:256].rearrange("a (p c) -> a p c", p=P4), scalar1=-1.0),
                reads=[pak], writes=[("wg", bi)])
            pa, pak = bank()
            for pi in range(P4):
                p.emit("tensor", lambda e, pi=pi, pa=pa: e.matmul(
                    out=pa[:, pi * 64:(pi + 1) * 64], lhsT=qcn[bi][:, pi, :], rhs=gsb[:, pi, :],
                    start=True, stop=False),
                    reads=[("qcn", bi), ("gsb",)], writes=[pak])
                p.emit("tensor", lambda e, pi=pi, pa=pa: e.matmul(
                    out=pa[:, pi * 64:(pi + 1) * 64], lhsT=m2t[bi][:, pi, :], rhs=wg[bi][:, pi, :],
                    start=False, stop=True),
                    reads=[("m2t", bi), ("wg", bi)], writes=[pak])
            for pi in range(P4):
                p.emit("vector", lambda e, pi=pi, pa=pa: e.tensor_scalar_mul(
                    out=o1[bi][:, pi, :], in0=pa[:, pi * 64:(pi + 1) * 64], scalar1=colse[bi][:, 4 + pi:5 + pi]),
                    reads=[pak, ("colse", bi)], writes=[("o1", bi)])
            pa2, pak2 = bank()
            for pi in range(P4):
                p.emit("tensor", lambda e, pi=pi, pa2=pa2: e.matmul(
                    out=pa2[:, pi * 64:(pi + 1) * 64], lhsT=m3t[bi][:, pi, :], rhs=Xs[bi][:, P4 + pi, :],
                    start=True, stop=True),
                    reads=[("m3t", bi), ("Xs", bi)], writes=[pak2])
            p.emit("vector", lambda e, pa2=pa2: e.tensor_add(
                out=och[bi][:, :, :], in0=o1[bi][:, :, :], in1=pa2[:, 0:256].rearrange("a (p c) -> a p c", p=P4)),
                reads=[pak2, ("o1", bi)], writes=[("och", bi)])
            p.dma("gpsimd", O[:, ci, :, :].rearrange("p a d -> a p d"), och[bi][:, :, :], reads=[("och", bi)])
            pa, pak = bank()
            for pi in range(P4):
                p.emit("tensor", lambda e, pi=pi, pa=pa: e.matmul(
                    out=pa[0:64, pi * 64:(pi + 1) * 64], lhsT=ktm[bi][:, pi, :], rhs=wg[bi][:, pi, :],
                    start=True, stop=True),
                    reads=[("ktm", bi), ("wg", bi)], writes=[pak])
            p.emit("vector", lambda e, pa=pa: e.tensor_add(
                out=g1t[bi][:, :, :], in0=gsb[:, :, :], in1=pa[0:64, 0:256].rearrange("d (p c) -> d p c", p=P4)),
                reads=[pak, ("gsb",)], writes=[("g1t", bi)])
            pa2, pak2 = bank()
            for pi in range(P4):
                p.emit("tensor", lambda e, pi=pi, pa2=pa2: e.matmul(
                    out=pa2[0:64, pi * 64:(pi + 1) * 64], lhsT=kbar[bi][:, pi, :], rhs=Xs[bi][:, P4 + pi, :],
                    start=True, stop=True),
                    reads=[("kbar", bi), ("Xs", bi)], writes=[pak2])
            for pi in range(P4):
                p.emit("vector", lambda e, pi=pi, pa2=pa2: e.scalar_tensor_tensor(
                    out=gsb[:, pi, :], in0=g1t[bi][:, pi, :], scalar=colse[bi][0:64, 12 + pi:13 + pi],
                    in1=pa2[0:64, pi * 64:(pi + 1) * 64], op0=ALU.mult, op1=ALU.add),
                    reads=[("g1t", bi), ("colse", bi), pak2], writes=[("gsb",)])

        for ci in range(NCH):
            chunk(ci)
    return prog


def mk_lm1(T=TPC, dbg=False):
    def prog(nc, p, es):
        X = nc.dram_tensor("x_fm", [D, T], F32, kind="ExternalInput").ap()
        WQ = nc.dram_tensor("wq", [D, D], F32, kind="ExternalInput").ap()
        WKVA = nc.dram_tensor("wkva", [D, 160], F32, kind="ExternalInput").ap()
        WKVB = nc.dram_tensor("wkvb", [128, 512], F32, kind="ExternalInput").ap()
        WLN1 = nc.dram_tensor("wln1", [128, 8], F32, kind="ExternalInput").ap()
        COS = nc.dram_tensor("cosT", [128, T], F32, kind="ExternalInput").ap()
        SIN = nc.dram_tensor("sinT", [128, T], F32, kind="ExternalInput").ap()
        YO = nc.dram_tensor("y_out", [D, T], F32, kind="ExternalOutput").ap()
        QO = nc.dram_tensor("q_out", [D, T], BF16, kind="ExternalOutput").ap()
        KVO = nc.dram_tensor("kv_out", [512, T], BF16, kind="ExternalOutput").ap()
        KRO = nc.dram_tensor("krope_out", [32, T], BF16, kind="ExternalOutput").ap()

        tl = TL(nc, p, es, T=T)
        sb = tl.sb
        wq = tl.load_weight_bf("wq", WQ, D, D)
        wkva = tl.load_weight_bf("wkva", WKVA, D, 160)
        wkvbs = sb("wkvb_stg", [128, 512], F32)
        p.dma("sync", wkvbs[:, :], WKVB, writes=[("wkvb_stg",)])
        wkvb = sb("wkvb_bf", [128, 512], BF16)
        p.emit("vector", lambda e: e.tensor_copy(out=wkvb[:, :], in_=wkvbs[:, :]),
               reads=[("wkvb_stg",)], writes=[("wkvb",)])
        wln1 = tl.load_rows_f32("wln1c", WLN1, D)
        cosb = sb("cosb", [128, T], F32)
        p.dma("sync", cosb[:, :], COS, writes=[("cos",)])
        sinb = sb("sinb", [128, T], F32)
        p.dma("sync", sinb[:, :], SIN, writes=[("sin",)])

        xsb = tl.load_fm("x", X, D, T, F32)
        ybf = None
        y32t = sb("y32t", [128, 8, 512], F32)
        ttile = sb("ttile", [128, 8, 512], BF16)
        qsb = sb("qsb", [128, 8, 512], F32)
        qstg = sb("qstg", [128, 8, 512], BF16)
        lat = sb("lat", [128, 512], BF16)
        kvt = sb("kvt", [128, 4, 512], BF16)
        krl = sb("krl", [16, 512], BF16)
        krh = sb("krh", [16, 512], BF16)
        rs0 = sb("rs0", [16, 512], F32)
        rs1 = sb("rs1", [16, 512], F32)
        pRL = tl.pmm[0][0:16, :]
        pRH = tl.pmm[1][0:16, :]
        Yre = YO.rearrange("(c p) t -> p c t", p=128)
        Qre = QO.rearrange("(c p) t -> p c t", p=128)
        KVre = KVO.rearrange("(c p) t -> p c t", p=128)

        def tile_cb(j):
            js = slice(j * 512, (j + 1) * 512)
            p.dma("gpsimd", Yre[:, :, js], y32t[:, :, :], reads=[("y32", c) for c in range(8)])
            for c in range(8):
                sq = tl.sq0 if c % 2 == 0 else tl.sq1
                sqk = ("sq", c % 2)
                p.emit("scalar", lambda e, c=c, sq=sq: e.activation(out=sq[:, :], in_=y32t[:, c, :], func=AF.Square),
                       reads=[("y32", c)], writes=[sqk])
                p.emit("tensor", lambda e, c=c, sq=sq: e.matmul(out=tl.prq[:, :], lhsT=tl.ones_col[:], rhs=sq[:, :],
                                                                start=(c == 0), stop=(c == 7)),
                       reads=[sqk, ("ones_col", 0)], writes=[("prq",)])
            epsap = tl.const(1e-5)
            p.emit("vector", lambda e: e.tensor_scalar_mul(out=tl.r1[:, :], in0=tl.prq[:, :], scalar1=1.0 / D),
                   reads=[("prq",)], writes=[("r1",)])
            p.emit("scalar", lambda e, epsap=epsap: e.activation(out=tl.r1[:, :], in_=tl.r1[:, :], func=AF.Sqrt, bias=epsap[0:1, :]),
                   reads=[("r1",), ("const", 1e-5)], writes=[("r1",)])
            p.emit("vector", lambda e: e.reciprocal(out=tl.r1[:, :], in_=tl.r1[:, :]),
                   reads=[("r1",)], writes=[("r1",)])
            p.emit("tensor", lambda e: e.matmul(out=tl.pbS[:, :], lhsT=tl.ones_row[:], rhs=tl.r1[:, :],
                                                start=True, stop=True),
                   reads=[("r1",), ("ones_row", 0)], writes=[("pbS",)])
            for c in range(8):
                p.emit("vector", lambda e, c=c: e.scalar_tensor_tensor(
                    out=ttile[:, c, :], in0=y32t[:, c, :], scalar=wln1[:, c:c + 1],
                    in1=tl.pbS[:, :], op0=ALU.mult, op1=ALU.mult),
                    reads=[("y32", c), ("wln1c", 0), ("pbS",)], writes=[("tt", c)])
            for fo in range(8):
                pm = tl.pmm[fo % 2]
                pmk = ("pmm", fo % 2)
                for c in range(8):
                    p.emit("tensor", lambda e, c=c, fo=fo, pm=pm: e.matmul(
                        out=pm[:, :], lhsT=wq[:, c, fo * 128:(fo + 1) * 128], rhs=ttile[:, c, :],
                        start=(c == 0), stop=(c == 7)),
                        reads=[("wq", c), ("tt", c)], writes=[pmk])
                p.emit("vector", lambda e, fo=fo, pm=pm: e.tensor_copy(out=qsb[:, fo, :], in_=pm[:, :]),
                       reads=[pmk], writes=[("qsb", fo)])
            for ci_ in (0, 1):
                lo, hi = 4 + ci_, 6 + ci_
                p.emit("vector", lambda e, lo=lo, js=js: e.tensor_mul(out=tl.sq0[:, :], in0=qsb[:, lo, :], in1=cosb[:, js]),
                       reads=[("qsb", lo), ("cos",)], writes=[("sq", 0)])
                p.emit("vector", lambda e, hi=hi, js=js: e.tensor_mul(out=tl.sq1[:, :], in0=qsb[:, hi, :], in1=sinb[:, js]),
                       reads=[("qsb", hi), ("sin",)], writes=[("sq", 1)])
                p.emit("vector", lambda e, lo=lo: e.tensor_sub(out=qstg[:, lo, :], in0=tl.sq0[:, :], in1=tl.sq1[:, :]),
                       reads=[("sq", 0), ("sq", 1)], writes=[("qst", lo)])
                p.emit("vector", lambda e, lo=lo, js=js: e.tensor_mul(out=tl.sq0[:, :], in0=qsb[:, lo, :], in1=sinb[:, js]),
                       reads=[("qsb", lo), ("sin",)], writes=[("sq", 0)])
                p.emit("vector", lambda e, hi=hi, js=js: e.tensor_mul(out=tl.sq1[:, :], in0=qsb[:, hi, :], in1=cosb[:, js]),
                       reads=[("qsb", hi), ("cos",)], writes=[("sq", 1)])
                p.emit("vector", lambda e, hi=hi: e.tensor_add(out=qstg[:, hi, :], in0=tl.sq0[:, :], in1=tl.sq1[:, :]),
                       reads=[("sq", 0), ("sq", 1)], writes=[("qst", hi)])
            for fo in range(4):
                p.emit("vector", lambda e, fo=fo: e.tensor_copy(out=qstg[:, fo, :], in_=qsb[:, fo, :]),
                       reads=[("qsb", fo)], writes=[("qst", fo)])
            p.dma("gpsimd", Qre[:, :, js], qstg[:, :, :], reads=[("qst", fo) for fo in range(8)])
            pm = tl.pmm[2]
            pmk = ("pmm", 2)
            for c in range(8):
                p.emit("tensor", lambda e, c=c, pm=pm: e.matmul(
                    out=pm[:, :], lhsT=wkva[:, c, 0:128], rhs=ttile[:, c, :],
                    start=(c == 0), stop=(c == 7)),
                    reads=[("wkva", c), ("tt", c)], writes=[pmk])
            p.emit("vector", lambda e, pm=pm: e.tensor_copy(out=lat[:, :], in_=pm[:, :]),
                   reads=[pmk], writes=[("lat",)])
            for fo in range(4):
                pm2 = tl.pmm[fo % 2]
                pmk2 = ("pmm", fo % 2)
                p.emit("tensor", lambda e, fo=fo, pm2=pm2: e.matmul(
                    out=pm2[:, :], lhsT=wkvb[:, fo * 128:(fo + 1) * 128], rhs=lat[:, :],
                    start=True, stop=True),
                    reads=[("wkvb",), ("lat",)], writes=[pmk2])
                p.emit("vector", lambda e, fo=fo, pm2=pm2: e.tensor_copy(out=kvt[:, fo, :], in_=pm2[:, :]),
                       reads=[pmk2], writes=[("kvt", fo)])
            p.dma("gpsimd", KVre[:, :, js], kvt[:, :, :], reads=[("kvt", fo) for fo in range(4)])
            for c in range(8):
                p.emit("tensor", lambda e, c=c: e.matmul(
                    out=pRL, lhsT=wkva[:, c, 128:144], rhs=ttile[:, c, :],
                    start=(c == 0), stop=(c == 7)),
                    reads=[("wkva", c), ("tt", c)], writes=[("pmm", 0)])
            for c in range(8):
                p.emit("tensor", lambda e, c=c: e.matmul(
                    out=pRH, lhsT=wkva[:, c, 144:160], rhs=ttile[:, c, :],
                    start=(c == 0), stop=(c == 7)),
                    reads=[("wkva", c), ("tt", c)], writes=[("pmm", 1)])
            p.emit("vector", lambda e, js=js: e.tensor_mul(out=rs0[:, :], in0=pRL, in1=cosb[0:16, js]),
                   reads=[("pmm", 0), ("cos",)], writes=[("rs0",)])
            p.emit("vector", lambda e, js=js: e.tensor_mul(out=rs1[:, :], in0=pRH, in1=sinb[0:16, js]),
                   reads=[("pmm", 1), ("sin",)], writes=[("rs1",)])
            p.emit("vector", lambda e: e.tensor_sub(out=krl[:, :], in0=rs0[:, :], in1=rs1[:, :]),
                   reads=[("rs0",), ("rs1",)], writes=[("krl",)])
            p.emit("vector", lambda e, js=js: e.tensor_mul(out=rs0[:, :], in0=pRL, in1=sinb[0:16, js]),
                   reads=[("pmm", 0), ("sin",)], writes=[("rs0",)])
            p.emit("vector", lambda e, js=js: e.tensor_mul(out=rs1[:, :], in0=pRH, in1=cosb[0:16, js]),
                   reads=[("pmm", 1), ("cos",)], writes=[("rs1",)])
            p.emit("vector", lambda e: e.tensor_add(out=krh[:, :], in0=rs0[:, :], in1=rs1[:, :]),
                   reads=[("rs0",), ("rs1",)], writes=[("krh",)])
            p.dma("gpsimd", KRO[0:16, js], krl[:, :], reads=[("krl",)])
            p.dma("gpsimd", KRO[16:32, js], krh[:, :], reads=[("krh",)])

        tl.emit_ln(xsb, "x", 8, ybf, "ybf", y32_tile=y32t, y32name="y32", tile_cb=tile_cb)
    return prog


def mk_lm3(T=TPC, dbg=False):
    E, HH = 8, 512

    def prog(nc, p, es):
        X = nc.dram_tensor("x_fm", [D, T], F32, kind="ExternalInput").ap()
        Y = nc.dram_tensor("y_fm", [D, T], F32, kind="ExternalInput").ap()
        A = nc.dram_tensor("a_fm", [512, T], BF16, kind="ExternalInput").ap()
        WO = nc.dram_tensor("wo", [512, D], F32, kind="ExternalInput").ap()
        WLN2 = nc.dram_tensor("wln2", [128, 8], F32, kind="ExternalInput").ap()
        WGATE = nc.dram_tensor("wgate", [D, E], F32, kind="ExternalInput").ap()
        EB = nc.dram_tensor("ebias", [8, 1], F32, kind="ExternalInput").ap()
        W1 = nc.dram_tensor("w1", [E, D, 2 * HH], F32, kind="ExternalInput").ap()
        W2 = nc.dram_tensor("w2", [E, HH, D], F32, kind="ExternalInput").ap()
        WS1 = nc.dram_tensor("ws1", [D, 2 * HH], F32, kind="ExternalInput").ap()
        WS2 = nc.dram_tensor("ws2", [HH, D], F32, kind="ExternalInput").ap()
        IDF = nc.dram_tensor("ident_f32", [128, 128], F32, kind="ExternalInput").ap()
        O = nc.dram_tensor("xout_fm", [D, T], F32, kind="ExternalOutput").ap()
        UD = nc.dram_tensor("u_scratch", [D, T], F32, kind="Internal").ap()
        if dbg:
            OCW = nc.dram_tensor("cw_dbg", [128, T // 128, E], F32, kind="ExternalOutput").ap()

        tl = TL(nc, p, es, T=T)
        sb, ps = tl.sb, tl.ps
        njt = tl.njt
        nsub = T // 128
        idsb = sb("idsb", [128, 128], F32)
        p.dma("sync", idsb[:, :], IDF, writes=[("id",)])
        wgt = sb("wgt", [128, 8, E], F32)
        p.dma("sync", wgt[:, :, :], WGATE.rearrange("(c p) e -> p c e", p=128), writes=[("wgt",)])
        eb = sb("eb", [8, 1], F32)
        p.dma("sync", eb[:, :], EB, writes=[("eb",)])
        wln2 = tl.load_rows_f32("wln2c", WLN2, D)
        Ure = UD.rearrange("(c p) t -> c p t", p=128)
        Xre = X.rearrange("(c p) t -> c p t", p=128)
        Yre = Y.rearrange("(c p) t -> c p t", p=128)
        Ore = O.rearrange("(c p) t -> c p t", p=128)

        with contextlib.ExitStack() as es2:
            sb2 = lambda n, s, d: es2.enter_context(nc.sbuf_tensor(n, s, d))
            asb = sb2("a", [128, 4, T], BF16)
            p.dma("sync", asb[:, :, :], A.rearrange("(c p) t -> p c t", p=128),
                  writes=[("a", c) for c in range(4)])
            wosb = sb2("wo_bf", [128, 4, D], BF16)
            wstg = [sb2(f"wo_stg{i}", [128, D], F32) for i in range(2)]
            srcw = WO.rearrange("(c p) f -> c p f", p=128)
            for c in range(4):
                s = wstg[c % 2]
                sk = ("wo_stg", c % 2)
                p.dma("sync", s[:, :], srcw[c], writes=[sk])
                p.emit("vector", lambda e, s=s, c=c: e.tensor_copy(out=wosb[:, c, :], in_=s[:, :]),
                       reads=[sk], writes=[("wo", c)])
            yst = [sb2(f"yst{i}", [128, T], F32) for i in range(2)]
            ust = [sb2(f"ust{i}", [128, T], F32) for i in range(2)]
            for fo in range(8):
                bb = fo % 2
                yk = ("yst", bb)
                uk = ("ust", bb)
                p.dma("sync", yst[bb][:, :], Yre[fo], writes=[yk])
                for j in range(njt):
                    js = slice(j * 512, (j + 1) * 512)
                    pm = tl.pmm[j % 4]
                    pmk = ("pmm", j % 4)
                    for c in range(4):
                        p.emit("tensor", lambda e, c=c, fo=fo, js=js, pm=pm: e.matmul(
                            out=pm[:, :], lhsT=wosb[:, c, fo * 128:(fo + 1) * 128], rhs=asb[:, c, js],
                            start=(c == 0), stop=(c == 3)),
                            reads=[("wo", c), ("a", c)], writes=[pmk])
                    p.emit("vector", lambda e, js=js, bb=bb, pm=pm: e.tensor_add(
                        out=ust[bb][:, js], in0=yst[bb][:, js], in1=pm[:, :]),
                        reads=[pmk, yk], writes=[uk])
                p.dma("gpsimd", Ure[fo], ust[bb][:, :], reads=[uk])
        p.barrier()

        xnbf = tl.sbuf_fm("xn", D, T, BF16)
        zbf = tl.sbuf_fm("z", D, T, BF16)
        cw_all = sb("cw_all", [128, nsub, E], F32)
        lg = sb("lg", [16, 512], F32)
        gs = [sb(f"gs{i}", [128, E], F32) for i in range(5)]
        gr = [sb(f"gr{i}", [128, 1], F32) for i in range(4)]
        ut = sb("ut", [128, 8, 512], F32)

        def gate_tile(j):
            plg = tl.pmm[0]
            for c in range(8):
                p.emit("tensor", lambda e, c=c: e.matmul(
                    out=plg[0:E, :], lhsT=wgt[:, c, :], rhs=ut[:, c, :],
                    start=(c == 0), stop=(c == 7)),
                    reads=[("wgt",), ("ut", c)], writes=[("pmm", 0)])
            p.emit("vector", lambda e: e.tensor_scalar_add(out=lg[0:E, :], in0=plg[0:E, :], scalar1=eb[:, 0:1]),
                   reads=[("pmm", 0), ("eb",)], writes=[("lg",)])
            for i in range(4):
                ti = j * 4 + i
                ltm, ptm, g1, l2, cwtm = gs
                m1, nm1, ssum, m2 = gr
                ptf = tl.pmm[1]
                p.emit("tensor", lambda e, i=i: e.transpose(
                    out=ptf[0:128, 0:E], in_=lg[0:E, i * 128:(i + 1) * 128], identity=idsb[0:E, 0:E]),
                    reads=[("lg",), ("id",)], writes=[("pmm", 1)])
                p.emit("vector", lambda e: e.tensor_copy(out=ltm[:, 0:E], in_=ptf[0:128, 0:E]),
                       reads=[("pmm", 1)], writes=[("ltm",)])
                p.emit("vector", lambda e: e.reduce_max(out=m1[:, :], in_=ltm[:, 0:E], axis=AX.X),
                       reads=[("ltm",)], writes=[("m1",)])
                p.emit("vector", lambda e: e.tensor_scalar_mul(out=nm1[:, :], in0=m1[:, :], scalar1=-1.0),
                       reads=[("m1",)], writes=[("nm1",)])
                p.emit("scalar", lambda e: e.activation(out=ptm[:, 0:E], in_=ltm[:, 0:E], func=AF.Exp,
                                                        bias=nm1[:, 0:1], accum_out=ssum[:, 0:1]),
                       reads=[("ltm",), ("nm1",)], writes=[("ptm",), ("ssum",)])
                p.emit("vector", lambda e: e.reciprocal(out=ssum[:, :], in_=ssum[:, :]),
                       reads=[("ssum",)], writes=[("ssum",)])
                p.emit("vector", lambda e: e.tensor_scalar_mul(out=ptm[:, 0:E], in0=ptm[:, 0:E], scalar1=ssum[:, 0:1]),
                       reads=[("ptm",), ("ssum",)], writes=[("ptm",)])
                p.emit("vector", lambda e: e.tensor_scalar(out=g1[:, 0:E], in0=ltm[:, 0:E], scalar1=m1[:, 0:1],
                                                           scalar2=None, op0=ALU.is_ge),
                       reads=[("ltm",), ("m1",)], writes=[("g1",)])
                p.emit("vector", lambda e: e.scalar_tensor_tensor(out=l2[:, 0:E], in0=g1[:, 0:E], scalar=-1e30,
                                                                  in1=ltm[:, 0:E], op0=ALU.mult, op1=ALU.add),
                       reads=[("g1",), ("ltm",)], writes=[("l2",)])
                p.emit("vector", lambda e: e.reduce_max(out=m2[:, :], in_=l2[:, 0:E], axis=AX.X),
                       reads=[("l2",)], writes=[("m2",)])
                p.emit("vector", lambda e: e.tensor_scalar(out=g1[:, 0:E], in0=ltm[:, 0:E], scalar1=m2[:, 0:1],
                                                           scalar2=None, op0=ALU.is_ge),
                       reads=[("ltm",), ("m2",)], writes=[("g1",)])
                p.emit("vector", lambda e: e.tensor_mul(out=cwtm[:, 0:E], in0=ptm[:, 0:E], in1=g1[:, 0:E]),
                       reads=[("ptm",), ("g1",)], writes=[("cwtm",)])
                p.emit("vector", lambda e: e.reduce_sum(out=m2[:, :], in_=cwtm[:, 0:E], axis=AX.X),
                       reads=[("cwtm",)], writes=[("m2",)])
                p.emit("vector", lambda e: e.reciprocal(out=m2[:, :], in_=m2[:, :]),
                       reads=[("m2",)], writes=[("m2",)])
                p.emit("vector", lambda e, ti=ti: e.tensor_scalar_mul(out=cw_all[:, ti, :], in0=cwtm[:, 0:E], scalar1=m2[:, 0:1]),
                       reads=[("cwtm",), ("m2",)], writes=[("cw", ti)])

        epsap = tl.const(1e-5)
        ustr = [sb(f"ustr{i}", [128, 512], F32) for i in range(2)]
        for j in range(njt):
            js = slice(j * 512, (j + 1) * 512)
            for c in range(8):
                p.dma("sync", ut[:, c, :], Ure[c][:, js], writes=[("ut", c)])
            for c in range(8):
                sq = tl.sq0 if c % 2 == 0 else tl.sq1
                sqk = ("sq", c % 2)
                p.emit("scalar", lambda e, c=c, sq=sq: e.activation(out=sq[:, :], in_=ut[:, c, :], func=AF.Square),
                       reads=[("ut", c)], writes=[sqk])
                p.emit("tensor", lambda e, c=c, sq=sq: e.matmul(out=tl.prq[:, :], lhsT=tl.ones_col[:], rhs=sq[:, :],
                                                                start=(c == 0), stop=(c == 7)),
                       reads=[sqk, ("ones_col", 0)], writes=[("prq",)])
            p.emit("vector", lambda e: e.tensor_scalar_mul(out=tl.r1[:, :], in0=tl.prq[:, :], scalar1=1.0 / D),
                   reads=[("prq",)], writes=[("r1",)])
            p.emit("scalar", lambda e: e.activation(out=tl.r1[:, :], in_=tl.r1[:, :], func=AF.Sqrt, bias=epsap[0:1, :]),
                   reads=[("r1",), ("const", 1e-5)], writes=[("r1",)])
            p.emit("vector", lambda e: e.reciprocal(out=tl.r1[:, :], in_=tl.r1[:, :]),
                   reads=[("r1",)], writes=[("r1",)])
            p.emit("tensor", lambda e: e.matmul(out=tl.pbS[:, :], lhsT=tl.ones_row[:], rhs=tl.r1[:, :],
                                                start=True, stop=True),
                   reads=[("r1",), ("ones_row", 0)], writes=[("pbS",)])
            for c in range(8):
                p.emit("vector", lambda e, c=c, js=js: e.scalar_tensor_tensor(
                    out=xnbf[:, c, js], in0=ut[:, c, :], scalar=wln2[:, c:c + 1],
                    in1=tl.pbS[:, :], op0=ALU.mult, op1=ALU.mult),
                    reads=[("ut", c), ("wln2c", 0), ("pbS",)], writes=[("xn", c)])
                p.emit("vector", lambda e, c=c: e.scalar_tensor_tensor(
                    out=ut[:, c, :], in0=ut[:, c, :], scalar=wln2[:, c:c + 1],
                    in1=tl.pbS[:, :], op0=ALU.mult, op1=ALU.mult),
                    reads=[("ut", c), ("wln2c", 0), ("pbS",)], writes=[("ut", c)])
            gate_tile(j)
        if dbg:
            p.dma("gpsimd", OCW, cw_all[:, :, :], reads=[("cw", ti) for ti in range(nsub)])
        p.barrier()

        h = [tl.sbuf_fm(f"h{i}", HH, T, BF16) for i in range(2)]
        stg = [sb(f"stgw{i}", [128, 1024], F32) for i in range(2)]
        w1b = [sb(f"w1b{i}", [128, 8, 2 * HH], BF16) for i in range(2)]
        w2b = [sb(f"w2b{i}", [128, 4, D], BF16) for i in range(2)]
        cwb = [sb(f"cwb{i}", [128, 512], BF16) for i in range(2)]
        sg = [sb(f"sg{i}", [128, 512], F32) for i in range(2)]
        hwt = [sb(f"hwt{i}", [128, 512], BF16) for i in range(2)]
        nst = [0]

        def load_cast(dram_src, dst, dstkey, nchunk, ncols):
            for c in range(nchunk):
                s = stg[nst[0] % 2]
                skey = ("stgw", nst[0] % 2)
                nst[0] += 1
                p.dma("sync", s[:, 0:ncols], dram_src[c], writes=[skey])
                p.emit("vector", lambda e, s=s, c=c, ncols=ncols, dst=dst: e.tensor_copy(
                    out=dst[:, c, 0:ncols], in_=s[:, 0:ncols]),
                    reads=[skey], writes=[(dstkey, c)])

        def up_and_down(eidx, w1cur, w1key, w2cur, w2key, hcur, hkey, weighted, first):
            for j in range(njt):
                js = slice(j * 512, (j + 1) * 512)
                if weighted:
                    pcw = tl.pbS
                    for i in range(4):
                        ti = j * 4 + i
                        p.emit("tensor", lambda e, ti=ti, i=i, eidx=eidx: e.transpose(
                            out=pcw[:, i * 128:(i + 1) * 128],
                            in_=cw_all[:, ti, eidx:eidx + 1].broadcast_to([128, 128]),
                            identity=idsb[:, :]),
                            reads=[("cw", ti), ("id",)], writes=[("pbS",)])
                    cb = cwb[j % 2]
                    cbkey = ("cwb", j % 2)
                    p.emit("vector", lambda e, cb=cb: e.tensor_copy(out=cb[:, :], in_=pcw[:, :]),
                           reads=[("pbS",)], writes=[cbkey])
                for blk in range(4):
                    pu, pukey = tl.pmm[blk % 2], ("pmm", blk % 2)
                    pg, pgkey = tl.pmm[2 + blk % 2], ("pmm", 2 + blk % 2)
                    for c in range(8):
                        p.emit("tensor", lambda e, c=c, blk=blk, js=js, w1cur=w1cur, pu=pu: e.matmul(
                            out=pu[:, :], lhsT=w1cur[:, c, blk * 128:(blk + 1) * 128],
                            rhs=xnbf[:, c, js], start=(c == 0), stop=(c == 7)),
                            reads=[(w1key, c), ("xn", c)], writes=[pukey])
                    for c in range(8):
                        p.emit("tensor", lambda e, c=c, blk=blk, js=js, w1cur=w1cur, pg=pg: e.matmul(
                            out=pg[:, :], lhsT=w1cur[:, c, (4 + blk) * 128:(4 + blk + 1) * 128],
                            rhs=xnbf[:, c, js], start=(c == 0), stop=(c == 7)),
                            reads=[(w1key, c), ("xn", c)], writes=[pgkey])
                    sgc, sgkey = sg[blk % 2], ("sg", blk % 2)
                    p.emit("scalar", lambda e, sgc=sgc, pg=pg: e.activation(out=sgc[:, :], in_=pg[:, :], func=AF.Silu),
                           reads=[pgkey], writes=[sgkey])
                    if weighted:
                        ht, htkey = hwt[blk % 2], ("hwt", blk % 2)
                        p.emit("vector", lambda e, ht=ht, sgc=sgc, pu=pu: e.tensor_mul(out=ht[:, :], in0=pu[:, :], in1=sgc[:, :]),
                               reads=[pukey, sgkey], writes=[htkey])
                        p.emit("gpsimd", lambda e, ht=ht, blk=blk, js=js, hcur=hcur, cb=cb: e.tensor_mul(
                            out=hcur[:, blk, js], in0=ht[:, :], in1=cb[:, :]),
                            reads=[htkey, cbkey], writes=[(hkey, blk, j)])
                    else:
                        p.emit("vector", lambda e, blk=blk, js=js, hcur=hcur, sgc=sgc, pu=pu: e.tensor_mul(
                            out=hcur[:, blk, js], in0=pu[:, :], in1=sgc[:, :]),
                               reads=[pukey, sgkey], writes=[(hkey, blk, j)])
                for fo in range(8):
                    pz, pzkey = tl.pmm[(fo % 4)], ("pmm", fo % 4)
                    for blk in range(4):
                        p.emit("tensor", lambda e, blk=blk, fo=fo, js=js, w2cur=w2cur, hcur=hcur, pz=pz, j=j: e.matmul(
                            out=pz[:, :], lhsT=w2cur[:, blk, fo * 128:(fo + 1) * 128],
                            rhs=hcur[:, blk, js], start=(blk == 0), stop=(blk == 3)),
                            reads=[(w2key, blk), (hkey, blk, j)], writes=[pzkey])
                    if first:
                        p.emit("vector", lambda e, fo=fo, js=js, pz=pz: e.tensor_copy(out=zbf[:, fo, js], in_=pz[:, :]),
                               reads=[pzkey], writes=[("z", fo, j)])
                    else:
                        p.emit("vector", lambda e, fo=fo, js=js, pz=pz: e.tensor_add(out=zbf[:, fo, js], in0=zbf[:, fo, js], in1=pz[:, :]),
                               reads=[pzkey, ("z", fo, j)], writes=[("z", fo, j)])

        for eidx in range(E):
            b = eidx % 2
            load_cast(W1[eidx].rearrange("(c p) f -> c p f", p=128), w1b[b], f"w1b{b}", 8, 2 * HH)
            load_cast(W2[eidx].rearrange("(c p) f -> c p f", p=128), w2b[b], f"w2b{b}", 4, D)
            up_and_down(eidx, w1b[b], f"w1b{b}", w2b[b], f"w2b{b}", h[b], f"h{b}",
                        weighted=True, first=(eidx == 0))
        load_cast(WS1.rearrange("(c p) f -> c p f", p=128), w1b[0], "w1b0", 8, 2 * HH)
        load_cast(WS2.rearrange("(c p) f -> c p f", p=128), w2b[0], "w2b0", 4, D)
        up_and_down(0, w1b[0], "w1b0", w2b[0], "w2b0", h[1], "h1", weighted=False, first=False)

        xc = [sb(f"xc{i}", [128, 512], F32) for i in range(2)]
        uc = [sb(f"uc{i}", [128, 512], F32) for i in range(2)]
        for fo in range(8):
            for j in range(njt):
                js = slice(j * 512, (j + 1) * 512)
                b = j % 2
                xck, uck = ("xc", b), ("uc", b)
                p.dma("sync", xc[b][:, :], Xre[fo][:, js], writes=[xck])
                p.dma("sync", uc[b][:, :], Ure[fo][:, js], writes=[uck])
                p.emit("vector", lambda e, b=b: e.tensor_add(
                    out=xc[b][:, :], in0=xc[b][:, :], in1=uc[b][:, :]),
                    reads=[xck, uck], writes=[xck])
                p.emit("vector", lambda e, fo=fo, b=b, js=js, j=j: e.tensor_add(
                    out=xc[b][:, :], in0=xc[b][:, :], in1=zbf[:, fo, js]),
                    reads=[xck, ("z", fo, j)], writes=[xck])
                p.dma("gpsimd", Ore[fo][:, js], xc[b][:, :], reads=[xck])
    return prog


def mk_lb1(T=TPC):
    def prog(nc, p, es):
        X = nc.dram_tensor("x_fm", [D, T], F32, kind="ExternalInput").ap()
        W = nc.dram_tensor("w_qkv", [D, 3 * D], F32, kind="ExternalInput").ap()
        WGB = nc.dram_tensor("w_gb", [D, 32], F32, kind="ExternalInput").ap()
        BGB = nc.dram_tensor("b_gb", [32, 1], F32, kind="ExternalInput").ap()
        O = nc.dram_tensor("qkv_fm", [3 * D, T], F32, kind="ExternalOutput").ap()
        OG = nc.dram_tensor("gates_fm", [32, T], F32, kind="ExternalOutput").ap()
        tl = TL(nc, p, es, T=T)
        xsb = tl.load_fm("x", X, D, T, F32)
        wsb = tl.load_weight_bf("w", W, D, 3 * D)
        wgb = tl.sb("wgb", [128, 8, 32], F32)
        p.dma("sync", wgb[:, :, :], WGB.rearrange("(c p) g -> p c g", p=128), writes=[("wgb",)])
        bgb = tl.sb("bgb", [32, 1], F32)
        p.dma("sync", bgb[:, :], BGB, writes=[("bgb",)])
        ybf = tl.sbuf_fm("y", D, T, BF16)
        y32t = tl.sb("y32t", [128, 8, 512], F32)
        gt = tl.sb("gt", [32, 512], F32)

        def tile_cb(j):
            js = slice(j * 512, (j + 1) * 512)
            pg = tl.pmm[2]
            for c in range(8):
                p.emit("tensor", lambda e, c=c, pg=pg: e.matmul(
                    out=pg[0:32, :], lhsT=wgb[:, c, :], rhs=y32t[:, c, :],
                    start=(c == 0), stop=(c == 7)),
                    reads=[("wgb",), ("y32", c)], writes=[("pmm", 2)])
            p.emit("scalar", lambda e: e.activation(out=gt[:, :], in_=pg[0:32, :], func=AF.Sigmoid,
                                                    bias=bgb[:, 0:1]),
                   reads=[("pmm", 2), ("bgb",)], writes=[("gt",)])
            p.dma("gpsimd", OG[:, js], gt[:, :], reads=[("gt",)])

        tl.emit_ln(xsb, "x", 8, ybf, "y", y32_tile=y32t, y32name="y32", tile_cb=tile_cb)
        eo = [tl.sb(f"eo{i}", [128, 512], F32) for i in range(3)]
        Ore = O.rearrange("(fo p) t -> fo p t", p=128)
        ecnt = [0]

        def epi(fo, j, pmap, pkey):
            js = slice(j * tl.tt, (j + 1) * tl.tt)
            i = ecnt[0] % 3
            ecnt[0] += 1
            ek = ("eo", i)
            p.emit("vector", lambda e, i=i: e.tensor_copy(out=eo[i][:, :], in_=pmap),
                   reads=[pkey], writes=[ek])
            p.dma("gpsimd", Ore[fo][:, js], eo[i][:, :], reads=[ek])

        tl.emit_mm(ybf, "y", 8, wsb, "w", 3 * D, epi)
    return prog


def _slices(x):
    return [np.ascontiguousarray(x[c * TPC:(c + 1) * TPC].T) for c in range(NCORES)]


def _gather_x(res, key="xout_fm"):
    return np.concatenate([np.asarray(res[c][key]).T for c in range(NCORES)], axis=0)


_ID128 = np.eye(128, dtype=np.float32)
_PP, _QQ = np.meshgrid(np.arange(128), np.arange(512), indexing="ij")
_CAUSAL_MASKS = np.stack([np.where(_PP - _QQ > -128 * m, -1e30, 0.0) for m in range(4)]).astype(np.float32)


def _slopes16():
    start = 2.0 ** (-0.5)
    return np.array([start ** (i + 1) for i in range(16)], np.float32)


def _vtile(v, dv):
    S_ = v.shape[1]
    return np.ascontiguousarray(v.T.reshape(S_ // 128, 128, dv).transpose(1, 0, 2))


def _attn_pair_inputs(qkv_cores, masks, slopes=None, dv=64, vofs=2048, kofs=1024):
    H = 16
    qkv_b = [np.concatenate([qkv_cores[4 * b + c] for c in range(4)], axis=1) for b in range(B)]
    NKT = S // 128
    pos = np.arange(S, dtype=np.float32)
    ims = []
    for core in range(NCORES):
        qs, ks, vs, ramps, augs = [], [], [], [], []
        for pi in range(4 * core, 4 * core + 4):
            b, h = divmod(pi, H)
            qs.append(qkv_b[b][h * 64:(h + 1) * 64])
            ks.append(qkv_b[b][kofs + h * 64:kofs + h * 64 + 64])
            vs.append(_vtile(qkv_b[b][vofs + h * 64:vofs + h * 64 + dv].astype(BF), dv))
            if slopes is not None:
                ramps.append((slopes[h] * pos).reshape(NKT, 128).T.astype(np.float32))
                augs.append((-slopes[h] * pos)[None, :].astype(np.float32))
            else:
                ramps.append(np.zeros((128, NKT), np.float32))
                augs.append(np.zeros((1, S), np.float32))
        ims.append({"qp": np.ascontiguousarray(np.stack(qs)), "kp": np.ascontiguousarray(np.stack(ks)),
                    "vp": np.ascontiguousarray(np.stack(vs)),
                    "ramp": np.stack(ramps), "augq": np.stack(augs), "masks": masks})
    return ims


def _scatter_attnout(res, dv=64):
    H = 16
    ao = np.zeros((B, H * dv, S), dtype=BF)
    for core in range(NCORES):
        r = np.asarray(res[core]["attnout"])
        for i, pi in enumerate(range(4 * core, 4 * core + 4)):
            b, h = divmod(pi, H)
            ao[b, h * dv:(h + 1) * dv] = r[i]
    return ao


_PROGS = {}


def _prog(name, fn):
    if name not in _PROGS:
        _PROGS[name] = fn()
    return _PROGS[name]


def sub_l0a(x, inp):
    im1 = [{"x_fm": xs, "w_qkv": inp["L0_attn_a_W_qkv"], "b_qkv": bias_host(inp["L0_attn_a_b_qkv"])}
           for xs in _slices(x)]
    r1 = run_prog(_prog("qkv", mk_l0a_qkv), im1)
    qkv = [np.asarray(r1[c]["qkv_fm"]) for c in range(NCORES)]
    im2 = _attn_pair_inputs(qkv, _CAUSAL_MASKS, slopes=_slopes16())
    r2 = run_prog(_prog("flash64", lambda: mk_flash(S_=S, npair=4, dv=64)), im2)
    ao = _scatter_attnout(r2)
    aof = ao.transpose(0, 2, 1).reshape(NTOK, D)
    im3 = [{"x_fm": xs, "a_fm": np.ascontiguousarray(aof[c * TPC:(c + 1) * TPC].T),
            "w": inp["L0_attn_a_W_out"], "b": bias_host(inp["L0_attn_a_b_out"])}
           for c, xs in enumerate(_slices(x))]
    r3 = run_prog(_prog("residb", lambda: mk_resid_proj(use_bias=True)), im3)
    return _gather_x(r3)


def sub_l0ffn(x, inp):
    im = [{"x_fm": xs, "wg_t": np.ascontiguousarray(inp["L0_ffn_Wg"].T),
           "w1": inp["L0_ffn_W1"], "w2": inp["L0_ffn_W2"],
           "ws1": inp["L0_ffn_Ws1"], "ws2": inp["L0_ffn_Ws2"], "ident_f32": _ID128}
          for xs in _slices(x)]
    r = run_prog(_prog("l0ffn", mk_l0_ffn), im)
    return _gather_x(r)


def sub_l0b(x, inp):
    Wgb = np.concatenate([inp["L0_attn_b_Wg"], inp["L0_attn_b_Wb"]], axis=0)
    bgb = np.concatenate([inp["L0_attn_b_bg"], inp["L0_attn_b_bb"]]).astype(np.float32).reshape(32, 1)
    im1 = [{"x_fm": xs, "w_qkv": np.ascontiguousarray(inp["L0_attn_b_Wqkv"].T),
            "w_gb": np.ascontiguousarray(Wgb.T), "b_gb": bgb} for xs in _slices(x)]
    r1 = run_prog(_prog("lb1", mk_lb1), im1)
    qkv = [np.concatenate([np.asarray(r1[4 * b + c]["qkv_fm"]) for c in range(4)], axis=1) for b in range(B)]
    gts = [np.concatenate([np.asarray(r1[4 * b + c]["gates_fm"]) for c in range(4)], axis=1) for b in range(B)]
    mtris = np.tile(np.tril(np.ones((128, 128), np.float32), -1), (1, 4))
    mtriui = np.tile(np.triu(np.ones((128, 128), np.float32), 0), (1, 4))
    im2 = []
    for core in range(NCORES):
        qs, ks, vs, azs, bzs = [], [], [], [], []
        for pi in range(4 * core, 4 * core + 4):
            b, h = divmod(pi, 16)
            qs.append(qkv[b][h * 64:(h + 1) * 64])
            ks.append(qkv[b][1024 + h * 64:1024 + h * 64 + 64])
            vs.append(qkv[b][2048 + h * 64:2048 + h * 64 + 64])
            azs.append(gts[b][h:h + 1])
            bzs.append(gts[b][16 + h:16 + h + 1])
        im2.append({"qs": np.ascontiguousarray(np.stack(qs)), "ks": np.ascontiguousarray(np.stack(ks)),
                    "vs": np.ascontiguousarray(np.stack(vs)), "az": np.ascontiguousarray(np.stack(azs)),
                    "bz": np.ascontiguousarray(np.stack(bzs)),
                    "mtris": np.ascontiguousarray(mtris), "mtriui": np.ascontiguousarray(mtriui),
                    "ident_f32": _ID128})
    r2 = run_prog(_prog("scan", lambda: mk_l0b_scan(S_=S, npair=4)), im2)
    ao = np.zeros((B, S, D), dtype=BF)
    for core in range(NCORES):
        o = np.asarray(r2[core]["out_tm"])
        for i, pi in enumerate(range(4 * core, 4 * core + 4)):
            b, h = divmod(pi, 16)
            ao[b, :, h * 64:(h + 1) * 64] = o[i].reshape(S, 64)
    aof = ao.reshape(NTOK, D)
    im3 = [{"x_fm": xs, "a_fm": np.ascontiguousarray(aof[c * TPC:(c + 1) * TPC].T),
            "w": np.ascontiguousarray(inp["L0_attn_b_Wout"].T)} for c, xs in enumerate(_slices(x))]
    r3 = run_prog(_prog("resid", lambda: mk_resid_proj(use_bias=False)), im3)
    return _gather_x(r3)


def sub_l1a(x, inp):
    Wc = np.concatenate([inp["L1_attn_a_Wq"], inp["L1_attn_a_Wk"], inp["L1_attn_a_Wv"]], axis=1)
    bc = np.concatenate([inp["L1_attn_a_bq"], inp["L1_attn_a_bk"], inp["L1_attn_a_bv"]])
    im1 = [{"x_fm": xs, "w_qkv": np.ascontiguousarray(Wc), "b_qkv": bias_host(bc)} for xs in _slices(x)]
    r1 = run_prog(_prog("qkv", mk_l0a_qkv), im1)
    qkv = [np.asarray(r1[c]["qkv_fm"]) for c in range(NCORES)]
    masks = np.stack([np.where(np.abs(_PP - _QQ + 128 * (m - 2)) <= 256, 0.0, -1e30)
                      for m in range(8)]).astype(np.float32)
    im2 = _attn_pair_inputs(qkv, masks, slopes=None)
    NKT = S // 128
    krange = lambda qt: range(max(0, 4 * qt - 2), min(NKT - 1, 4 * qt + 5) + 1)
    maskid = lambda qt, kt: kt - 4 * qt + 2
    r2 = run_prog(_prog("band", lambda: mk_flash(S_=S, npair=4, dv=64, alibi=False,
                                                 krange=krange, maskid=maskid, nmasks=8)), im2)
    ao = _scatter_attnout(r2)
    aof = ao.transpose(0, 2, 1).reshape(NTOK, D)
    im3 = [{"x_fm": xs, "a_fm": np.ascontiguousarray(aof[c * TPC:(c + 1) * TPC].T),
            "w": inp["L1_attn_a_Wo"], "b": bias_host(inp["L1_attn_a_bo"])}
           for c, xs in enumerate(_slices(x))]
    r3 = run_prog(_prog("residb", lambda: mk_resid_proj(use_bias=True)), im3)
    return _gather_x(r3)


def _l1ffn_static(inp):
    THETA = 1e5
    permq = np.zeros(D, np.int64)
    for r_ in range(512):
        h, d = r_ // 32, r_ % 32
        permq[r_] = h * 64 + d
    for r_ in range(256):
        h, i = r_ // 16, r_ % 16
        permq[512 + r_] = h * 64 + 32 + 2 * i
        permq[768 + r_] = h * 64 + 33 + 2 * i
    Wq_p = np.ascontiguousarray(np.asarray(inp["L1_ffn_Wq"])[:, permq])
    permk = np.concatenate([np.arange(128), 128 + 2 * np.arange(16), 129 + 2 * np.arange(16)])
    Wkva_p = np.ascontiguousarray(np.asarray(inp["L1_ffn_Wkva"])[:, permk])
    fr = 1.0 / THETA ** (np.arange(0, 32, 2) / 32.0)
    cos_t, sin_t = [], []
    for c in range(NCORES):
        t0 = (c % 4) * TPC
        tpos = t0 + np.arange(TPC, dtype=np.float32)
        angr = np.array([fr[r_ % 16] * tpos for r_ in range(128)], np.float32)
        cos_t.append(np.cos(angr).astype(np.float32))
        sin_t.append(np.sin(angr).astype(np.float32))
    Wo_r = np.asarray(inp["L1_ffn_Wo"]).reshape(16, 64, D)
    Wo_sl = np.ascontiguousarray(Wo_r[:, :32, :].reshape(512, D))
    W1cat = np.ascontiguousarray(np.concatenate([inp["L1_ffn_Wu"], inp["L1_ffn_Wg"]], axis=2))
    WS1cat = np.ascontiguousarray(np.concatenate([inp["L1_ffn_sWu"][0], inp["L1_ffn_sWg"][0]], axis=1))
    return dict(Wq_p=Wq_p, Wkva_p=Wkva_p, cos_t=cos_t, sin_t=sin_t, Wo_sl=Wo_sl,
                W1cat=W1cat, WS1cat=WS1cat)


def sub_l1ffn(x, inp, st):
    im1 = [{"x_fm": xs, "wq": st["Wq_p"], "wkva": st["Wkva_p"], "wkvb": inp["L1_ffn_Wkvb"],
            "wln1": bias_host(inp["L1_ffn_w_ln1"]), "cosT": st["cos_t"][c], "sinT": st["sin_t"][c]}
           for c, xs in enumerate(_slices(x))]
    r1 = run_prog(_prog("lm1", mk_lm1), im1)
    NKT = S // 128
    qf = [np.concatenate([np.asarray(r1[4 * b + c]["q_out"]) for c in range(4)], axis=1) for b in range(B)]
    kvf = [np.concatenate([np.asarray(r1[4 * b + c]["kv_out"]) for c in range(4)], axis=1) for b in range(B)]
    krf = [np.concatenate([np.asarray(r1[4 * b + c]["krope_out"]) for c in range(4)], axis=1) for b in range(B)]
    im2 = []
    for core in range(NCORES):
        qs, ks, vs = [], [], []
        for pi in range(4 * core, 4 * core + 4):
            b, h = divmod(pi, 16)
            qs.append(np.concatenate([qf[b][h * 32:h * 32 + 32], qf[b][512 + h * 16:512 + h * 16 + 16],
                                      qf[b][768 + h * 16:768 + h * 16 + 16]], axis=0))
            ks.append(np.concatenate([kvf[b][h * 32:h * 32 + 32], krf[b]], axis=0))
            vs.append(_vtile(kvf[b][h * 32:h * 32 + 32], 32))
        im2.append({"qp": np.ascontiguousarray(np.stack(qs)), "kp": np.ascontiguousarray(np.stack(ks)),
                    "vp": np.ascontiguousarray(np.stack(vs)),
                    "ramp": np.zeros((4, 128, NKT), np.float32),
                    "augq": np.zeros((4, 1, S), np.float32), "masks": _CAUSAL_MASKS})
    r2 = run_prog(_prog("flash32", lambda: mk_flash(S_=S, npair=4, dv=32, alibi=False)), im2)
    ao = _scatter_attnout(r2, dv=32)
    im3 = []
    for c, xs in enumerate(_slices(x)):
        b = c // 4
        im3.append({"x_fm": xs, "y_fm": np.ascontiguousarray(np.asarray(r1[c]["y_out"])),
                    "a_fm": np.ascontiguousarray(ao[b][:, (c % 4) * TPC:(c % 4 + 1) * TPC]),
                    "wo": st["Wo_sl"], "wln2": bias_host(inp["L1_ffn_w_ln2"]),
                    "wgate": inp["L1_ffn_Wgate"],
                    "ebias": np.asarray(inp["L1_ffn_expert_bias"], np.float32).reshape(8, 1),
                    "w1": st["W1cat"], "w2": inp["L1_ffn_Wd"],
                    "ws1": st["WS1cat"], "ws2": inp["L1_ffn_sWd"][0], "ident_f32": _ID128})
    r3 = run_prog(_prog("lm3", mk_lm3), im3)
    return _gather_x(r3)


def sub_l1b(x, inp):
    Wc = np.concatenate([np.asarray(inp["L1_attn_b_Wq"]).T, np.asarray(inp["L1_attn_b_Wk"]).T,
                         np.asarray(inp["L1_attn_b_Wv"]).T], axis=1)
    im1 = [{"x_fm": xs, "w_qkv": np.ascontiguousarray(Wc),
            "b_qkv": np.zeros((128, 24), np.float32)} for xs in _slices(x)]
    r1 = run_prog(_prog("qkv", mk_l0a_qkv), im1)
    qkv_b = [np.concatenate([np.asarray(r1[4 * b + c]["qkv_fm"]) for c in range(4)], axis=1) for b in range(B)]
    Sq, HALO = TPC, 128
    Sk = Sq + 2 * HALO
    NKTk = Sk // 128
    NQB = Sq // 512
    im2 = []
    for core in range(NCORES):
        b, quad = core // 4, core % 4
        t0 = quad * Sq
        lo, hi = t0 - HALO, t0 + Sq + HALO
        def sl(rows):
            out = np.zeros((rows.shape[0], Sk), dtype=BF)
            a0, a1 = max(lo, 0), min(hi, S)
            out[:, a0 - lo:a1 - lo] = rows[:, a0:a1]
            return out
        qs, ks, vs = [], [], []
        for h in range(16):
            qs.append(qkv_b[b][h * 64:(h + 1) * 64, t0:t0 + Sq])
            ks.append(sl(qkv_b[b][1024 + h * 64:1024 + h * 64 + 64]))
            vs.append(_vtile(sl(qkv_b[b][2048 + h * 64:2048 + h * 64 + 64]), 64))
        masks = np.zeros((NQB * 6, 128, 512), np.float32)
        for qt in range(NQB):
            for m in range(6):
                kt = 4 * qt + m
                dd = 128 * kt - HALO - 512 * qt
                rel = _PP + dd - _QQ
                jg = t0 + 512 * qt + _QQ + rel
                ok = (rel >= -8) & (rel <= 7) & (jg >= 0) & (jg < S)
                masks[qt * 6 + m] = np.where(ok, 0.0, -1e30)
        im2.append({"qp": np.ascontiguousarray(np.stack(qs)), "kp": np.ascontiguousarray(np.stack(ks)),
                    "vp": np.ascontiguousarray(np.stack(vs)),
                    "ramp": np.zeros((16, 128, NKTk), np.float32),
                    "augq": np.zeros((16, 1, Sq), np.float32), "masks": masks})
    krange = lambda qt: range(4 * qt, 4 * qt + 6)
    maskid = lambda qt, kt: qt * 6 + (kt - 4 * qt)
    r2 = run_prog(_prog("win", lambda: mk_flash(S_=Sq, npair=16, dv=64, alibi=False,
                                                krange=krange, maskid=maskid, nmasks=NQB * 6, Sk=Sk)), im2)
    im3 = []
    for c, xs in enumerate(_slices(x)):
        a = np.asarray(r2[c]["attnout"]).reshape(1024, Sq)
        im3.append({"x_fm": xs, "a_fm": np.ascontiguousarray(a),
                    "w": np.ascontiguousarray(np.asarray(inp["L1_attn_b_Wo"]).T)})
    r3 = run_prog(_prog("resid", lambda: mk_resid_proj(use_bias=False)), im3)
    return _gather_x(r3)


def kernel(**inputs):
    x = np.ascontiguousarray(np.asarray(inputs["L0_attn_a_x"], np.float32).reshape(NTOK, D))
    st = _l1ffn_static(inputs)
    x = sub_l0a(x, inputs)
    x = sub_l0ffn(x, inputs)
    x = sub_l0b(x, inputs)
    x = sub_l0ffn(x, inputs)
    x = sub_l1a(x, inputs)
    x = sub_l1ffn(x, inputs, st)
    x = sub_l1b(x, inputs)
    x = sub_l1ffn(x, inputs, st)
    return x.reshape(B, S, D).astype(np.float32)


# version 07
# speedup vs baseline: 1.2767x; 1.1894x over previous
import math
import contextlib
import numpy as np
import ml_dtypes
import concourse.bass as bass
import concourse.mybir as mybir
from concourse.bass_utils import run_bass_kernel_spmd

F32 = mybir.dt.float32
BF16 = mybir.dt.bfloat16
AF = mybir.ActivationFunctionType
ALU = mybir.AluOpType
AX = mybir.AxisListType
BF = ml_dtypes.bfloat16

ENGINES = ("sync", "tensor", "vector", "scalar", "gpsimd")

B, S, D = 2, 8192, 1024
NTOK = B * S
NCORES = 8
TPC = NTOK // NCORES


class Prog:

    def __init__(self, nc):
        self.nc = nc
        self.ops = {e: [] for e in ENGINES}
        self.cnt = {e: 0 for e in ENGINES}
        self.ndma = {e: 0 for e in ENGINES}
        self.markers = {e: 0 for e in ENGINES}
        self.last_w = {}
        self.readers = {}
        self.hw = {e: {} for e in ENGINES}

    def _marker(self, q, ndma_needed):
        if self.markers[q] >= ndma_needed:
            return 16 * self.markers[q]
        cur = self.ndma[q]
        self.ops[q].append(([(("D", q), 16 * cur)], None, 0))
        self.markers[q] = cur
        self.hw[q][("D", q)] = max(self.hw[q].get(("D", q), 0), 16 * cur)
        return 16 * cur

    def emit(self, eng, fn, reads=(), writes=(), dma=False):
        need = {}

        def add(e2, v, isdma):
            if isdma:
                sem = ("D", e2)
                if e2 != eng:
                    v = self._marker(e2, v // 16)
                else:
                    v = self._marker(e2, v // 16)
            else:
                if e2 == eng and eng == "tensor" and not dma:
                    return
                sem = ("C", e2)
            if need.get(sem, 0) < v:
                need[sem] = v

        for k in reads:
            lw = self.last_w.get(k)
            if lw is not None:
                add(*lw)
        for k in writes:
            lw = self.last_w.get(k)
            if lw is not None:
                add(*lw)
            for (e2, isdma), v in self.readers.get(k, {}).items():
                add(e2, v, isdma)
        waits = []
        for sem in sorted(need):
            v = need[sem]
            if v > self.hw[eng].get(sem, 0):
                waits.append((sem, v))
                self.hw[eng][sem] = v
        if dma:
            self.ndma[eng] += 1
            my = (eng, 16 * self.ndma[eng], True)
        else:
            self.cnt[eng] += 1
            my = (eng, self.cnt[eng], False)
        for k in writes:
            self.last_w[k] = my
            self.readers[k] = {}
        for k in reads:
            self.readers.setdefault(k, {})[(eng, dma)] = my[1]
        self.ops[eng].append((waits, fn, 16 if dma else 1))
        return my

    def barrier(self):
        vals = {}
        for q in ENGINES:
            if self.ndma[q] > 0:
                vals[("D", q)] = self._marker(q, self.ndma[q])
            if self.cnt[q] > 0:
                vals[("C", q)] = self.cnt[q]
        for e in ENGINES:
            waits = []
            for sem, v in sorted(vals.items()):
                if sem == ("C", e):
                    continue
                if v > self.hw[e].get(sem, 0):
                    waits.append((sem, v))
                    self.hw[e][sem] = v
            if waits:
                self.ops[e].append((waits, None, 0))

    def dma(self, eng, out, in_, reads=(), writes=()):
        return self.emit(eng, lambda e, o=out, i=in_: e.dma_start(out=o, in_=i),
                         reads=reads, writes=writes, dma=True)

    def finalize(self, es):
        sems = {}
        for e in ENGINES:
            if self.cnt[e] > 0:
                sems[("C", e)] = es.enter_context(self.nc.semaphore(f"c_{e}"))
            if self.ndma[e] > 0:
                sems[("D", e)] = es.enter_context(self.nc.semaphore(f"d_{e}"))
        final = [(sem, self.cnt[e] if sem[0] == "C" else 16 * self.ndma[e])
                 for e in ENGINES for sem in [("C", e), ("D", e)] if sem in sems]
        block = es.enter_context(self.nc.Block())
        for e in ENGINES:
            ops = self.ops[e]
            if not ops and e != "gpsimd":
                continue

            def mk(e, ops):
                def body(engobj):
                    for waits, fn, amt in ops:
                        for (sem, v) in waits:
                            engobj.wait_ge(sems[sem], v)
                        if fn is None:
                            continue
                        inst = fn(engobj)
                        inst.then_inc(sems[("D" if amt == 16 else "C", e)], amt)
                    if e == "gpsimd":
                        for sem, v in final:
                            if v > 0 and self.hw["gpsimd"].get(sem, 0) < v:
                                engobj.wait_ge(sems[sem], v)
                return body

            getattr(block, e)(mk(e, ops))


def bias_host(b):
    return np.ascontiguousarray(np.asarray(b, dtype=np.float32).reshape(-1, 128).T)


def run_prog(progfn, in_maps, sim=False, **kwargs):
    if sim:
        from concourse import bacc
        from concourse.bass_interp import CoreSim
        nc = bacc.Bacc("TRN2", target_bir_lowering=False, debug=True)
    else:
        nc = bass.Bass("TRN2", target_bir_lowering=False)
    with contextlib.ExitStack() as es:
        p = Prog(nc)
        progfn(nc, p, es)
        p.finalize(es)
    if sim:
        from concourse.bass_interp import CoreSim
        nc.compile()
        s = CoreSim(nc)
        outs = [a.memorylocations[0].name for a in nc.m.functions[0].allocations
                if getattr(a, "kind", None) == "ExternalOutput"]
        for name, arr in in_maps[0].items():
            s.tensor(name)[:] = arr
        s.simulate(check_with_hw=False)
        return [{k: np.asarray(s.tensor(k)) for k in outs}]
    res = run_bass_kernel_spmd(nc, in_maps, core_ids=list(range(len(in_maps))), **kwargs)
    return res.results



class TL:
    def __init__(self, nc, p, es, T=TPC, tt=512):
        self.nc, self.p, self.es, self.T, self.tt = nc, p, es, T, tt
        self.njt = T // tt
        sb = self.sb = lambda n, s, d: es.enter_context(nc.sbuf_tensor(n, s, d))
        ps = self.ps = lambda n, s, d: es.enter_context(nc.psum_tensor(n, s, d))
        self.ones_col = sb("ones_col", [128, 1], F32)
        self.ones_row = sb("ones_row", [1, 128], F32)
        p.emit("vector", lambda e: e.memset(self.ones_col[:], 1.0), writes=[("ones_col", 0)])
        p.emit("vector", lambda e: e.memset(self.ones_row[:], 1.0), writes=[("ones_row", 0)])
        self.sq0 = sb("sq0", [128, tt], F32)
        self.sq1 = sb("sq1", [128, tt], F32)
        self.r0 = sb("r0", [1, tt], F32)
        self.r1 = sb("r1", [1, tt], F32)
        self.r2 = sb("r2", [1, tt], F32)
        self.prx = ps("prx", [1, 512], F32)
        self.prq = ps("prq", [1, 512], F32)
        self.pbS = ps("pbS", [128, 512], F32)
        self.pbC = ps("pbC", [128, 512], F32)
        self.pmm = [ps(f"pmm{i}", [128, 512], F32) for i in range(4)]
        self._mm_rot = 0
        self._uid = 0
        self._consts = {}
        self._cst = sb("consts", [128, 16], F32)

    def const(self, value):
        if value not in self._consts:
            assert len(self._consts) < 16
            k = len(self._consts)
            self.p.emit("vector", lambda e, k=k, v=value: e.memset(self._cst[:, k:k + 1], v),
                        writes=[("const", value)])
            self._consts[value] = k
        k = self._consts[value]
        return self._cst[:, k:k + 1]

    def uid(self):
        self._uid += 1
        return self._uid

    def sbuf_fm(self, name, rows, T, dtype):
        assert rows % 128 == 0
        return self.sb(name, [128, rows // 128, T], dtype)

    def load_fm(self, name, hbm_ap, rows, T, dtype, eng="sync"):
        t = self.sbuf_fm(name, rows, T, dtype)
        nchunk = rows // 128
        src = hbm_ap.rearrange("(c p) t -> p c t", p=128)
        self.p.dma(eng, t[:, :, :], src, writes=[(name, c) for c in range(nchunk)])
        return t

    def load_weight_bf(self, name, hbm_ap, kin, nfo, stage_cols=None, eng="sync"):
        nchunk = kin // 128
        wb = self.sb(name + "_bf", [128, nchunk, nfo], BF16)
        stg = [self.sb(f"{name}_stg{i}", [128, nfo], F32) for i in range(2)]
        src = hbm_ap.rearrange("(c p) f -> c p f", p=128)
        for c in range(nchunk):
            s = stg[c % 2]
            skey = (name + "_stg", c % 2)
            self.p.dma(eng, s[:, :], src[c], reads=[], writes=[skey])
            self.p.emit("vector", lambda e, s=s, c=c: e.tensor_copy(out=wb[:, c, :], in_=s[:, :]),
                        reads=[skey], writes=[(name, c)])
        return wb

    def load_rows_f32(self, name, hbm_ap, n):
        assert n % 128 == 0
        t = self.sb(name, [128, n // 128], F32)
        self.p.dma("sync", t[:, :], hbm_ap, writes=[(name, 0)])
        return t

    def emit_ln(self, xsb, xname, nchunk, out_bf, oname, out_f32=None, o32name=None,
               y32_tile=None, y32name=None, tile_cb=None, rms_w=None, eps=1e-5):
        p, tt = self.p, self.tt
        nd = nchunk * 128
        for j in range(self.njt):
            js = slice(j * tt, (j + 1) * tt)
            u = self.uid()
            for c in range(nchunk):
                p.emit("tensor", lambda e, c=c, js=js: e.matmul(
                    out=self.prx[:, :], lhsT=self.ones_col[:], rhs=xsb[:, c, js],
                    start=(c == 0), stop=(c == nchunk - 1)),
                    reads=[("ones_col", 0), (xname, c)], writes=[("prx",)])
            for c in range(nchunk):
                sq = self.sq0 if c % 2 == 0 else self.sq1
                sqk = ("sq", c % 2)
                p.emit("scalar", lambda e, c=c, sq=sq, js=js: e.activation(
                    out=sq[:, :], in_=xsb[:, c, js], func=AF.Square),
                    reads=[(xname, c)], writes=[sqk])
                p.emit("tensor", lambda e, c=c, sq=sq: e.matmul(
                    out=self.prq[:, :], lhsT=self.ones_col[:], rhs=sq[:, :],
                    start=(c == 0), stop=(c == nchunk - 1)),
                    reads=[("ones_col", 0), sqk], writes=[("prq",)])
            p.emit("vector", lambda e: e.tensor_scalar_mul(out=self.r0[:, :], in0=self.prx[:, :], scalar1=-1.0 / nd),
                   reads=[("prx",)], writes=[("r0",)])
            p.emit("vector", lambda e: e.tensor_scalar_mul(out=self.r1[:, :], in0=self.prq[:, :], scalar1=1.0 / nd),
                   reads=[("prq",)], writes=[("r1",)])
            p.emit("vector", lambda e: e.tensor_mul(out=self.r2[:, :], in0=self.r0[:, :], in1=self.r0[:, :]),
                   reads=[("r0",)], writes=[("r2",)])
            p.emit("vector", lambda e: e.tensor_sub(out=self.r1[:, :], in0=self.r1[:, :], in1=self.r2[:, :]),
                   reads=[("r1",), ("r2",)], writes=[("r1",)])
            epsap = self.const(eps)
            p.emit("scalar", lambda e: e.activation(out=self.r1[:, :], in_=self.r1[:, :], func=AF.Sqrt,
                                                    bias=epsap[0:1, :]),
                   reads=[("r1",), ("const", eps)], writes=[("r1",)])
            p.emit("vector", lambda e: e.reciprocal(out=self.r1[:, :], in_=self.r1[:, :]),
                   reads=[("r1",)], writes=[("r1",)])
            p.emit("vector", lambda e: e.tensor_mul(out=self.r2[:, :], in0=self.r0[:, :], in1=self.r1[:, :]),
                   reads=[("r0",), ("r1",)], writes=[("r2",)])
            p.emit("tensor", lambda e: e.matmul(out=self.pbS[:, :], lhsT=self.ones_row[:], rhs=self.r1[:, :],
                                                start=True, stop=True),
                   reads=[("ones_row", 0), ("r1",)], writes=[("pbS",)])
            p.emit("tensor", lambda e: e.matmul(out=self.pbC[:, :], lhsT=self.ones_row[:], rhs=self.r2[:, :],
                                                start=True, stop=True),
                   reads=[("ones_row", 0), ("r2",)], writes=[("pbC",)])
            for c in range(nchunk):
                sq = self.sq0 if c % 2 == 0 else self.sq1
                sqk = ("sq", c % 2)
                p.emit("vector", lambda e, c=c, sq=sq, js=js: e.tensor_mul(out=sq[:, :], in0=xsb[:, c, js], in1=self.pbS[:, :]),
                       reads=[(xname, c), ("pbS",)], writes=[sqk])
                if out_bf is not None:
                    p.emit("vector", lambda e, c=c, sq=sq, js=js: e.tensor_add(out=out_bf[:, c, js], in0=sq[:, :], in1=self.pbC[:, :]),
                           reads=[sqk, ("pbC",)], writes=[(oname, c)])
                if out_f32 is not None:
                    p.emit("vector", lambda e, c=c, sq=sq, js=js: e.tensor_add(out=out_f32[:, c, js], in0=sq[:, :], in1=self.pbC[:, :]),
                           reads=[sqk, ("pbC",)], writes=[(o32name, c)])
                if y32_tile is not None:
                    p.emit("vector", lambda e, c=c, sq=sq: e.tensor_add(out=y32_tile[:, c, :], in0=sq[:, :], in1=self.pbC[:, :]),
                           reads=[sqk, ("pbC",)], writes=[(y32name, c)])
            if tile_cb is not None:
                tile_cb(j)

    def emit_mm(self, act, aname, ncin, w, wname, nfo, epilogue, jlist=None):
        p, tt = self.p, self.tt
        nfob = nfo // 128
        for j in (range(self.njt) if jlist is None else jlist):
            js = slice(j * tt, (j + 1) * tt)
            for fo in range(nfob):
                pm = self.pmm[self._mm_rot % len(self.pmm)]
                pmkey = ("pmm", self._mm_rot % len(self.pmm))
                self._mm_rot += 1
                for c in range(ncin):
                    p.emit("tensor", lambda e, c=c, fo=fo, pm=pm, js=js: e.matmul(
                        out=pm[:, :tt], lhsT=w[:, c, fo * 128:(fo + 1) * 128], rhs=act[:, c, js],
                        start=(c == 0), stop=(c == ncin - 1)),
                        reads=[(wname, c), (aname, c)], writes=[pmkey])
                epilogue(fo, j, pm[:, :tt], pmkey)



def mk_l0a_qkv(T=TPC, dbg=False):
    def prog(nc, p, es):
        X = nc.dram_tensor("x_fm", [D, T], F32, kind="ExternalInput").ap()
        W = nc.dram_tensor("w_qkv", [D, 3 * D], F32, kind="ExternalInput").ap()
        Bq = nc.dram_tensor("b_qkv", [128, (3 * D) // 128], F32, kind="ExternalInput").ap()
        O = nc.dram_tensor("qkv_fm", [3 * D, T], BF16, kind="ExternalOutput").ap()
        tl = TL(nc, p, es, T=T)
        xsb = tl.load_fm("x", X, D, T, F32)
        wsb = tl.load_weight_bf("w", W, D, 3 * D)
        bsb = tl.load_rows_f32("b", Bq, 3 * D)
        ybf = tl.sbuf_fm("y", D, T, BF16)
        tl.emit_ln(xsb, "x", 8, ybf, "y")
        if dbg:
            Yo = nc.dram_tensor("y_dbg", [D, T], BF16, kind="ExternalOutput").ap()
            p.dma("gpsimd", Yo.rearrange("(c p) t -> p c t", p=128), ybf[:, :, :],
                  reads=[("y", c) for c in range(8)])
        eo = [tl.sb(f"eo{i}", [128, 512], BF16) for i in range(3)]
        Ore = O.rearrange("(fo p) t -> fo p t", p=128)
        ecnt = [0]

        def epi(fo, j, pmap, pkey):
            js = slice(j * tl.tt, (j + 1) * tl.tt)
            i = ecnt[0] % 3
            ecnt[0] += 1
            ek = ("eo", i)
            p.emit("vector", lambda e: e.tensor_scalar_add(
                out=eo[i][:, :], in0=pmap, scalar1=bsb[:, fo:fo + 1]),
                reads=[pkey, ("b", 0)], writes=[ek])
            p.dma("gpsimd", Ore[fo][:, js], eo[i][:, :], reads=[ek])

        tl.emit_mm(ybf, "y", 8, wsb, "w", 3 * D, epi)
    return prog





def mk_l0_ffn(T=TPC, dbg=False):
    E, HH = 16, 512

    def prog(nc, p, es):
        X = nc.dram_tensor("x_fm", [D, T], F32, kind="ExternalInput").ap()
        WGT = nc.dram_tensor("wg_t", [D, E], F32, kind="ExternalInput").ap()
        W1 = nc.dram_tensor("w1", [E, D, 2 * HH], F32, kind="ExternalInput").ap()
        W2 = nc.dram_tensor("w2", [E, HH, D], F32, kind="ExternalInput").ap()
        WS1 = nc.dram_tensor("ws1", [D, 2048], F32, kind="ExternalInput").ap()
        WS2 = nc.dram_tensor("ws2", [1024, D], F32, kind="ExternalInput").ap()
        IDF = nc.dram_tensor("ident_f32", [128, 128], F32, kind="ExternalInput").ap()
        O = nc.dram_tensor("xout_fm", [D, T], F32, kind="ExternalOutput").ap()
        if dbg:
            OCW = nc.dram_tensor("cw_dbg", [128, T // 128, E], F32, kind="ExternalOutput").ap()

        tl = TL(nc, p, es, T=T)
        sb, ps = tl.sb, tl.ps
        njt = tl.njt
        nsub = T // 128

        idsb = sb("idsb", [128, 128], F32)
        p.dma("sync", idsb[:, :], IDF, writes=[("id",)])
        wgt = sb("wgt", [128, 8, E], F32)
        p.dma("sync", wgt[:, :, :], WGT.rearrange("(c p) e -> p c e", p=128),
              writes=[("wgt",)])
        ybf = tl.sbuf_fm("y", D, T, BF16)
        zbf = tl.sbuf_fm("z", D, T, BF16)
        cw_all = sb("cw_all", [128, nsub, E], F32)
        lg = sb("lg", [16, 512], F32)
        gs = [sb(f"gs{i}", [128, E], F32) for i in range(5)]
        gr = [sb(f"gr{i}", [128, 1], F32) for i in range(4)]

        def gate_tile(j, y32t):
            plg = tl.pmm[0]
            for c in range(8):
                p.emit("tensor", lambda e, c=c: e.matmul(
                    out=plg[0:E, :], lhsT=wgt[:, c, :], rhs=y32t[:, c, :],
                    start=(c == 0), stop=(c == 7)),
                    reads=[("wgt",), ("y32", c)], writes=[("pmm", 0)])
            p.emit("vector", lambda e: e.tensor_copy(out=lg[:, :], in_=plg[0:E, :]),
                   reads=[("pmm", 0)], writes=[("lg",)])
            for i in range(4):
                ti = j * 4 + i
                ltm, ptm, g1, l2, cwtm = gs
                m1, nm1, ssum, m2 = gr
                ptf = tl.pmm[1]
                p.emit("tensor", lambda e, i=i: e.transpose(
                    out=ptf[0:128, 0:E], in_=lg[:, i * 128:(i + 1) * 128], identity=idsb[0:E, 0:E]),
                    reads=[("lg",), ("id",)], writes=[("pmm", 1)])
                p.emit("vector", lambda e: e.tensor_copy(out=ltm[:, :], in_=ptf[0:128, 0:E]),
                       reads=[("pmm", 1)], writes=[("ltm",)])
                p.emit("vector", lambda e: e.reduce_max(out=m1[:, :], in_=ltm[:, :], axis=AX.X),
                       reads=[("ltm",)], writes=[("m1",)])
                p.emit("vector", lambda e: e.tensor_scalar_mul(out=nm1[:, :], in0=m1[:, :], scalar1=-1.0),
                       reads=[("m1",)], writes=[("nm1",)])
                p.emit("scalar", lambda e: e.activation(out=ptm[:, :], in_=ltm[:, :], func=AF.Exp,
                                                        bias=nm1[:, 0:1], accum_out=ssum[:, 0:1]),
                       reads=[("ltm",), ("nm1",)], writes=[("ptm",), ("ssum",)])
                p.emit("vector", lambda e: e.reciprocal(out=ssum[:, :], in_=ssum[:, :]),
                       reads=[("ssum",)], writes=[("ssum",)])
                p.emit("vector", lambda e: e.tensor_scalar_mul(out=ptm[:, :], in0=ptm[:, :], scalar1=ssum[:, 0:1]),
                       reads=[("ptm",), ("ssum",)], writes=[("ptm",)])
                p.emit("vector", lambda e: e.tensor_scalar(out=g1[:, :], in0=ltm[:, :], scalar1=m1[:, 0:1],
                                                           scalar2=None, op0=ALU.is_ge),
                       reads=[("ltm",), ("m1",)], writes=[("g1",)])
                p.emit("vector", lambda e: e.scalar_tensor_tensor(out=l2[:, :], in0=g1[:, :], scalar=-1e30,
                                                                  in1=ltm[:, :], op0=ALU.mult, op1=ALU.add),
                       reads=[("g1",), ("ltm",)], writes=[("l2",)])
                p.emit("vector", lambda e: e.reduce_max(out=m2[:, :], in_=l2[:, :], axis=AX.X),
                       reads=[("l2",)], writes=[("m2",)])
                p.emit("vector", lambda e: e.tensor_scalar(out=g1[:, :], in0=ltm[:, :], scalar1=m2[:, 0:1],
                                                           scalar2=None, op0=ALU.is_ge),
                       reads=[("ltm",), ("m2",)], writes=[("g1",)])
                p.emit("vector", lambda e, ti=ti: e.tensor_mul(out=cw_all[:, ti, :], in0=ptm[:, :], in1=g1[:, :]),
                       reads=[("ptm",), ("g1",)], writes=[("cw", ti)])

        with contextlib.ExitStack() as es2:
            tl2 = contextlib.nullcontext()
            xsb = es2.enter_context(nc.sbuf_tensor("x", [128, 8, T], F32))
            p.dma("sync", xsb[:, :, :], X.rearrange("(c p) t -> p c t", p=128),
                  writes=[("x", c) for c in range(8)])
            y32t = es2.enter_context(nc.sbuf_tensor("y32t", [128, 8, 512], F32))
            tl.emit_ln(xsb, "x", 8, ybf, "y", y32_tile=y32t, y32name="y32",
                       tile_cb=lambda j: gate_tile(j, y32t))
        p.barrier()
        if dbg:
            p.dma("gpsimd", OCW, cw_all[:, :, :], reads=[("cw", ti) for ti in range(nsub)])

        h = [tl.sbuf_fm(f"h{i}", HH, T, BF16) for i in range(2)]
        stg = [sb(f"stgw{i}", [128, 2048], F32) for i in range(2)]
        w1b = [sb(f"w1b{i}", [128, 8, 2 * HH], BF16) for i in range(2)]
        w2b = [sb(f"w2b{i}", [128, 4, D], BF16) for i in range(2)]
        hs = sb("hs", [128, 8, 512], BF16)
        cwb = [sb(f"cwb{i}", [128, 512], BF16) for i in range(2)]
        sg = [sb(f"sg{i}", [128, 512], F32) for i in range(2)]
        hwt = [sb(f"hwt{i}", [128, 512], BF16) for i in range(2)]
        nst = [0]

        def load_cast(dram_src, dst, dstkey, nchunk, ncols):
            for c in range(nchunk):
                s = stg[nst[0] % 2]
                skey = ("stgw", nst[0] % 2)
                nst[0] += 1
                p.dma("sync", s[:, 0:ncols], dram_src[c], writes=[skey])
                p.emit("vector", lambda e, s=s, c=c, ncols=ncols, dst=dst: e.tensor_copy(
                    out=dst[:, c, 0:ncols], in_=s[:, 0:ncols]),
                    reads=[skey], writes=[(dstkey, c)])

        def up_and_down(eidx, w1cur, w1key, w2cur, w2key, hcur, hkey, weighted, nuc, first):
            for j in range(njt):
                js = slice(j * 512, (j + 1) * 512)
                if weighted:
                    pcw = tl.pbS
                    for i in range(4):
                        ti = j * 4 + i
                        p.emit("tensor", lambda e, ti=ti, i=i, eidx=eidx: e.transpose(
                            out=pcw[:, i * 128:(i + 1) * 128],
                            in_=cw_all[:, ti, eidx:eidx + 1].broadcast_to([128, 128]),
                            identity=idsb[:, :]),
                            reads=[("cw", ti), ("id",)], writes=[("pbS",)])
                    cb = cwb[j % 2]
                    cbkey = ("cwb", j % 2)
                    p.emit("vector", lambda e, cb=cb: e.tensor_copy(out=cb[:, :], in_=pcw[:, :]),
                           reads=[("pbS",)], writes=[cbkey])
                for blk in range(nuc):
                    pu, pukey = tl.pmm[blk % 2], ("pmm", blk % 2)
                    pg, pgkey = tl.pmm[2 + blk % 2], ("pmm", 2 + blk % 2)
                    for c in range(8):
                        p.emit("tensor", lambda e, c=c, blk=blk, js=js, w1cur=w1cur, pu=pu: e.matmul(
                            out=pu[:, :], lhsT=w1cur[:, c, blk * 128:(blk + 1) * 128],
                            rhs=ybf[:, c, js], start=(c == 0), stop=(c == 7)),
                            reads=[(w1key, c), ("y", c)], writes=[pukey])
                    for c in range(8):
                        p.emit("tensor", lambda e, c=c, blk=blk, js=js, w1cur=w1cur, pg=pg, nuc=nuc: e.matmul(
                            out=pg[:, :], lhsT=w1cur[:, c, (nuc + blk) * 128:(nuc + blk + 1) * 128],
                            rhs=ybf[:, c, js], start=(c == 0), stop=(c == 7)),
                            reads=[(w1key, c), ("y", c)], writes=[pgkey])
                    sgc, sgkey = sg[blk % 2], ("sg", blk % 2)
                    p.emit("scalar", lambda e, sgc=sgc, pg=pg: e.activation(out=sgc[:, :], in_=pg[:, :], func=AF.Silu),
                           reads=[pgkey], writes=[sgkey])
                    if weighted:
                        ht, htkey = hwt[blk % 2], ("hwt", blk % 2)
                        p.emit("vector", lambda e, ht=ht, sgc=sgc, pu=pu: e.tensor_mul(out=ht[:, :], in0=pu[:, :], in1=sgc[:, :]),
                               reads=[pukey, sgkey], writes=[htkey])
                        p.emit("gpsimd", lambda e, ht=ht, blk=blk, js=js, hcur=hcur, cb=cb: e.tensor_mul(
                            out=hcur[:, blk, js], in0=ht[:, :], in1=cb[:, :]),
                            reads=[htkey, cbkey], writes=[(hkey, blk, j)])
                    else:
                        p.emit("vector", lambda e, blk=blk, js=js, hcur=hcur, sgc=sgc, pu=pu: e.tensor_mul(
                            out=hcur[:, blk, js], in0=pu[:, :], in1=sgc[:, :]),
                               reads=[pukey, sgkey], writes=[(hkey, blk, j)])
                for fo in range(8):
                    pz, pzkey = tl.pmm[(fo % 4)], ("pmm", fo % 4)
                    for blk in range(nuc):
                        p.emit("tensor", lambda e, blk=blk, fo=fo, js=js, w2cur=w2cur, hcur=hcur, pz=pz, j=j: e.matmul(
                            out=pz[:, :], lhsT=w2cur[:, blk, fo * 128:(fo + 1) * 128],
                            rhs=hcur[:, blk, js], start=(blk == 0), stop=(blk == nuc - 1)),
                            reads=[(w2key, blk), (hkey, blk, j)], writes=[pzkey])
                    if first:
                        p.emit("vector", lambda e, fo=fo, js=js, pz=pz: e.tensor_copy(out=zbf[:, fo, js], in_=pz[:, :]),
                               reads=[pzkey], writes=[("z", fo, j)])
                    else:
                        p.emit("vector", lambda e, fo=fo, js=js, pz=pz: e.tensor_add(out=zbf[:, fo, js], in0=zbf[:, fo, js], in1=pz[:, :]),
                               reads=[pzkey, ("z", fo, j)], writes=[("z", fo, j)])

        for eidx in range(E):
            b = eidx % 2
            load_cast(W1[eidx].rearrange("(c p) f -> c p f", p=128), w1b[b], f"w1b{b}", 8, 2 * HH)
            load_cast(W2[eidx].rearrange("(c p) f -> c p f", p=128), w2b[b], f"w2b{b}", 4, D)
            up_and_down(eidx, w1b[b], f"w1b{b}", w2b[b], f"w2b{b}", h[b], f"h{b}",
                        weighted=True, nuc=4, first=(eidx == 0))

        load_cast(WS1[:, 0:1024].rearrange("(c p) f -> c p f", p=128), w1b[0], "w1b0", 8, 1024)
        load_cast(WS1[:, 1024:2048].rearrange("(c p) f -> c p f", p=128), w1b[1], "w1b1", 8, 1024)
        load_cast(WS2[0:512].rearrange("(c p) f -> c p f", p=128), w2b[0], "w2b0", 4, D)
        load_cast(WS2[512:1024].rearrange("(c p) f -> c p f", p=128), w2b[1], "w2b1", 4, D)
        for j in range(njt):
            js = slice(j * 512, (j + 1) * 512)
            for blk in range(8):
                pu, pukey = tl.pmm[blk % 2], ("pmm", blk % 2)
                pg, pgkey = tl.pmm[2 + blk % 2], ("pmm", 2 + blk % 2)
                for c in range(8):
                    p.emit("tensor", lambda e, c=c, blk=blk, js=js, pu=pu: e.matmul(
                        out=pu[:, :], lhsT=w1b[0][:, c, blk * 128:(blk + 1) * 128],
                        rhs=ybf[:, c, js], start=(c == 0), stop=(c == 7)),
                        reads=[("w1b0", c), ("y", c)], writes=[pukey])
                for c in range(8):
                    p.emit("tensor", lambda e, c=c, blk=blk, js=js, pg=pg: e.matmul(
                        out=pg[:, :], lhsT=w1b[1][:, c, blk * 128:(blk + 1) * 128],
                        rhs=ybf[:, c, js], start=(c == 0), stop=(c == 7)),
                        reads=[("w1b1", c), ("y", c)], writes=[pgkey])
                sgc, sgkey = sg[blk % 2], ("sg", blk % 2)
                p.emit("scalar", lambda e, sgc=sgc, pg=pg: e.activation(out=sgc[:, :], in_=pg[:, :], func=AF.Silu),
                       reads=[pgkey], writes=[sgkey])
                p.emit("vector", lambda e, blk=blk, sgc=sgc, pu=pu: e.tensor_mul(out=hs[:, blk, :], in0=pu[:, :], in1=sgc[:, :]),
                       reads=[pukey, sgkey], writes=[("hs", blk)])
            for fo in range(8):
                pz, pzkey = tl.pmm[fo % 4], ("pmm", fo % 4)
                for blk in range(8):
                    wsel, wselkey = (w2b[0], "w2b0") if blk < 4 else (w2b[1], "w2b1")
                    p.emit("tensor", lambda e, blk=blk, fo=fo, wsel=wsel, pz=pz: e.matmul(
                        out=pz[:, :], lhsT=wsel[:, blk % 4, fo * 128:(fo + 1) * 128],
                        rhs=hs[:, blk, :], start=(blk == 0), stop=(blk == 7)),
                        reads=[(wselkey, blk % 4), ("hs", blk)], writes=[pzkey])
                p.emit("vector", lambda e, fo=fo, js=js, pz=pz: e.tensor_add(out=zbf[:, fo, js], in0=zbf[:, fo, js], in1=pz[:, :]),
                       reads=[pzkey, ("z", fo, j)], writes=[("z", fo, j)])

        xc = [sb(f"xc{i}", [128, T], F32) for i in range(2)]
        Xre = X.rearrange("(c p) t -> c p t", p=128)
        Ore = O.rearrange("(c p) t -> c p t", p=128)
        for fo in range(8):
            b = fo % 2
            xck = ("xc", b)
            p.dma("sync", xc[b][:, :], Xre[fo], writes=[xck])
            p.emit("vector", lambda e, fo=fo, b=b: e.tensor_add(
                out=xc[b][:, :], in0=xc[b][:, :], in1=zbf[:, fo, :]),
                reads=[xck] + [("z", fo, j) for j in range(njt)], writes=[xck])
            p.dma("gpsimd", Ore[fo], xc[b][:, :], reads=[xck])
    return prog


def mk_flash(S_=8192, npair=4, dv=64, alibi=True, krange=None, maskid=None, nmasks=4,
             Sk=None, dbg=False):
    if Sk is None:
        Sk = S_
    NKT = Sk // 128
    NQB = S_ // 512
    NKB = Sk // 512
    if krange is None:
        krange = lambda qt: range(0, qt * 4 + 4)
    if maskid is None:
        maskid = lambda qt, kt: (kt - 4 * qt) if kt - 4 * qt >= 0 else None

    def prog(nc, p, es):
        Q = nc.dram_tensor("qp", [npair, 64, S_], BF16, kind="ExternalInput").ap()
        K_ = nc.dram_tensor("kp", [npair, 64, Sk], BF16, kind="ExternalInput").ap()
        V = nc.dram_tensor("vp", [npair, 128, Sk // 128, dv], BF16, kind="ExternalInput").ap()
        RAMP = nc.dram_tensor("ramp", [npair, 128, NKT], F32, kind="ExternalInput").ap()
        AUGQ = nc.dram_tensor("augq", [npair, 1, S_], F32, kind="ExternalInput").ap()
        MSK = nc.dram_tensor("masks", [nmasks, 128, 512], F32, kind="ExternalInput").ap()
        O = nc.dram_tensor("attnout", [npair, dv, S_], BF16, kind="ExternalOutput").ap()

        sb = lambda n, s, d: es.enter_context(nc.sbuf_tensor(n, s, d))
        ps = lambda n, s, d: es.enter_context(nc.psum_tensor(n, s, d))
        onesr = sb("onesr", [1, 128], F32)
        onesc = sb("onesc", [128, 1], F32)
        p.emit("vector", lambda e: e.memset(onesr[:, :], 1.0), writes=[("onesr",)])
        p.emit("vector", lambda e: e.memset(onesc[:, :], 1.0), writes=[("onesc",)])
        msk = sb("msk", [128, nmasks, 512], F32)
        p.dma("sync", msk[:, :, :], MSK.rearrange("m p q -> p m q"), writes=[("msk",)])

        qh = [sb(f"qh{i}", [65, S_], BF16) for i in range(2)]
        kh = [sb(f"kh{i}", [65, Sk], BF16) for i in range(2)]
        vt = [sb(f"vt{i}", [128, NKT, dv + 1], BF16) for i in range(2)]
        rmp = [sb(f"rmp{i}", [128, NKT], F32) for i in range(2)]
        qr = sb("qr", [1, 512], F32)
        kr = sb("kr", [1, 1], F32)
        sqt = sb("sqt", [64, 512], F32)
        kms = sb("kms", [1, 1], F32)
        outsb = [sb(f"outsb{i}", [dv, 512], BF16) for i in range(2)]
        rr = [sb(f"rr{i}", [1, 512], F32) for i in range(2)]
        rrb = [sb(f"rrb{i}", [dv, 512], F32) for i in range(2)]
        aqt = sb("aqt", [1, 512], F32)
        NPA = 4
        psA = [ps(f"psA{i}", [128, 512], F32) for i in range(NPA)]
        psO = [ps(f"psO{i}", [dv + 1, 512], F32) for i in range(2)]
        psR = ps("psR", [1, 512], F32)
        psB = ps("psB", [dv, 512], F32)
        pT = [sb(f"pT{i}", [128, 512], BF16) for i in range(NPA)]

        def load_pair(pi, b):
            kq = ("qh", b)
            kk = ("kh", b)
            kv = ("vt", b)
            p.dma("sync", qh[b][0:64, :], Q[pi], writes=[kq])
            p.dma("sync", kh[b][0:64, :], K_[pi], writes=[kk])
            p.dma("sync", vt[b][:, :, 0:dv], V[pi], writes=[kv])
            p.dma("sync", rmp[b][:, :], RAMP[pi], writes=[("rmp", b)])
            p.emit("vector", lambda e, b=b: e.memset(vt[b][:, :, dv:dv + 1], 1.0),
                   reads=[kv], writes=[kv])
            p.emit("vector", lambda e, b=b: e.memset(kh[b][64:65, :], 1.0),
                   reads=[kk], writes=[kk])
            p.emit("vector", lambda e, b=b: e.tensor_scalar_mul(out=qh[b][0:64, :], in0=qh[b][0:64, :], scalar1=0.125),
                   reads=[kq], writes=[kq])
            for t in range(NKB):
                js = slice(t * 512, (t + 1) * 512)
                p.emit("scalar", lambda e, b=b, js=js: e.activation(out=sqt[:, :], in_=kh[b][0:64, js], func=AF.Square),
                       reads=[kk], writes=[("sqt",)])
                p.emit("tensor", lambda e: e.matmul(out=psR[:, :], lhsT=onesc[0:64, :], rhs=sqt[:, :],
                                                    start=True, stop=True),
                       reads=[("sqt",), ("onesc",)], writes=[("psR",)])
                p.emit("vector", lambda e: e.reduce_max(out=kr[:, :], in_=psR[:, :], axis=AX.X),
                       reads=[("psR",)], writes=[("kr",)])
                if t == 0:
                    p.emit("vector", lambda e: e.tensor_copy(out=kms[:, :], in_=kr[:, :]),
                           reads=[("kr",)], writes=[("kms",)])
                else:
                    p.emit("vector", lambda e: e.tensor_max(out=kms[:, :], in0=kms[:, :], in1=kr[:, :]),
                           reads=[("kr",), ("kms",)], writes=[("kms",)])
            p.emit("scalar", lambda e: e.activation(out=kms[:, :], in_=kms[:, :], func=AF.Sqrt),
                   reads=[("kms",)], writes=[("kms",)])
            for t in range(NQB):
                js = slice(t * 512, (t + 1) * 512)
                p.emit("scalar", lambda e, b=b, js=js: e.activation(out=sqt[:, :], in_=qh[b][0:64, js], func=AF.Square),
                       reads=[kq], writes=[("sqt",)])
                p.emit("tensor", lambda e: e.matmul(out=psR[:, :], lhsT=onesc[0:64, :], rhs=sqt[:, :],
                                                    start=True, stop=True),
                       reads=[("sqt",), ("onesc",)], writes=[("psR",)])
                p.emit("scalar", lambda e: e.activation(out=qr[:, :], in_=psR[:, :], func=AF.Sqrt),
                       reads=[("psR",)], writes=[("qr",)])
                p.emit("vector", lambda e: e.tensor_scalar_mul(out=qr[:, :], in0=qr[:, :], scalar1=kms[0:1, 0:1]),
                       reads=[("qr",), ("kms",)], writes=[("qr",)])
                p.dma("sync", aqt[:, :], AUGQ[pi][:, js], writes=[("aqt",)])
                p.emit("vector", lambda e, b=b, js=js: e.tensor_sub(out=qh[b][64:65, js], in0=aqt[:, :], in1=qr[:, :]),
                       reads=[("aqt",), ("qr",)], writes=[kq])

        pending = [None]

        def do_pair(pi, b):
            kq, kk, kv = ("qh", b), ("kh", b), ("vt", b)
            LA = 2
            for qt in range(NQB):
                js = slice(qt * 512, (qt + 1) * 512)
                kts = list(krange(qt))
                n = len(kts)
                ob = qt % 2

                def emit_qk(i, qt=qt, js=js, kts=kts):
                    kt = kts[i]
                    ab = i % NPA
                    pa, pakey = psA[ab], ("psA", ab)
                    p.emit("tensor", lambda e, kt=kt, js=js, b=b, pa=pa: e.matmul(
                        out=pa[:, :], lhsT=kh[b][:, kt * 128:(kt + 1) * 128], rhs=qh[b][:, js],
                        start=True, stop=True),
                        reads=[kk, kq], writes=[pakey])
                    mid = maskid(qt, kt)
                    if mid is not None:
                        p.emit("vector", lambda e, mid=mid, pa=pa: e.tensor_add(
                            out=pa[:, :], in0=pa[:, :], in1=msk[:, mid, :]),
                            reads=[pakey, ("msk",)], writes=[pakey])
                    pt, ptkey = pT[ab], ("pT", ab)
                    p.emit("scalar", lambda e, kt=kt, b=b, pa=pa, pt=pt: e.activation(
                        out=pt[:, :], in_=pa[:, :], func=AF.Exp, bias=rmp[b][:, kt:kt + 1]),
                        reads=[pakey, ("rmp", b)], writes=[ptkey])

                def emit_av(i, n=n, ob=ob, kts=kts):
                    kt = kts[i]
                    ab = i % NPA
                    pt, ptkey = pT[ab], ("pT", ab)
                    p.emit("tensor", lambda e, kt=kt, b=b, pt=pt, first=(i == 0), last=(i == n - 1), ob=ob: e.matmul(
                        out=psO[ob][:, :], lhsT=vt[b][:, kt, :], rhs=pt[:, :],
                        start=first, stop=last),
                        reads=[kv, ptkey], writes=[("psO", ob)])

                for i in range(n):
                    emit_qk(i)
                    if i == LA and pending[0] is not None:
                        pending[0]()
                        pending[0] = None
                    if i >= LA:
                        emit_av(i - LA)
                if pending[0] is not None:
                    pending[0]()
                    pending[0] = None
                for i in range(max(0, n - LA), n):
                    emit_av(i)

                def epi(pi=pi, qt=qt, ob=ob):
                    p.emit("scalar", lambda e: e.activation(out=rr[ob][:, :], in_=psO[ob][dv:dv + 1, :],
                                                            func=AF.Ln),
                           reads=[("psO", ob)], writes=[("rr", ob)])
                    p.emit("tensor", lambda e: e.matmul(out=psB[:, :], lhsT=onesr[:, 0:dv], rhs=rr[ob][:, :],
                                                        start=True, stop=True),
                           reads=[("rr", ob), ("onesr",)], writes=[("psB",)])
                    p.emit("scalar", lambda e: e.activation(out=rrb[ob][:, :], in_=psB[:, :],
                                                            func=AF.Exp, scale=-1.0),
                           reads=[("psB",)], writes=[("rrb", ob)])
                    p.emit("vector", lambda e: e.tensor_mul(
                        out=outsb[ob][:, :], in0=psO[ob][0:dv, :], in1=rrb[ob][:, :]),
                        reads=[("psO", ob), ("rrb", ob)], writes=[("outsb", ob)])
                    p.dma("gpsimd", O[pi][:, qt * 512:(qt + 1) * 512], outsb[ob][:, :],
                          reads=[("outsb", ob)])
                pending[0] = epi

        load_pair(0, 0)
        if npair > 1:
            load_pair(1, 1)
        for pi in range(npair):
            do_pair(pi, pi % 2)
            if pi + 2 < npair:
                load_pair(pi + 2, (pi + 2) % 2)
        if pending[0] is not None:
            pending[0]()
            pending[0] = None
    return prog


def mk_resid_proj(T=TPC, din=1024, use_bias=True):
    nck = din // 128

    def prog(nc, p, es):
        X = nc.dram_tensor("x_fm", [D, T], F32, kind="ExternalInput").ap()
        A = nc.dram_tensor("a_fm", [din, T], BF16, kind="ExternalInput").ap()
        W = nc.dram_tensor("w", [din, D], F32, kind="ExternalInput").ap()
        if use_bias:
            Bv = nc.dram_tensor("b", [128, D // 128], F32, kind="ExternalInput").ap()
        O = nc.dram_tensor("xout_fm", [D, T], F32, kind="ExternalOutput").ap()
        tl = TL(nc, p, es, T=T)
        xsb = tl.load_fm("x", X, D, T, F32)
        asb = tl.load_fm("a", A, din, T, BF16)
        wsb = tl.load_weight_bf("w", W, din, D)
        bsb = tl.load_rows_f32("bo", Bv, D) if use_bias else None
        ost = [tl.sb(f"ost{i}", [128, 512], F32) for i in range(3)]
        cnt = [0]
        Ore = O.rearrange("(c p) t -> c p t", p=128)

        def epi(fo, j, pmap, pkey):
            js = slice(j * tl.tt, (j + 1) * tl.tt)
            i = cnt[0] % 3
            cnt[0] += 1
            ok = ("ost", i)
            if use_bias:
                p.emit("vector", lambda e, i=i, fo=fo: e.tensor_scalar_add(
                    out=ost[i][:, :], in0=pmap, scalar1=bsb[:, fo:fo + 1]),
                    reads=[pkey, ("bo", 0)], writes=[ok])
                p.emit("vector", lambda e, i=i, fo=fo, js=js: e.tensor_add(
                    out=ost[i][:, :], in0=ost[i][:, :], in1=xsb[:, fo, js]),
                    reads=[ok, ("x", fo)], writes=[ok])
            else:
                p.emit("vector", lambda e, i=i, fo=fo, js=js: e.tensor_add(
                    out=ost[i][:, :], in0=pmap, in1=xsb[:, fo, js]),
                    reads=[pkey, ("x", fo)], writes=[ok])
            p.dma("gpsimd", Ore[fo][:, js], ost[i][:, :], reads=[ok])

        tl.emit_mm(asb, "a", nck, wsb, "w", D, epi)
    return prog


def mk_l0b_scan(S_=8192, npair=4, dbg=False):
    C = 128
    NCH = S_ // C
    P4 = npair

    def prog(nc, p, es):
        Q = nc.dram_tensor("qs", [P4, 64, S_], F32, kind="ExternalInput").ap()
        K_ = nc.dram_tensor("ks", [P4, 64, S_], F32, kind="ExternalInput").ap()
        V = nc.dram_tensor("vs", [P4, 64, S_], F32, kind="ExternalInput").ap()
        AZ = nc.dram_tensor("az", [P4, 1, S_], F32, kind="ExternalInput").ap()
        BZ = nc.dram_tensor("bz", [P4, 1, S_], F32, kind="ExternalInput").ap()
        MTS = nc.dram_tensor("mtris", [128, 512], F32, kind="ExternalInput").ap()
        MTU = nc.dram_tensor("mtriui", [128, 512], F32, kind="ExternalInput").ap()
        IDF = nc.dram_tensor("ident_f32", [128, 128], F32, kind="ExternalInput").ap()
        O = nc.dram_tensor("out_tm", [P4, NCH, 128, 64], BF16, kind="ExternalOutput").ap()

        sb = lambda n, s, d: es.enter_context(nc.sbuf_tensor(n, s, d))
        ps = lambda n, s, d: es.enter_context(nc.psum_tensor(n, s, d))
        dbl = lambda n, s, d: [sb(n + "0", s, d), sb(n + "1", s, d)]
        onesr = sb("onesr", [1, 128], F32)
        p.emit("vector", lambda e: e.memset(onesr[:, :], 1.0), writes=[("onesr",)])
        onesc = sb("onesc", [128, 1], F32)
        p.emit("vector", lambda e: e.memset(onesc[:, :], 1.0), writes=[("onesc",)])
        zrow = sb("zrow", [P4, 128], F32)
        p.emit("vector", lambda e: e.memset(zrow[:, :], 0.0), writes=[("zrow",)])
        idsb = sb("idsb", [128, 128], F32)
        p.dma("sync", idsb[:, :], IDF, writes=[("id",)])
        mts = sb("mts", [128, 512], F32)
        p.dma("sync", mts[:, :], MTS, writes=[("mts",)])
        mtu = sb("mtu", [128, 512], F32)
        p.dma("sync", mtu[:, :], MTU, writes=[("mtu",)])
        azc = dbl("azc", [P4, 128], F32)
        bzc = dbl("bzc", [P4, 128], F32)
        AZr = AZ.rearrange("p one s -> (p one) s")
        BZr = BZ.rearrange("p one s -> (p one) s")

        kc = dbl("kc", [64, P4, 128], F32)
        qc = dbl("qc", [64, P4, 128], F32)
        vc = dbl("vc", [64, P4, 128], F32)
        kcn = dbl("kcn", [64, P4, 128], F32)
        qcn = dbl("qcn", [64, P4, 128], F32)
        sqs = dbl("sqs", [64, P4 * 128], F32)
        nrm = dbl("nrm", [1, P4 * 128], F32)
        rbc = dbl("rbc", [64, P4 * 128], F32)
        l4 = dbl("l4", [P4, 128], F32)
        lna = dbl("lna", [P4, 128], F32)
        ldc = dbl("ldc", [P4, 128], F32)
        lcb = dbl("lcb", [P4, 128], F32)
        cols = dbl("cols", [128, 16], F32)
        colse = dbl("colse", [128, 16], F32)
        kk = dbl("kk", [128, P4, 128], F32)
        kq = dbl("kq", [128, P4, 128], F32)
        lb = dbl("lb", [128, P4, 128], F32)
        ldif = dbl("ldif", [128, P4, 128], F32)
        dm = dbl("dm", [128, P4, 128], F32)
        dmx = dbl("dmx", [128, P4, 128], F32)
        lw = dbl("lw", [128, P4, 128], F32)
        lu = dbl("lu", [128, P4, 128], F32)
        m2t = dbl("m2t", [128, P4, 128], F32)
        m3t = dbl("m3t", [128, P4, 128], F32)
        ktm = dbl("ktm", [128, P4, 64], F32)
        vtm = dbl("vtm", [128, P4, 64], F32)
        kbar = dbl("kbar", [128, P4, 64], F32)
        Pp = dbl("Pp", [128, 2 * P4, 128], F32)
        Tt = dbl("Tt", [128, 2 * P4, 128], F32)
        Xs = dbl("Xs", [128, 2 * P4, 64], F32)
        wts = dbl("wts", [64, P4, 128], F32)
        wg = dbl("wg", [128, P4, 64], F32)
        o1 = dbl("o1", [128, P4, 64], F32)
        g1t = dbl("g1t", [64, P4, 64], F32)
        gsb = sb("gsb", [64, P4, 64], F32)
        och = dbl("och", [128, P4, 64], BF16)
        T2_ = sb("T2", [128, 2 * P4, 128], F32)
        P2_ = sb("P2", [128, 2 * P4, 128], F32)
        T4_ = sb("T4", [128, 2 * P4, 128], F32)
        P4b_ = sb("P4b", [128, 2 * P4, 128], F32)
        T8_ = sb("T8", [128, 2 * P4, 128], F32)
        T2 = [T2_, T2_]; P2 = [P2_, P2_]; T4 = [T4_, T4_]
        P4b = [P4b_, P4b_]; T8 = [T8_, T8_]
        pA = [ps(f"pA{i}", [128, 512], F32) for i in range(4)]
        pB = [ps(f"pB{i}", [128, 512], F32) for i in range(2)]
        pN = [ps(f"pN{i}", [1, 512], F32) for i in range(2)]

        p.emit("vector", lambda e: e.memset(gsb[:, :, :], 0.0), writes=[("gsb",)])

        prot = [0]

        def bank():
            b = prot[0] % 4
            prot[0] += 1
            return pA[b], ("pA", b)

        def chunk(ci):
            bi = ci % 2
            cs = slice(ci * 128, (ci + 1) * 128)
            kck, qck, vck = ("kc", bi), ("qc", bi), ("vc", bi)
            kB, kNk = ("pB", bi), ("pN", bi)
            pBb, pNb = pB[bi], pN[bi]
            p.dma("sync", azc[bi][:, :], AZr[:, cs], writes=[("azc", bi)])
            p.dma("sync", bzc[bi][:, :], BZr[:, cs], writes=[("bzc", bi)])
            p.dma("sync", kc[bi][:, :, :], K_[:, :, cs].rearrange("p d c -> d p c"), writes=[kck])
            p.dma("sync", qc[bi][:, :, :], Q[:, :, cs].rearrange("p d c -> d p c"), writes=[qck])
            p.dma("sync", vc[bi][:, :, :], V[:, :, cs].rearrange("p d c -> d p c"), writes=[vck])
            for src, srk, dst, dsk in ((kc[bi], kck, kcn[bi], ("kcn", bi)), (qc[bi], qck, qcn[bi], ("qcn", bi))):
                p.emit("scalar", lambda e, src=src: e.activation(out=sqs[bi][:, :], in_=src[:, :, :], func=AF.Square),
                       reads=[srk], writes=[("sqs", bi)])
                for pi in range(P4):
                    pslc = slice(pi * 128, (pi + 1) * 128)
                    p.emit("tensor", lambda e, pslc=pslc: e.matmul(out=pNb[:, pslc], lhsT=onesc[0:64, :],
                                                                   rhs=sqs[bi][:, pslc], start=True, stop=True),
                           reads=[("sqs", bi), ("onesc",)], writes=[kNk])
                p.emit("scalar", lambda e: e.activation(out=nrm[bi][:, :], in_=pNb[:, :], func=AF.Sqrt),
                       reads=[kNk], writes=[("nrm", bi)])
                p.emit("vector", lambda e: e.tensor_scalar_max(out=nrm[bi][:, :], in0=nrm[bi][:, :], scalar1=1e-12),
                       reads=[("nrm", bi)], writes=[("nrm", bi)])
                p.emit("tensor", lambda e: e.matmul(out=pBb[0:64, :], lhsT=onesr[:, 0:64], rhs=nrm[bi][:, :],
                                                    start=True, stop=True),
                       reads=[("nrm", bi), ("onesr",)], writes=[kB])
                p.emit("vector", lambda e: e.reciprocal(out=rbc[bi][:, :], in_=pBb[0:64, :]),
                       reads=[kB], writes=[("rbc", bi)])
                p.emit("vector", lambda e, src=src, dst=dst: e.tensor_mul(
                    out=dst[:, :, :], in0=src[:, :, :], in1=rbc[bi][:, :].rearrange("d (p c) -> d p c", p=P4)),
                    reads=[srk, ("rbc", bi)], writes=[dsk])
            p.emit("scalar", lambda e: e.activation(out=lna[bi][:, :], in_=azc[bi][:, :], func=AF.Ln),
                   reads=[("azc", bi)], writes=[("lna", bi)])
            p.emit("vector", lambda e: e.tensor_tensor_scan(out=l4[bi][:, :], data0=lna[bi][:, :], data1=zrow[:, :],
                                                            initial=0.0, op0=ALU.add, op1=ALU.add),
                   reads=[("lna", bi), ("zrow",)], writes=[("l4", bi)])
            p.emit("vector", lambda e: e.tensor_scalar(out=ldc[bi][:, :], in0=l4[bi][:, :], scalar1=l4[bi][:, 127:128],
                                                       scalar2=-1.0, op0=ALU.subtract, op1=ALU.mult),
                   reads=[("l4", bi)], writes=[("ldc", bi)])
            p.emit("vector", lambda e: e.tensor_scalar(out=lcb[bi][:, :], in0=l4[bi][:, :], scalar1=0.0,
                                                       scalar2=l4[bi][:, 127:128], op0=ALU.mult, op1=ALU.add),
                   reads=[("l4", bi)], writes=[("lcb", bi)])
            for i, (rows, rk) in enumerate(((bzc[bi], ("bzc", bi)), (l4[bi], ("l4", bi)),
                                            (ldc[bi], ("ldc", bi)), (lcb[bi], ("lcb", bi)))):
                inap = rows[:, :]
                p.emit("tensor", lambda e, i=i, inap=inap: e.transpose(
                    out=pBb[:, i * 4:(i + 1) * 4], in_=inap, identity=idsb[0:P4, 0:P4]),
                    reads=[rk, ("id",)], writes=[kB])
            p.emit("vector", lambda e: e.tensor_copy(out=cols[bi][:, :], in_=pBb[:, 0:16]),
                   reads=[kB], writes=[("cols", bi)])
            p.emit("scalar", lambda e: e.activation(out=colse[bi][:, 4:16], in_=cols[bi][:, 4:16], func=AF.Exp),
                   reads=[("cols", bi)], writes=[("colse", bi)])
            for dst, dk_, lh, lk, rh, rk in ((kk[bi], ("kk", bi), kcn[bi], ("kcn", bi), kcn[bi], ("kcn", bi)),
                                             (kq[bi], ("kq", bi), kcn[bi], ("kcn", bi), qcn[bi], ("qcn", bi))):
                pa, pak = bank()
                for pi in range(P4):
                    p.emit("tensor", lambda e, pi=pi, lh=lh, rh=rh, pa=pa: e.matmul(
                        out=pa[:, pi * 128:(pi + 1) * 128], lhsT=lh[:, pi, :], rhs=rh[:, pi, :],
                        start=True, stop=True),
                        reads=[lk, rk], writes=[pak])
                p.emit("vector", lambda e, dst=dst, pa=pa: e.tensor_copy(out=dst[:, :, :], in_=pa[:, :]),
                       reads=[pak], writes=[dk_])
            pa, pak = bank()
            for pi in range(P4):
                p.emit("tensor", lambda e, pi=pi, pa=pa: e.transpose(
                    out=pa[:, pi * 128:(pi + 1) * 128],
                    in_=cols[bi][:, 4 + pi:5 + pi].broadcast_to([128, 128]), identity=idsb[:, :]),
                    reads=[("cols", bi), ("id",)], writes=[pak])
            p.emit("vector", lambda e, pa=pa: e.tensor_copy(out=lb[bi][:, :, :], in_=pa[:, :]),
                   reads=[pak], writes=[("lb", bi)])
            for pi in range(P4):
                p.emit("vector", lambda e, pi=pi: e.tensor_scalar(
                    out=ldif[bi][:, pi, :], in0=lb[bi][:, pi, :], scalar1=cols[bi][:, 4 + pi:5 + pi],
                    scalar2=None, op0=ALU.subtract),
                    reads=[("lb", bi), ("cols", bi)], writes=[("ldif", bi)])
            p.emit("vector", lambda e: e.tensor_scalar(out=dm[bi][:, :, :], in0=ldif[bi][:, :, :], scalar1=-1.0,
                                                       scalar2=0.0, op0=ALU.mult, op1=ALU.min),
                   reads=[("ldif", bi)], writes=[("dm", bi)])
            p.emit("scalar", lambda e: e.activation(out=dm[bi][:, :, :], in_=dm[bi][:, :, :], func=AF.Exp),
                   reads=[("dm", bi)], writes=[("dm", bi)])
            p.emit("vector", lambda e: e.tensor_scalar_min(out=dmx[bi][:, :, :], in0=ldif[bi][:, :, :], scalar1=0.0),
                   reads=[("ldif", bi)], writes=[("dmx", bi)])
            p.emit("scalar", lambda e: e.activation(out=dmx[bi][:, :, :], in_=dmx[bi][:, :, :], func=AF.Exp),
                   reads=[("dmx", bi)], writes=[("dmx", bi)])
            for pi in range(P4):
                p.emit("vector", lambda e, pi=pi: e.tensor_scalar_mul(
                    out=lw[bi][:, pi, :], in0=kk[bi][:, pi, :], scalar1=cols[bi][:, pi:pi + 1]),
                    reads=[("kk", bi), ("cols", bi)], writes=[("lw", bi)])
            p.emit("vector", lambda e: e.tensor_mul(out=lw[bi][:, :, :], in0=lw[bi][:, :, :],
                                                    in1=mts[:, :].rearrange("a (p c) -> a p c", p=P4)),
                   reads=[("lw", bi), ("mts",)], writes=[("lw", bi)])
            p.emit("vector", lambda e: e.tensor_mul(out=lu[bi][:, :, :], in0=lw[bi][:, :, :], in1=dm[bi][:, :, :]),
                   reads=[("lw", bi), ("dm", bi)], writes=[("lu", bi)])
            p.emit("vector", lambda e: e.tensor_mul(out=m2t[bi][:, :, :], in0=kq[bi][:, :, :],
                                                    in1=mtu[:, :].rearrange("a (p c) -> a p c", p=P4)),
                   reads=[("kq", bi), ("mtu",)], writes=[("m2t", bi)])
            p.emit("vector", lambda e: e.tensor_mul(out=m3t[bi][:, :, :], in0=m2t[bi][:, :, :], in1=dmx[bi][:, :, :]),
                   reads=[("m2t", bi), ("dmx", bi)], writes=[("m3t", bi)])
            pa, pak = bank()
            for pi in range(P4):
                p.emit("tensor", lambda e, pi=pi, pa=pa: e.transpose(
                    out=pa[:, pi * 64:(pi + 1) * 64], in_=kcn[bi][:, pi, :], identity=idsb[0:64, 0:64]),
                    reads=[("kcn", bi), ("id",)], writes=[pak])
            p.emit("vector", lambda e, pa=pa: e.tensor_copy(out=ktm[bi][:, :, :], in_=pa[:, 0:256]),
                   reads=[pak], writes=[("ktm", bi)])
            pa, pak = bank()
            for pi in range(P4):
                p.emit("tensor", lambda e, pi=pi, pa=pa: e.transpose(
                    out=pa[:, pi * 64:(pi + 1) * 64], in_=vc[bi][:, pi, :], identity=idsb[0:64, 0:64]),
                    reads=[vck, ("id",)], writes=[pak])
            p.emit("vector", lambda e, pa=pa: e.tensor_copy(out=vtm[bi][:, :, :], in_=pa[:, 0:256]),
                   reads=[pak], writes=[("vtm", bi)])
            for pi in range(P4):
                p.emit("vector", lambda e, pi=pi: e.tensor_scalar_mul(
                    out=Xs[bi][:, pi, :], in0=ktm[bi][:, pi, :], scalar1=cols[bi][:, pi:pi + 1]),
                    reads=[("ktm", bi), ("cols", bi)], writes=[("Xs", bi)])
                p.emit("vector", lambda e, pi=pi: e.tensor_scalar_mul(
                    out=Xs[bi][:, P4 + pi, :], in0=vtm[bi][:, pi, :], scalar1=cols[bi][:, pi:pi + 1]),
                    reads=[("vtm", bi), ("cols", bi)], writes=[("Xs", bi)])
                p.emit("vector", lambda e, pi=pi: e.tensor_scalar_mul(
                    out=kbar[bi][:, pi, :], in0=ktm[bi][:, pi, :], scalar1=colse[bi][:, 8 + pi:9 + pi]),
                    reads=[("ktm", bi), ("colse", bi)], writes=[("kbar", bi)])
            pa, pak = bank()
            for pi in range(P4):
                for si, lsrc, lk in ((0, lw[bi], ("lw", bi)), (1, lu[bi], ("lu", bi))):
                    p.emit("tensor", lambda e, pi=pi, si=si, lsrc=lsrc, pa=pa: e.transpose(
                        out=pa[:, 0:128], in_=lsrc[:, pi, :], identity=idsb[:, :]),
                        reads=[lk, ("id",)], writes=[pak])
                    p.emit("vector", lambda e, pi=pi, si=si, pa=pa: e.tensor_copy(
                        out=Tt[bi][:, si * P4 + pi, :], in_=pa[:, 0:128]),
                        reads=[pak], writes=[("Tt", bi)])
            p.emit("vector", lambda e: e.tensor_copy(out=Pp[bi][:, 0:P4, :], in_=lw[bi][:, :, :]),
                   reads=[("lw", bi)], writes=[("Pp", bi)])
            p.emit("vector", lambda e: e.tensor_copy(out=Pp[bi][:, P4:2 * P4, :], in_=lu[bi][:, :, :]),
                   reads=[("lu", bi)], writes=[("Pp", bi)])
            do_powers(ci, bi)

        def do_powers(ci, bi):
            seq = [((Pp[bi], ("Pp", bi)), (Tt[bi], ("Tt", bi)), (P2[bi], ("P2",)), (T2[bi], ("T2",)), True),
                   ((P2[bi], ("P2",)), (T2[bi], ("T2",)), (P4b[bi], ("P4b",)), (T4[bi], ("T4",)), True),
                   ((P4b[bi], ("P4b",)), (T4[bi], ("T4",)), (None, None), (T8[bi], ("T8",)), False)]
            for (Pn, Pk), (Tn, Tk), (Pd, Pdk), (Td, Tdk), needP in seq:
                if needP:
                    for half in range(2):
                        pa, pak = bank()
                        for i in range(P4):
                            m = half * P4 + i
                            p.emit("tensor", lambda e, m=m, Tn=Tn, Pn=Pn, i=i, pa=pa: e.matmul(
                                out=pa[:, i * 128:(i + 1) * 128], lhsT=Tn[:, m, :], rhs=Pn[:, m, :],
                                start=True, stop=True),
                                reads=[Tk, Pk], writes=[pak])
                        p.emit("vector", lambda e, half=half, pa=pa, Pd=Pd: e.tensor_copy(
                            out=Pd[:, half * P4:(half + 1) * P4, :], in_=pa[:, :]),
                            reads=[pak], writes=[Pdk])
                for half in range(2):
                    pa, pak = bank()
                    for i in range(P4):
                        m = half * P4 + i
                        p.emit("tensor", lambda e, m=m, Tn=Tn, Pn=Pn, i=i, pa=pa: e.matmul(
                            out=pa[:, i * 128:(i + 1) * 128], lhsT=Pn[:, m, :], rhs=Tn[:, m, :],
                            start=True, stop=True),
                            reads=[Tk, Pk], writes=[pak])
                    p.emit("vector", lambda e, half=half, pa=pa, Td=Td: e.tensor_copy(
                        out=Td[:, half * P4:(half + 1) * P4, :], in_=pa[:, :]),
                        reads=[pak], writes=[Tdk])
            for Tn, Tk, sign in ((T8[bi], ("T8",), 1.0), (T4[bi], ("T4",), 1.0),
                                 (T2[bi], ("T2",), 1.0), (Tt[bi], ("Tt", bi), -1.0)):
                pa, pak = bank()
                for m in range(2 * P4):
                    p.emit("tensor", lambda e, m=m, Tn=Tn, pa=pa: e.matmul(
                        out=pa[:, m * 64:(m + 1) * 64], lhsT=Tn[:, m, :], rhs=Xs[bi][:, m, :],
                        start=True, stop=True),
                        reads=[Tk, ("Xs", bi)], writes=[pak])
                pav = pa[:, :].rearrange("a (m c) -> a m c", m=2 * P4)
                if sign > 0:
                    p.emit("vector", lambda e, pav=pav: e.tensor_add(out=Xs[bi][:, :, :], in0=Xs[bi][:, :, :], in1=pav),
                           reads=[pak, ("Xs", bi)], writes=[("Xs", bi)])
                else:
                    p.emit("vector", lambda e, pav=pav: e.tensor_sub(out=Xs[bi][:, :, :], in0=Xs[bi][:, :, :], in1=pav),
                           reads=[pak, ("Xs", bi)], writes=[("Xs", bi)])
            finish_chunk(ci, bi)

        def finish_chunk(ci, bi):
            pa, pak = bank()
            for pi in range(P4):
                p.emit("tensor", lambda e, pi=pi, pa=pa: e.transpose(
                    out=pa[:, pi * 128:(pi + 1) * 128][0:64, :], in_=Xs[bi][:, pi, :], identity=idsb[:, :]),
                    reads=[("Xs", bi), ("id",)], writes=[pak])
            p.emit("vector", lambda e, pa=pa: e.tensor_copy(
                out=wts[bi][:, :, :], in_=pa[0:64, :].rearrange("d (p c) -> d p c", p=P4)),
                reads=[pak], writes=[("wts", bi)])
            pa, pak = bank()
            for pi in range(P4):
                p.emit("tensor", lambda e, pi=pi, pa=pa: e.matmul(
                    out=pa[:, pi * 64:(pi + 1) * 64], lhsT=wts[bi][:, pi, :], rhs=gsb[:, pi, :],
                    start=True, stop=True),
                    reads=[("wts", bi), ("gsb",)], writes=[pak])
            p.emit("vector", lambda e, pa=pa: e.tensor_scalar_mul(
                out=wg[bi][:, :, :], in0=pa[:, 0:256].rearrange("a (p c) -> a p c", p=P4), scalar1=-1.0),
                reads=[pak], writes=[("wg", bi)])
            pa, pak = bank()
            for pi in range(P4):
                p.emit("tensor", lambda e, pi=pi, pa=pa: e.matmul(
                    out=pa[:, pi * 64:(pi + 1) * 64], lhsT=qcn[bi][:, pi, :], rhs=gsb[:, pi, :],
                    start=True, stop=False),
                    reads=[("qcn", bi), ("gsb",)], writes=[pak])
                p.emit("tensor", lambda e, pi=pi, pa=pa: e.matmul(
                    out=pa[:, pi * 64:(pi + 1) * 64], lhsT=m2t[bi][:, pi, :], rhs=wg[bi][:, pi, :],
                    start=False, stop=True),
                    reads=[("m2t", bi), ("wg", bi)], writes=[pak])
            for pi in range(P4):
                p.emit("vector", lambda e, pi=pi, pa=pa: e.tensor_scalar_mul(
                    out=o1[bi][:, pi, :], in0=pa[:, pi * 64:(pi + 1) * 64], scalar1=colse[bi][:, 4 + pi:5 + pi]),
                    reads=[pak, ("colse", bi)], writes=[("o1", bi)])
            pa2, pak2 = bank()
            for pi in range(P4):
                p.emit("tensor", lambda e, pi=pi, pa2=pa2: e.matmul(
                    out=pa2[:, pi * 64:(pi + 1) * 64], lhsT=m3t[bi][:, pi, :], rhs=Xs[bi][:, P4 + pi, :],
                    start=True, stop=True),
                    reads=[("m3t", bi), ("Xs", bi)], writes=[pak2])
            p.emit("vector", lambda e, pa2=pa2: e.tensor_add(
                out=och[bi][:, :, :], in0=o1[bi][:, :, :], in1=pa2[:, 0:256].rearrange("a (p c) -> a p c", p=P4)),
                reads=[pak2, ("o1", bi)], writes=[("och", bi)])
            p.dma("gpsimd", O[:, ci, :, :].rearrange("p a d -> a p d"), och[bi][:, :, :], reads=[("och", bi)])
            pa, pak = bank()
            for pi in range(P4):
                p.emit("tensor", lambda e, pi=pi, pa=pa: e.matmul(
                    out=pa[0:64, pi * 64:(pi + 1) * 64], lhsT=ktm[bi][:, pi, :], rhs=wg[bi][:, pi, :],
                    start=True, stop=True),
                    reads=[("ktm", bi), ("wg", bi)], writes=[pak])
            p.emit("vector", lambda e, pa=pa: e.tensor_add(
                out=g1t[bi][:, :, :], in0=gsb[:, :, :], in1=pa[0:64, 0:256].rearrange("d (p c) -> d p c", p=P4)),
                reads=[pak, ("gsb",)], writes=[("g1t", bi)])
            pa2, pak2 = bank()
            for pi in range(P4):
                p.emit("tensor", lambda e, pi=pi, pa2=pa2: e.matmul(
                    out=pa2[0:64, pi * 64:(pi + 1) * 64], lhsT=kbar[bi][:, pi, :], rhs=Xs[bi][:, P4 + pi, :],
                    start=True, stop=True),
                    reads=[("kbar", bi), ("Xs", bi)], writes=[pak2])
            for pi in range(P4):
                p.emit("vector", lambda e, pi=pi, pa2=pa2: e.scalar_tensor_tensor(
                    out=gsb[:, pi, :], in0=g1t[bi][:, pi, :], scalar=colse[bi][0:64, 12 + pi:13 + pi],
                    in1=pa2[0:64, pi * 64:(pi + 1) * 64], op0=ALU.mult, op1=ALU.add),
                    reads=[("g1t", bi), ("colse", bi), pak2], writes=[("gsb",)])

        for ci in range(NCH):
            chunk(ci)
    return prog


def mk_lm1(T=TPC, dbg=False):
    def prog(nc, p, es):
        X = nc.dram_tensor("x_fm", [D, T], F32, kind="ExternalInput").ap()
        WQ = nc.dram_tensor("wq", [D, D], F32, kind="ExternalInput").ap()
        WKVA = nc.dram_tensor("wkva", [D, 160], F32, kind="ExternalInput").ap()
        WKVB = nc.dram_tensor("wkvb", [128, 512], F32, kind="ExternalInput").ap()
        WLN1 = nc.dram_tensor("wln1", [128, 8], F32, kind="ExternalInput").ap()
        COS = nc.dram_tensor("cosT", [128, T], F32, kind="ExternalInput").ap()
        SIN = nc.dram_tensor("sinT", [128, T], F32, kind="ExternalInput").ap()
        YO = nc.dram_tensor("y_out", [D, T], F32, kind="ExternalOutput").ap()
        QO = nc.dram_tensor("q_out", [D, T], BF16, kind="ExternalOutput").ap()
        KVO = nc.dram_tensor("kv_out", [512, T], BF16, kind="ExternalOutput").ap()
        KRO = nc.dram_tensor("krope_out", [32, T], BF16, kind="ExternalOutput").ap()

        tl = TL(nc, p, es, T=T)
        sb = tl.sb
        wq = tl.load_weight_bf("wq", WQ, D, D)
        wkva = tl.load_weight_bf("wkva", WKVA, D, 160)
        wkvbs = sb("wkvb_stg", [128, 512], F32)
        p.dma("sync", wkvbs[:, :], WKVB, writes=[("wkvb_stg",)])
        wkvb = sb("wkvb_bf", [128, 512], BF16)
        p.emit("vector", lambda e: e.tensor_copy(out=wkvb[:, :], in_=wkvbs[:, :]),
               reads=[("wkvb_stg",)], writes=[("wkvb",)])
        wln1 = tl.load_rows_f32("wln1c", WLN1, D)
        cosb = sb("cosb", [128, T], F32)
        p.dma("sync", cosb[:, :], COS, writes=[("cos",)])
        sinb = sb("sinb", [128, T], F32)
        p.dma("sync", sinb[:, :], SIN, writes=[("sin",)])

        xsb = tl.load_fm("x", X, D, T, F32)
        ybf = None
        y32t = sb("y32t", [128, 8, 512], F32)
        ttile = sb("ttile", [128, 8, 512], BF16)
        qsb = sb("qsb", [128, 8, 512], F32)
        qstg = sb("qstg", [128, 8, 512], BF16)
        lat = sb("lat", [128, 512], BF16)
        kvt = sb("kvt", [128, 4, 512], BF16)
        krl = sb("krl", [16, 512], BF16)
        krh = sb("krh", [16, 512], BF16)
        rs0 = sb("rs0", [16, 512], F32)
        rs1 = sb("rs1", [16, 512], F32)
        pRL = tl.pmm[0][0:16, :]
        pRH = tl.pmm[1][0:16, :]
        Yre = YO.rearrange("(c p) t -> p c t", p=128)
        Qre = QO.rearrange("(c p) t -> p c t", p=128)
        KVre = KVO.rearrange("(c p) t -> p c t", p=128)

        def tile_cb(j):
            js = slice(j * 512, (j + 1) * 512)
            p.dma("gpsimd", Yre[:, :, js], y32t[:, :, :], reads=[("y32", c) for c in range(8)])
            for c in range(8):
                sq = tl.sq0 if c % 2 == 0 else tl.sq1
                sqk = ("sq", c % 2)
                p.emit("scalar", lambda e, c=c, sq=sq: e.activation(out=sq[:, :], in_=y32t[:, c, :], func=AF.Square),
                       reads=[("y32", c)], writes=[sqk])
                p.emit("tensor", lambda e, c=c, sq=sq: e.matmul(out=tl.prq[:, :], lhsT=tl.ones_col[:], rhs=sq[:, :],
                                                                start=(c == 0), stop=(c == 7)),
                       reads=[sqk, ("ones_col", 0)], writes=[("prq",)])
            epsap = tl.const(1e-5)
            p.emit("vector", lambda e: e.tensor_scalar_mul(out=tl.r1[:, :], in0=tl.prq[:, :], scalar1=1.0 / D),
                   reads=[("prq",)], writes=[("r1",)])
            p.emit("scalar", lambda e, epsap=epsap: e.activation(out=tl.r1[:, :], in_=tl.r1[:, :], func=AF.Sqrt, bias=epsap[0:1, :]),
                   reads=[("r1",), ("const", 1e-5)], writes=[("r1",)])
            p.emit("vector", lambda e: e.reciprocal(out=tl.r1[:, :], in_=tl.r1[:, :]),
                   reads=[("r1",)], writes=[("r1",)])
            p.emit("tensor", lambda e: e.matmul(out=tl.pbS[:, :], lhsT=tl.ones_row[:], rhs=tl.r1[:, :],
                                                start=True, stop=True),
                   reads=[("r1",), ("ones_row", 0)], writes=[("pbS",)])
            for c in range(8):
                p.emit("vector", lambda e, c=c: e.scalar_tensor_tensor(
                    out=ttile[:, c, :], in0=y32t[:, c, :], scalar=wln1[:, c:c + 1],
                    in1=tl.pbS[:, :], op0=ALU.mult, op1=ALU.mult),
                    reads=[("y32", c), ("wln1c", 0), ("pbS",)], writes=[("tt", c)])
            for fo in range(8):
                pm = tl.pmm[fo % 2]
                pmk = ("pmm", fo % 2)
                for c in range(8):
                    p.emit("tensor", lambda e, c=c, fo=fo, pm=pm: e.matmul(
                        out=pm[:, :], lhsT=wq[:, c, fo * 128:(fo + 1) * 128], rhs=ttile[:, c, :],
                        start=(c == 0), stop=(c == 7)),
                        reads=[("wq", c), ("tt", c)], writes=[pmk])
                p.emit("vector", lambda e, fo=fo, pm=pm: e.tensor_copy(out=qsb[:, fo, :], in_=pm[:, :]),
                       reads=[pmk], writes=[("qsb", fo)])
            for ci_ in (0, 1):
                lo, hi = 4 + ci_, 6 + ci_
                p.emit("vector", lambda e, lo=lo, js=js: e.tensor_mul(out=tl.sq0[:, :], in0=qsb[:, lo, :], in1=cosb[:, js]),
                       reads=[("qsb", lo), ("cos",)], writes=[("sq", 0)])
                p.emit("vector", lambda e, hi=hi, js=js: e.tensor_mul(out=tl.sq1[:, :], in0=qsb[:, hi, :], in1=sinb[:, js]),
                       reads=[("qsb", hi), ("sin",)], writes=[("sq", 1)])
                p.emit("vector", lambda e, lo=lo: e.tensor_sub(out=qstg[:, lo, :], in0=tl.sq0[:, :], in1=tl.sq1[:, :]),
                       reads=[("sq", 0), ("sq", 1)], writes=[("qst", lo)])
                p.emit("vector", lambda e, lo=lo, js=js: e.tensor_mul(out=tl.sq0[:, :], in0=qsb[:, lo, :], in1=sinb[:, js]),
                       reads=[("qsb", lo), ("sin",)], writes=[("sq", 0)])
                p.emit("vector", lambda e, hi=hi, js=js: e.tensor_mul(out=tl.sq1[:, :], in0=qsb[:, hi, :], in1=cosb[:, js]),
                       reads=[("qsb", hi), ("cos",)], writes=[("sq", 1)])
                p.emit("vector", lambda e, hi=hi: e.tensor_add(out=qstg[:, hi, :], in0=tl.sq0[:, :], in1=tl.sq1[:, :]),
                       reads=[("sq", 0), ("sq", 1)], writes=[("qst", hi)])
            for fo in range(4):
                p.emit("vector", lambda e, fo=fo: e.tensor_copy(out=qstg[:, fo, :], in_=qsb[:, fo, :]),
                       reads=[("qsb", fo)], writes=[("qst", fo)])
            p.dma("gpsimd", Qre[:, :, js], qstg[:, :, :], reads=[("qst", fo) for fo in range(8)])
            pm = tl.pmm[2]
            pmk = ("pmm", 2)
            for c in range(8):
                p.emit("tensor", lambda e, c=c, pm=pm: e.matmul(
                    out=pm[:, :], lhsT=wkva[:, c, 0:128], rhs=ttile[:, c, :],
                    start=(c == 0), stop=(c == 7)),
                    reads=[("wkva", c), ("tt", c)], writes=[pmk])
            p.emit("vector", lambda e, pm=pm: e.tensor_copy(out=lat[:, :], in_=pm[:, :]),
                   reads=[pmk], writes=[("lat",)])
            for fo in range(4):
                pm2 = tl.pmm[fo % 2]
                pmk2 = ("pmm", fo % 2)
                p.emit("tensor", lambda e, fo=fo, pm2=pm2: e.matmul(
                    out=pm2[:, :], lhsT=wkvb[:, fo * 128:(fo + 1) * 128], rhs=lat[:, :],
                    start=True, stop=True),
                    reads=[("wkvb",), ("lat",)], writes=[pmk2])
                p.emit("vector", lambda e, fo=fo, pm2=pm2: e.tensor_copy(out=kvt[:, fo, :], in_=pm2[:, :]),
                       reads=[pmk2], writes=[("kvt", fo)])
            p.dma("gpsimd", KVre[:, :, js], kvt[:, :, :], reads=[("kvt", fo) for fo in range(4)])
            for c in range(8):
                p.emit("tensor", lambda e, c=c: e.matmul(
                    out=pRL, lhsT=wkva[:, c, 128:144], rhs=ttile[:, c, :],
                    start=(c == 0), stop=(c == 7)),
                    reads=[("wkva", c), ("tt", c)], writes=[("pmm", 0)])
            for c in range(8):
                p.emit("tensor", lambda e, c=c: e.matmul(
                    out=pRH, lhsT=wkva[:, c, 144:160], rhs=ttile[:, c, :],
                    start=(c == 0), stop=(c == 7)),
                    reads=[("wkva", c), ("tt", c)], writes=[("pmm", 1)])
            p.emit("vector", lambda e, js=js: e.tensor_mul(out=rs0[:, :], in0=pRL, in1=cosb[0:16, js]),
                   reads=[("pmm", 0), ("cos",)], writes=[("rs0",)])
            p.emit("vector", lambda e, js=js: e.tensor_mul(out=rs1[:, :], in0=pRH, in1=sinb[0:16, js]),
                   reads=[("pmm", 1), ("sin",)], writes=[("rs1",)])
            p.emit("vector", lambda e: e.tensor_sub(out=krl[:, :], in0=rs0[:, :], in1=rs1[:, :]),
                   reads=[("rs0",), ("rs1",)], writes=[("krl",)])
            p.emit("vector", lambda e, js=js: e.tensor_mul(out=rs0[:, :], in0=pRL, in1=sinb[0:16, js]),
                   reads=[("pmm", 0), ("sin",)], writes=[("rs0",)])
            p.emit("vector", lambda e, js=js: e.tensor_mul(out=rs1[:, :], in0=pRH, in1=cosb[0:16, js]),
                   reads=[("pmm", 1), ("cos",)], writes=[("rs1",)])
            p.emit("vector", lambda e: e.tensor_add(out=krh[:, :], in0=rs0[:, :], in1=rs1[:, :]),
                   reads=[("rs0",), ("rs1",)], writes=[("krh",)])
            p.dma("gpsimd", KRO[0:16, js], krl[:, :], reads=[("krl",)])
            p.dma("gpsimd", KRO[16:32, js], krh[:, :], reads=[("krh",)])

        tl.emit_ln(xsb, "x", 8, ybf, "ybf", y32_tile=y32t, y32name="y32", tile_cb=tile_cb)
    return prog


def mk_lm3(T=TPC, dbg=False):
    E, HH = 8, 512

    def prog(nc, p, es):
        X = nc.dram_tensor("x_fm", [D, T], F32, kind="ExternalInput").ap()
        Y = nc.dram_tensor("y_fm", [D, T], F32, kind="ExternalInput").ap()
        A = nc.dram_tensor("a_fm", [512, T], BF16, kind="ExternalInput").ap()
        WO = nc.dram_tensor("wo", [512, D], F32, kind="ExternalInput").ap()
        WLN2 = nc.dram_tensor("wln2", [128, 8], F32, kind="ExternalInput").ap()
        WGATE = nc.dram_tensor("wgate", [D, E], F32, kind="ExternalInput").ap()
        EB = nc.dram_tensor("ebias", [8, 1], F32, kind="ExternalInput").ap()
        W1 = nc.dram_tensor("w1", [E, D, 2 * HH], F32, kind="ExternalInput").ap()
        W2 = nc.dram_tensor("w2", [E, HH, D], F32, kind="ExternalInput").ap()
        WS1 = nc.dram_tensor("ws1", [D, 2 * HH], F32, kind="ExternalInput").ap()
        WS2 = nc.dram_tensor("ws2", [HH, D], F32, kind="ExternalInput").ap()
        IDF = nc.dram_tensor("ident_f32", [128, 128], F32, kind="ExternalInput").ap()
        O = nc.dram_tensor("xout_fm", [D, T], F32, kind="ExternalOutput").ap()
        UD = nc.dram_tensor("u_scratch", [D, T], F32, kind="Internal").ap()
        if dbg:
            OCW = nc.dram_tensor("cw_dbg", [128, T // 128, E], F32, kind="ExternalOutput").ap()

        tl = TL(nc, p, es, T=T)
        sb, ps = tl.sb, tl.ps
        njt = tl.njt
        nsub = T // 128
        idsb = sb("idsb", [128, 128], F32)
        p.dma("sync", idsb[:, :], IDF, writes=[("id",)])
        wgt = sb("wgt", [128, 8, E], F32)
        p.dma("sync", wgt[:, :, :], WGATE.rearrange("(c p) e -> p c e", p=128), writes=[("wgt",)])
        eb = sb("eb", [8, 1], F32)
        p.dma("sync", eb[:, :], EB, writes=[("eb",)])
        wln2 = tl.load_rows_f32("wln2c", WLN2, D)
        Ure = UD.rearrange("(c p) t -> c p t", p=128)
        Xre = X.rearrange("(c p) t -> c p t", p=128)
        Yre = Y.rearrange("(c p) t -> c p t", p=128)
        Ore = O.rearrange("(c p) t -> c p t", p=128)

        with contextlib.ExitStack() as es2:
            sb2 = lambda n, s, d: es2.enter_context(nc.sbuf_tensor(n, s, d))
            asb = sb2("a", [128, 4, T], BF16)
            p.dma("sync", asb[:, :, :], A.rearrange("(c p) t -> p c t", p=128),
                  writes=[("a", c) for c in range(4)])
            wosb = sb2("wo_bf", [128, 4, D], BF16)
            wstg = [sb2(f"wo_stg{i}", [128, D], F32) for i in range(2)]
            srcw = WO.rearrange("(c p) f -> c p f", p=128)
            for c in range(4):
                s = wstg[c % 2]
                sk = ("wo_stg", c % 2)
                p.dma("sync", s[:, :], srcw[c], writes=[sk])
                p.emit("vector", lambda e, s=s, c=c: e.tensor_copy(out=wosb[:, c, :], in_=s[:, :]),
                       reads=[sk], writes=[("wo", c)])
            yst = [sb2(f"yst{i}", [128, T], F32) for i in range(2)]
            ust = [sb2(f"ust{i}", [128, T], F32) for i in range(2)]
            for fo in range(8):
                bb = fo % 2
                yk = ("yst", bb)
                uk = ("ust", bb)
                p.dma("sync", yst[bb][:, :], Yre[fo], writes=[yk])
                for j in range(njt):
                    js = slice(j * 512, (j + 1) * 512)
                    pm = tl.pmm[j % 4]
                    pmk = ("pmm", j % 4)
                    for c in range(4):
                        p.emit("tensor", lambda e, c=c, fo=fo, js=js, pm=pm: e.matmul(
                            out=pm[:, :], lhsT=wosb[:, c, fo * 128:(fo + 1) * 128], rhs=asb[:, c, js],
                            start=(c == 0), stop=(c == 3)),
                            reads=[("wo", c), ("a", c)], writes=[pmk])
                    p.emit("vector", lambda e, js=js, bb=bb, pm=pm: e.tensor_add(
                        out=ust[bb][:, js], in0=yst[bb][:, js], in1=pm[:, :]),
                        reads=[pmk, yk], writes=[uk])
                p.dma("gpsimd", Ure[fo], ust[bb][:, :], reads=[uk])
        p.barrier()

        xnbf = tl.sbuf_fm("xn", D, T, BF16)
        zbf = tl.sbuf_fm("z", D, T, BF16)
        cw_all = sb("cw_all", [128, nsub, E], F32)
        lg = sb("lg", [16, 512], F32)
        gs = [sb(f"gs{i}", [128, E], F32) for i in range(5)]
        gr = [sb(f"gr{i}", [128, 1], F32) for i in range(4)]
        ut = sb("ut", [128, 8, 512], F32)

        def gate_tile(j):
            plg = tl.pmm[0]
            for c in range(8):
                p.emit("tensor", lambda e, c=c: e.matmul(
                    out=plg[0:E, :], lhsT=wgt[:, c, :], rhs=ut[:, c, :],
                    start=(c == 0), stop=(c == 7)),
                    reads=[("wgt",), ("ut", c)], writes=[("pmm", 0)])
            p.emit("vector", lambda e: e.tensor_scalar_add(out=lg[0:E, :], in0=plg[0:E, :], scalar1=eb[:, 0:1]),
                   reads=[("pmm", 0), ("eb",)], writes=[("lg",)])
            for i in range(4):
                ti = j * 4 + i
                ltm, ptm, g1, l2, cwtm = gs
                m1, nm1, ssum, m2 = gr
                ptf = tl.pmm[1]
                p.emit("tensor", lambda e, i=i: e.transpose(
                    out=ptf[0:128, 0:E], in_=lg[0:E, i * 128:(i + 1) * 128], identity=idsb[0:E, 0:E]),
                    reads=[("lg",), ("id",)], writes=[("pmm", 1)])
                p.emit("vector", lambda e: e.tensor_copy(out=ltm[:, 0:E], in_=ptf[0:128, 0:E]),
                       reads=[("pmm", 1)], writes=[("ltm",)])
                p.emit("vector", lambda e: e.reduce_max(out=m1[:, :], in_=ltm[:, 0:E], axis=AX.X),
                       reads=[("ltm",)], writes=[("m1",)])
                p.emit("vector", lambda e: e.tensor_scalar_mul(out=nm1[:, :], in0=m1[:, :], scalar1=-1.0),
                       reads=[("m1",)], writes=[("nm1",)])
                p.emit("scalar", lambda e: e.activation(out=ptm[:, 0:E], in_=ltm[:, 0:E], func=AF.Exp,
                                                        bias=nm1[:, 0:1], accum_out=ssum[:, 0:1]),
                       reads=[("ltm",), ("nm1",)], writes=[("ptm",), ("ssum",)])
                p.emit("vector", lambda e: e.reciprocal(out=ssum[:, :], in_=ssum[:, :]),
                       reads=[("ssum",)], writes=[("ssum",)])
                p.emit("vector", lambda e: e.tensor_scalar_mul(out=ptm[:, 0:E], in0=ptm[:, 0:E], scalar1=ssum[:, 0:1]),
                       reads=[("ptm",), ("ssum",)], writes=[("ptm",)])
                p.emit("vector", lambda e: e.tensor_scalar(out=g1[:, 0:E], in0=ltm[:, 0:E], scalar1=m1[:, 0:1],
                                                           scalar2=None, op0=ALU.is_ge),
                       reads=[("ltm",), ("m1",)], writes=[("g1",)])
                p.emit("vector", lambda e: e.scalar_tensor_tensor(out=l2[:, 0:E], in0=g1[:, 0:E], scalar=-1e30,
                                                                  in1=ltm[:, 0:E], op0=ALU.mult, op1=ALU.add),
                       reads=[("g1",), ("ltm",)], writes=[("l2",)])
                p.emit("vector", lambda e: e.reduce_max(out=m2[:, :], in_=l2[:, 0:E], axis=AX.X),
                       reads=[("l2",)], writes=[("m2",)])
                p.emit("vector", lambda e: e.tensor_scalar(out=g1[:, 0:E], in0=ltm[:, 0:E], scalar1=m2[:, 0:1],
                                                           scalar2=None, op0=ALU.is_ge),
                       reads=[("ltm",), ("m2",)], writes=[("g1",)])
                p.emit("vector", lambda e: e.tensor_mul(out=cwtm[:, 0:E], in0=ptm[:, 0:E], in1=g1[:, 0:E]),
                       reads=[("ptm",), ("g1",)], writes=[("cwtm",)])
                p.emit("vector", lambda e: e.reduce_sum(out=m2[:, :], in_=cwtm[:, 0:E], axis=AX.X),
                       reads=[("cwtm",)], writes=[("m2",)])
                p.emit("vector", lambda e: e.reciprocal(out=m2[:, :], in_=m2[:, :]),
                       reads=[("m2",)], writes=[("m2",)])
                p.emit("vector", lambda e, ti=ti: e.tensor_scalar_mul(out=cw_all[:, ti, :], in0=cwtm[:, 0:E], scalar1=m2[:, 0:1]),
                       reads=[("cwtm",), ("m2",)], writes=[("cw", ti)])

        epsap = tl.const(1e-5)
        ustr = [sb(f"ustr{i}", [128, 512], F32) for i in range(2)]
        for j in range(njt):
            js = slice(j * 512, (j + 1) * 512)
            for c in range(8):
                p.dma("sync", ut[:, c, :], Ure[c][:, js], writes=[("ut", c)])
            for c in range(8):
                sq = tl.sq0 if c % 2 == 0 else tl.sq1
                sqk = ("sq", c % 2)
                p.emit("scalar", lambda e, c=c, sq=sq: e.activation(out=sq[:, :], in_=ut[:, c, :], func=AF.Square),
                       reads=[("ut", c)], writes=[sqk])
                p.emit("tensor", lambda e, c=c, sq=sq: e.matmul(out=tl.prq[:, :], lhsT=tl.ones_col[:], rhs=sq[:, :],
                                                                start=(c == 0), stop=(c == 7)),
                       reads=[sqk, ("ones_col", 0)], writes=[("prq",)])
            p.emit("vector", lambda e: e.tensor_scalar_mul(out=tl.r1[:, :], in0=tl.prq[:, :], scalar1=1.0 / D),
                   reads=[("prq",)], writes=[("r1",)])
            p.emit("scalar", lambda e: e.activation(out=tl.r1[:, :], in_=tl.r1[:, :], func=AF.Sqrt, bias=epsap[0:1, :]),
                   reads=[("r1",), ("const", 1e-5)], writes=[("r1",)])
            p.emit("vector", lambda e: e.reciprocal(out=tl.r1[:, :], in_=tl.r1[:, :]),
                   reads=[("r1",)], writes=[("r1",)])
            p.emit("tensor", lambda e: e.matmul(out=tl.pbS[:, :], lhsT=tl.ones_row[:], rhs=tl.r1[:, :],
                                                start=True, stop=True),
                   reads=[("r1",), ("ones_row", 0)], writes=[("pbS",)])
            for c in range(8):
                p.emit("vector", lambda e, c=c, js=js: e.scalar_tensor_tensor(
                    out=xnbf[:, c, js], in0=ut[:, c, :], scalar=wln2[:, c:c + 1],
                    in1=tl.pbS[:, :], op0=ALU.mult, op1=ALU.mult),
                    reads=[("ut", c), ("wln2c", 0), ("pbS",)], writes=[("xn", c)])
                p.emit("vector", lambda e, c=c: e.scalar_tensor_tensor(
                    out=ut[:, c, :], in0=ut[:, c, :], scalar=wln2[:, c:c + 1],
                    in1=tl.pbS[:, :], op0=ALU.mult, op1=ALU.mult),
                    reads=[("ut", c), ("wln2c", 0), ("pbS",)], writes=[("ut", c)])
            gate_tile(j)
        if dbg:
            p.dma("gpsimd", OCW, cw_all[:, :, :], reads=[("cw", ti) for ti in range(nsub)])
        p.barrier()

        h = [tl.sbuf_fm(f"h{i}", HH, T, BF16) for i in range(2)]
        stg = [sb(f"stgw{i}", [128, 1024], F32) for i in range(2)]
        w1b = [sb(f"w1b{i}", [128, 8, 2 * HH], BF16) for i in range(2)]
        w2b = [sb(f"w2b{i}", [128, 4, D], BF16) for i in range(2)]
        cwb = [sb(f"cwb{i}", [128, 512], BF16) for i in range(2)]
        sg = [sb(f"sg{i}", [128, 512], F32) for i in range(2)]
        hwt = [sb(f"hwt{i}", [128, 512], BF16) for i in range(2)]
        nst = [0]

        def load_cast(dram_src, dst, dstkey, nchunk, ncols):
            for c in range(nchunk):
                s = stg[nst[0] % 2]
                skey = ("stgw", nst[0] % 2)
                nst[0] += 1
                p.dma("sync", s[:, 0:ncols], dram_src[c], writes=[skey])
                p.emit("vector", lambda e, s=s, c=c, ncols=ncols, dst=dst: e.tensor_copy(
                    out=dst[:, c, 0:ncols], in_=s[:, 0:ncols]),
                    reads=[skey], writes=[(dstkey, c)])

        def up_and_down(eidx, w1cur, w1key, w2cur, w2key, hcur, hkey, weighted, first):
            for j in range(njt):
                js = slice(j * 512, (j + 1) * 512)
                if weighted:
                    pcw = tl.pbS
                    for i in range(4):
                        ti = j * 4 + i
                        p.emit("tensor", lambda e, ti=ti, i=i, eidx=eidx: e.transpose(
                            out=pcw[:, i * 128:(i + 1) * 128],
                            in_=cw_all[:, ti, eidx:eidx + 1].broadcast_to([128, 128]),
                            identity=idsb[:, :]),
                            reads=[("cw", ti), ("id",)], writes=[("pbS",)])
                    cb = cwb[j % 2]
                    cbkey = ("cwb", j % 2)
                    p.emit("vector", lambda e, cb=cb: e.tensor_copy(out=cb[:, :], in_=pcw[:, :]),
                           reads=[("pbS",)], writes=[cbkey])
                for blk in range(4):
                    pu, pukey = tl.pmm[blk % 2], ("pmm", blk % 2)
                    pg, pgkey = tl.pmm[2 + blk % 2], ("pmm", 2 + blk % 2)
                    for c in range(8):
                        p.emit("tensor", lambda e, c=c, blk=blk, js=js, w1cur=w1cur, pu=pu: e.matmul(
                            out=pu[:, :], lhsT=w1cur[:, c, blk * 128:(blk + 1) * 128],
                            rhs=xnbf[:, c, js], start=(c == 0), stop=(c == 7)),
                            reads=[(w1key, c), ("xn", c)], writes=[pukey])
                    for c in range(8):
                        p.emit("tensor", lambda e, c=c, blk=blk, js=js, w1cur=w1cur, pg=pg: e.matmul(
                            out=pg[:, :], lhsT=w1cur[:, c, (4 + blk) * 128:(4 + blk + 1) * 128],
                            rhs=xnbf[:, c, js], start=(c == 0), stop=(c == 7)),
                            reads=[(w1key, c), ("xn", c)], writes=[pgkey])
                    sgc, sgkey = sg[blk % 2], ("sg", blk % 2)
                    p.emit("scalar", lambda e, sgc=sgc, pg=pg: e.activation(out=sgc[:, :], in_=pg[:, :], func=AF.Silu),
                           reads=[pgkey], writes=[sgkey])
                    if weighted:
                        ht, htkey = hwt[blk % 2], ("hwt", blk % 2)
                        p.emit("vector", lambda e, ht=ht, sgc=sgc, pu=pu: e.tensor_mul(out=ht[:, :], in0=pu[:, :], in1=sgc[:, :]),
                               reads=[pukey, sgkey], writes=[htkey])
                        p.emit("gpsimd", lambda e, ht=ht, blk=blk, js=js, hcur=hcur, cb=cb: e.tensor_mul(
                            out=hcur[:, blk, js], in0=ht[:, :], in1=cb[:, :]),
                            reads=[htkey, cbkey], writes=[(hkey, blk, j)])
                    else:
                        p.emit("vector", lambda e, blk=blk, js=js, hcur=hcur, sgc=sgc, pu=pu: e.tensor_mul(
                            out=hcur[:, blk, js], in0=pu[:, :], in1=sgc[:, :]),
                               reads=[pukey, sgkey], writes=[(hkey, blk, j)])
                for fo in range(8):
                    pz, pzkey = tl.pmm[(fo % 4)], ("pmm", fo % 4)
                    for blk in range(4):
                        p.emit("tensor", lambda e, blk=blk, fo=fo, js=js, w2cur=w2cur, hcur=hcur, pz=pz, j=j: e.matmul(
                            out=pz[:, :], lhsT=w2cur[:, blk, fo * 128:(fo + 1) * 128],
                            rhs=hcur[:, blk, js], start=(blk == 0), stop=(blk == 3)),
                            reads=[(w2key, blk), (hkey, blk, j)], writes=[pzkey])
                    if first:
                        p.emit("vector", lambda e, fo=fo, js=js, pz=pz: e.tensor_copy(out=zbf[:, fo, js], in_=pz[:, :]),
                               reads=[pzkey], writes=[("z", fo, j)])
                    else:
                        p.emit("vector", lambda e, fo=fo, js=js, pz=pz: e.tensor_add(out=zbf[:, fo, js], in0=zbf[:, fo, js], in1=pz[:, :]),
                               reads=[pzkey, ("z", fo, j)], writes=[("z", fo, j)])

        for eidx in range(E):
            b = eidx % 2
            load_cast(W1[eidx].rearrange("(c p) f -> c p f", p=128), w1b[b], f"w1b{b}", 8, 2 * HH)
            load_cast(W2[eidx].rearrange("(c p) f -> c p f", p=128), w2b[b], f"w2b{b}", 4, D)
            up_and_down(eidx, w1b[b], f"w1b{b}", w2b[b], f"w2b{b}", h[b], f"h{b}",
                        weighted=True, first=(eidx == 0))
        load_cast(WS1.rearrange("(c p) f -> c p f", p=128), w1b[0], "w1b0", 8, 2 * HH)
        load_cast(WS2.rearrange("(c p) f -> c p f", p=128), w2b[0], "w2b0", 4, D)
        up_and_down(0, w1b[0], "w1b0", w2b[0], "w2b0", h[1], "h1", weighted=False, first=False)

        xc = [sb(f"xc{i}", [128, 512], F32) for i in range(2)]
        uc = [sb(f"uc{i}", [128, 512], F32) for i in range(2)]
        for fo in range(8):
            for j in range(njt):
                js = slice(j * 512, (j + 1) * 512)
                b = j % 2
                xck, uck = ("xc", b), ("uc", b)
                p.dma("sync", xc[b][:, :], Xre[fo][:, js], writes=[xck])
                p.dma("sync", uc[b][:, :], Ure[fo][:, js], writes=[uck])
                p.emit("vector", lambda e, b=b: e.tensor_add(
                    out=xc[b][:, :], in0=xc[b][:, :], in1=uc[b][:, :]),
                    reads=[xck, uck], writes=[xck])
                p.emit("vector", lambda e, fo=fo, b=b, js=js, j=j: e.tensor_add(
                    out=xc[b][:, :], in0=xc[b][:, :], in1=zbf[:, fo, js]),
                    reads=[xck, ("z", fo, j)], writes=[xck])
                p.dma("gpsimd", Ore[fo][:, js], xc[b][:, :], reads=[xck])
    return prog


def mk_lb1(T=TPC):
    def prog(nc, p, es):
        X = nc.dram_tensor("x_fm", [D, T], F32, kind="ExternalInput").ap()
        W = nc.dram_tensor("w_qkv", [D, 3 * D], F32, kind="ExternalInput").ap()
        WGB = nc.dram_tensor("w_gb", [D, 32], F32, kind="ExternalInput").ap()
        BGB = nc.dram_tensor("b_gb", [32, 1], F32, kind="ExternalInput").ap()
        O = nc.dram_tensor("qkv_fm", [3 * D, T], F32, kind="ExternalOutput").ap()
        OG = nc.dram_tensor("gates_fm", [32, T], F32, kind="ExternalOutput").ap()
        tl = TL(nc, p, es, T=T)
        xsb = tl.load_fm("x", X, D, T, F32)
        wsb = tl.load_weight_bf("w", W, D, 3 * D)
        wgb = tl.sb("wgb", [128, 8, 32], F32)
        p.dma("sync", wgb[:, :, :], WGB.rearrange("(c p) g -> p c g", p=128), writes=[("wgb",)])
        bgb = tl.sb("bgb", [32, 1], F32)
        p.dma("sync", bgb[:, :], BGB, writes=[("bgb",)])
        ybf = tl.sbuf_fm("y", D, T, BF16)
        y32t = tl.sb("y32t", [128, 8, 512], F32)
        gt = tl.sb("gt", [32, 512], F32)

        def tile_cb(j):
            js = slice(j * 512, (j + 1) * 512)
            pg = tl.pmm[2]
            for c in range(8):
                p.emit("tensor", lambda e, c=c, pg=pg: e.matmul(
                    out=pg[0:32, :], lhsT=wgb[:, c, :], rhs=y32t[:, c, :],
                    start=(c == 0), stop=(c == 7)),
                    reads=[("wgb",), ("y32", c)], writes=[("pmm", 2)])
            p.emit("scalar", lambda e: e.activation(out=gt[:, :], in_=pg[0:32, :], func=AF.Sigmoid,
                                                    bias=bgb[:, 0:1]),
                   reads=[("pmm", 2), ("bgb",)], writes=[("gt",)])
            p.dma("gpsimd", OG[:, js], gt[:, :], reads=[("gt",)])

        tl.emit_ln(xsb, "x", 8, ybf, "y", y32_tile=y32t, y32name="y32", tile_cb=tile_cb)
        eo = [tl.sb(f"eo{i}", [128, 512], F32) for i in range(3)]
        Ore = O.rearrange("(fo p) t -> fo p t", p=128)
        ecnt = [0]

        def epi(fo, j, pmap, pkey):
            js = slice(j * tl.tt, (j + 1) * tl.tt)
            i = ecnt[0] % 3
            ecnt[0] += 1
            ek = ("eo", i)
            p.emit("vector", lambda e, i=i: e.tensor_copy(out=eo[i][:, :], in_=pmap),
                   reads=[pkey], writes=[ek])
            p.dma("gpsimd", Ore[fo][:, js], eo[i][:, :], reads=[ek])

        tl.emit_mm(ybf, "y", 8, wsb, "w", 3 * D, epi)
    return prog


def _slices(x):
    return [np.ascontiguousarray(x[c * TPC:(c + 1) * TPC].T) for c in range(NCORES)]


def _gather_x(res, key="xout_fm"):
    return np.concatenate([np.asarray(res[c][key]).T for c in range(NCORES)], axis=0)


_ID128 = np.eye(128, dtype=np.float32)
_PP, _QQ = np.meshgrid(np.arange(128), np.arange(512), indexing="ij")
_CAUSAL_MASKS = np.stack([np.where(_PP - _QQ > -128 * m, -1e30, 0.0) for m in range(4)]).astype(np.float32)


def _slopes16():
    start = 2.0 ** (-0.5)
    return np.array([start ** (i + 1) for i in range(16)], np.float32)


def _vtile(v, dv):
    S_ = v.shape[1]
    return np.ascontiguousarray(v.T.reshape(S_ // 128, 128, dv).transpose(1, 0, 2))


def _attn_pair_inputs(qkv_cores, masks, slopes=None, dv=64, vofs=2048, kofs=1024):
    H = 16
    qkv_b = [np.concatenate([qkv_cores[4 * b + c] for c in range(4)], axis=1) for b in range(B)]
    NKT = S // 128
    pos = np.arange(S, dtype=np.float32)
    ims = []
    for core in range(NCORES):
        qs, ks, vs, ramps, augs = [], [], [], [], []
        for pi in range(4 * core, 4 * core + 4):
            b, h = divmod(pi, H)
            qs.append(qkv_b[b][h * 64:(h + 1) * 64])
            ks.append(qkv_b[b][kofs + h * 64:kofs + h * 64 + 64])
            vs.append(_vtile(qkv_b[b][vofs + h * 64:vofs + h * 64 + dv].astype(BF), dv))
            if slopes is not None:
                ramps.append((slopes[h] * pos).reshape(NKT, 128).T.astype(np.float32))
                augs.append((-slopes[h] * pos)[None, :].astype(np.float32))
            else:
                ramps.append(np.zeros((128, NKT), np.float32))
                augs.append(np.zeros((1, S), np.float32))
        ims.append({"qp": np.ascontiguousarray(np.stack(qs)), "kp": np.ascontiguousarray(np.stack(ks)),
                    "vp": np.ascontiguousarray(np.stack(vs)),
                    "ramp": np.stack(ramps), "augq": np.stack(augs), "masks": masks})
    return ims


def _scatter_attnout(res, dv=64):
    H = 16
    ao = np.zeros((B, H * dv, S), dtype=BF)
    for core in range(NCORES):
        r = np.asarray(res[core]["attnout"])
        for i, pi in enumerate(range(4 * core, 4 * core + 4)):
            b, h = divmod(pi, H)
            ao[b, h * dv:(h + 1) * dv] = r[i]
    return ao


_PROGS = {}


def _prog(name, fn):
    if name not in _PROGS:
        _PROGS[name] = fn()
    return _PROGS[name]


def sub_l0a(x, inp):
    im1 = [{"x_fm": xs, "w_qkv": inp["L0_attn_a_W_qkv"], "b_qkv": bias_host(inp["L0_attn_a_b_qkv"])}
           for xs in _slices(x)]
    r1 = run_prog(_prog("qkv", mk_l0a_qkv), im1)
    qkv = [np.asarray(r1[c]["qkv_fm"]) for c in range(NCORES)]
    im2 = _attn_pair_inputs(qkv, _CAUSAL_MASKS, slopes=_slopes16())
    r2 = run_prog(_prog("flash64", lambda: mk_flash(S_=S, npair=4, dv=64)), im2)
    ao = _scatter_attnout(r2)
    aof = ao.transpose(0, 2, 1).reshape(NTOK, D)
    im3 = [{"x_fm": xs, "a_fm": np.ascontiguousarray(aof[c * TPC:(c + 1) * TPC].T),
            "w": inp["L0_attn_a_W_out"], "b": bias_host(inp["L0_attn_a_b_out"])}
           for c, xs in enumerate(_slices(x))]
    r3 = run_prog(_prog("residb", lambda: mk_resid_proj(use_bias=True)), im3)
    return _gather_x(r3)


def sub_l0ffn(x, inp):
    im = [{"x_fm": xs, "wg_t": np.ascontiguousarray(inp["L0_ffn_Wg"].T),
           "w1": inp["L0_ffn_W1"], "w2": inp["L0_ffn_W2"],
           "ws1": inp["L0_ffn_Ws1"], "ws2": inp["L0_ffn_Ws2"], "ident_f32": _ID128}
          for xs in _slices(x)]
    r = run_prog(_prog("l0ffn", mk_l0_ffn), im)
    return _gather_x(r)


def sub_l0b(x, inp):
    Wgb = np.concatenate([inp["L0_attn_b_Wg"], inp["L0_attn_b_Wb"]], axis=0)
    bgb = np.concatenate([inp["L0_attn_b_bg"], inp["L0_attn_b_bb"]]).astype(np.float32).reshape(32, 1)
    im1 = [{"x_fm": xs, "w_qkv": np.ascontiguousarray(inp["L0_attn_b_Wqkv"].T),
            "w_gb": np.ascontiguousarray(Wgb.T), "b_gb": bgb} for xs in _slices(x)]
    r1 = run_prog(_prog("lb1", mk_lb1), im1)
    qkv = [np.concatenate([np.asarray(r1[4 * b + c]["qkv_fm"]) for c in range(4)], axis=1) for b in range(B)]
    gts = [np.concatenate([np.asarray(r1[4 * b + c]["gates_fm"]) for c in range(4)], axis=1) for b in range(B)]
    mtris = np.tile(np.tril(np.ones((128, 128), np.float32), -1), (1, 4))
    mtriui = np.tile(np.triu(np.ones((128, 128), np.float32), 0), (1, 4))
    im2 = []
    for core in range(NCORES):
        qs, ks, vs, azs, bzs = [], [], [], [], []
        for pi in range(4 * core, 4 * core + 4):
            b, h = divmod(pi, 16)
            qs.append(qkv[b][h * 64:(h + 1) * 64])
            ks.append(qkv[b][1024 + h * 64:1024 + h * 64 + 64])
            vs.append(qkv[b][2048 + h * 64:2048 + h * 64 + 64])
            azs.append(gts[b][h:h + 1])
            bzs.append(gts[b][16 + h:16 + h + 1])
        im2.append({"qs": np.ascontiguousarray(np.stack(qs)), "ks": np.ascontiguousarray(np.stack(ks)),
                    "vs": np.ascontiguousarray(np.stack(vs)), "az": np.ascontiguousarray(np.stack(azs)),
                    "bz": np.ascontiguousarray(np.stack(bzs)),
                    "mtris": np.ascontiguousarray(mtris), "mtriui": np.ascontiguousarray(mtriui),
                    "ident_f32": _ID128})
    r2 = run_prog(_prog("scan", lambda: mk_l0b_scan(S_=S, npair=4)), im2)
    ao = np.zeros((B, S, D), dtype=BF)
    for core in range(NCORES):
        o = np.asarray(r2[core]["out_tm"])
        for i, pi in enumerate(range(4 * core, 4 * core + 4)):
            b, h = divmod(pi, 16)
            ao[b, :, h * 64:(h + 1) * 64] = o[i].reshape(S, 64)
    aof = ao.reshape(NTOK, D)
    im3 = [{"x_fm": xs, "a_fm": np.ascontiguousarray(aof[c * TPC:(c + 1) * TPC].T),
            "w": np.ascontiguousarray(inp["L0_attn_b_Wout"].T)} for c, xs in enumerate(_slices(x))]
    r3 = run_prog(_prog("resid", lambda: mk_resid_proj(use_bias=False)), im3)
    return _gather_x(r3)


def sub_l1a(x, inp):
    Wc = np.concatenate([inp["L1_attn_a_Wq"], inp["L1_attn_a_Wk"], inp["L1_attn_a_Wv"]], axis=1)
    bc = np.concatenate([inp["L1_attn_a_bq"], inp["L1_attn_a_bk"], inp["L1_attn_a_bv"]])
    im1 = [{"x_fm": xs, "w_qkv": np.ascontiguousarray(Wc), "b_qkv": bias_host(bc)} for xs in _slices(x)]
    r1 = run_prog(_prog("qkv", mk_l0a_qkv), im1)
    qkv = [np.asarray(r1[c]["qkv_fm"]) for c in range(NCORES)]
    masks = np.stack([np.where(np.abs(_PP - _QQ + 128 * (m - 2)) <= 256, 0.0, -1e30)
                      for m in range(8)]).astype(np.float32)
    im2 = _attn_pair_inputs(qkv, masks, slopes=None)
    NKT = S // 128
    krange = lambda qt: range(max(0, 4 * qt - 2), min(NKT - 1, 4 * qt + 5) + 1)
    maskid = lambda qt, kt: kt - 4 * qt + 2
    r2 = run_prog(_prog("band", lambda: mk_flash(S_=S, npair=4, dv=64, alibi=False,
                                                 krange=krange, maskid=maskid, nmasks=8)), im2)
    ao = _scatter_attnout(r2)
    aof = ao.transpose(0, 2, 1).reshape(NTOK, D)
    im3 = [{"x_fm": xs, "a_fm": np.ascontiguousarray(aof[c * TPC:(c + 1) * TPC].T),
            "w": inp["L1_attn_a_Wo"], "b": bias_host(inp["L1_attn_a_bo"])}
           for c, xs in enumerate(_slices(x))]
    r3 = run_prog(_prog("residb", lambda: mk_resid_proj(use_bias=True)), im3)
    return _gather_x(r3)


def _l1ffn_static(inp):
    THETA = 1e5
    permq = np.zeros(D, np.int64)
    for r_ in range(512):
        h, d = r_ // 32, r_ % 32
        permq[r_] = h * 64 + d
    for r_ in range(256):
        h, i = r_ // 16, r_ % 16
        permq[512 + r_] = h * 64 + 32 + 2 * i
        permq[768 + r_] = h * 64 + 33 + 2 * i
    Wq_p = np.ascontiguousarray(np.asarray(inp["L1_ffn_Wq"])[:, permq])
    permk = np.concatenate([np.arange(128), 128 + 2 * np.arange(16), 129 + 2 * np.arange(16)])
    Wkva_p = np.ascontiguousarray(np.asarray(inp["L1_ffn_Wkva"])[:, permk])
    fr = 1.0 / THETA ** (np.arange(0, 32, 2) / 32.0)
    cos_t, sin_t = [], []
    for c in range(NCORES):
        t0 = (c % 4) * TPC
        tpos = t0 + np.arange(TPC, dtype=np.float32)
        angr = np.array([fr[r_ % 16] * tpos for r_ in range(128)], np.float32)
        cos_t.append(np.cos(angr).astype(np.float32))
        sin_t.append(np.sin(angr).astype(np.float32))
    Wo_r = np.asarray(inp["L1_ffn_Wo"]).reshape(16, 64, D)
    Wo_sl = np.ascontiguousarray(Wo_r[:, :32, :].reshape(512, D))
    W1cat = np.ascontiguousarray(np.concatenate([inp["L1_ffn_Wu"], inp["L1_ffn_Wg"]], axis=2))
    WS1cat = np.ascontiguousarray(np.concatenate([inp["L1_ffn_sWu"][0], inp["L1_ffn_sWg"][0]], axis=1))
    return dict(Wq_p=Wq_p, Wkva_p=Wkva_p, cos_t=cos_t, sin_t=sin_t, Wo_sl=Wo_sl,
                W1cat=W1cat, WS1cat=WS1cat)


def sub_l1ffn(x, inp, st):
    im1 = [{"x_fm": xs, "wq": st["Wq_p"], "wkva": st["Wkva_p"], "wkvb": inp["L1_ffn_Wkvb"],
            "wln1": bias_host(inp["L1_ffn_w_ln1"]), "cosT": st["cos_t"][c], "sinT": st["sin_t"][c]}
           for c, xs in enumerate(_slices(x))]
    r1 = run_prog(_prog("lm1", mk_lm1), im1)
    NKT = S // 128
    qf = [np.concatenate([np.asarray(r1[4 * b + c]["q_out"]) for c in range(4)], axis=1) for b in range(B)]
    kvf = [np.concatenate([np.asarray(r1[4 * b + c]["kv_out"]) for c in range(4)], axis=1) for b in range(B)]
    krf = [np.concatenate([np.asarray(r1[4 * b + c]["krope_out"]) for c in range(4)], axis=1) for b in range(B)]
    im2 = []
    for core in range(NCORES):
        qs, ks, vs = [], [], []
        for pi in range(4 * core, 4 * core + 4):
            b, h = divmod(pi, 16)
            qs.append(np.concatenate([qf[b][h * 32:h * 32 + 32], qf[b][512 + h * 16:512 + h * 16 + 16],
                                      qf[b][768 + h * 16:768 + h * 16 + 16]], axis=0))
            ks.append(np.concatenate([kvf[b][h * 32:h * 32 + 32], krf[b]], axis=0))
            vs.append(_vtile(kvf[b][h * 32:h * 32 + 32], 32))
        im2.append({"qp": np.ascontiguousarray(np.stack(qs)), "kp": np.ascontiguousarray(np.stack(ks)),
                    "vp": np.ascontiguousarray(np.stack(vs)),
                    "ramp": np.zeros((4, 128, NKT), np.float32),
                    "augq": np.zeros((4, 1, S), np.float32), "masks": _CAUSAL_MASKS})
    r2 = run_prog(_prog("flash32", lambda: mk_flash(S_=S, npair=4, dv=32, alibi=False)), im2)
    ao = _scatter_attnout(r2, dv=32)
    im3 = []
    for c, xs in enumerate(_slices(x)):
        b = c // 4
        im3.append({"x_fm": xs, "y_fm": np.ascontiguousarray(np.asarray(r1[c]["y_out"])),
                    "a_fm": np.ascontiguousarray(ao[b][:, (c % 4) * TPC:(c % 4 + 1) * TPC]),
                    "wo": st["Wo_sl"], "wln2": bias_host(inp["L1_ffn_w_ln2"]),
                    "wgate": inp["L1_ffn_Wgate"],
                    "ebias": np.asarray(inp["L1_ffn_expert_bias"], np.float32).reshape(8, 1),
                    "w1": st["W1cat"], "w2": inp["L1_ffn_Wd"],
                    "ws1": st["WS1cat"], "ws2": inp["L1_ffn_sWd"][0], "ident_f32": _ID128})
    r3 = run_prog(_prog("lm3", mk_lm3), im3)
    return _gather_x(r3)


def sub_l1b(x, inp):
    Wc = np.concatenate([np.asarray(inp["L1_attn_b_Wq"]).T, np.asarray(inp["L1_attn_b_Wk"]).T,
                         np.asarray(inp["L1_attn_b_Wv"]).T], axis=1)
    im1 = [{"x_fm": xs, "w_qkv": np.ascontiguousarray(Wc),
            "b_qkv": np.zeros((128, 24), np.float32)} for xs in _slices(x)]
    r1 = run_prog(_prog("qkv", mk_l0a_qkv), im1)
    qkv_b = [np.concatenate([np.asarray(r1[4 * b + c]["qkv_fm"]) for c in range(4)], axis=1) for b in range(B)]
    Sq, HALO = TPC, 128
    Sk = Sq + 2 * HALO
    NKTk = Sk // 128
    NQB = Sq // 512
    im2 = []
    for core in range(NCORES):
        b, quad = core // 4, core % 4
        t0 = quad * Sq
        lo, hi = t0 - HALO, t0 + Sq + HALO
        def sl(rows):
            out = np.zeros((rows.shape[0], Sk), dtype=BF)
            a0, a1 = max(lo, 0), min(hi, S)
            out[:, a0 - lo:a1 - lo] = rows[:, a0:a1]
            return out
        qs, ks, vs = [], [], []
        for h in range(16):
            qs.append(qkv_b[b][h * 64:(h + 1) * 64, t0:t0 + Sq])
            ks.append(sl(qkv_b[b][1024 + h * 64:1024 + h * 64 + 64]))
            vs.append(_vtile(sl(qkv_b[b][2048 + h * 64:2048 + h * 64 + 64]), 64))
        masks = np.zeros((NQB * 6, 128, 512), np.float32)
        for qt in range(NQB):
            for m in range(6):
                kt = 4 * qt + m
                dd = 128 * kt - HALO - 512 * qt
                rel = _PP + dd - _QQ
                jg = t0 + 512 * qt + _QQ + rel
                ok = (rel >= -8) & (rel <= 7) & (jg >= 0) & (jg < S)
                masks[qt * 6 + m] = np.where(ok, 0.0, -1e30)
        im2.append({"qp": np.ascontiguousarray(np.stack(qs)), "kp": np.ascontiguousarray(np.stack(ks)),
                    "vp": np.ascontiguousarray(np.stack(vs)),
                    "ramp": np.zeros((16, 128, NKTk), np.float32),
                    "augq": np.zeros((16, 1, Sq), np.float32), "masks": masks})
    krange = lambda qt: range(4 * qt, 4 * qt + 6)
    maskid = lambda qt, kt: qt * 6 + (kt - 4 * qt)
    r2 = run_prog(_prog("win", lambda: mk_flash(S_=Sq, npair=16, dv=64, alibi=False,
                                                krange=krange, maskid=maskid, nmasks=NQB * 6, Sk=Sk)), im2)
    im3 = []
    for c, xs in enumerate(_slices(x)):
        a = np.asarray(r2[c]["attnout"]).reshape(1024, Sq)
        im3.append({"x_fm": xs, "a_fm": np.ascontiguousarray(a),
                    "w": np.ascontiguousarray(np.asarray(inp["L1_attn_b_Wo"]).T)})
    r3 = run_prog(_prog("resid", lambda: mk_resid_proj(use_bias=False)), im3)
    return _gather_x(r3)


def kernel(**inputs):
    x = np.ascontiguousarray(np.asarray(inputs["L0_attn_a_x"], np.float32).reshape(NTOK, D))
    st = _l1ffn_static(inputs)
    x = sub_l0a(x, inputs)
    x = sub_l0ffn(x, inputs)
    x = sub_l0b(x, inputs)
    x = sub_l0ffn(x, inputs)
    x = sub_l1a(x, inputs)
    x = sub_l1ffn(x, inputs, st)
    x = sub_l1b(x, inputs)
    x = sub_l1ffn(x, inputs, st)
    return x.reshape(B, S, D).astype(np.float32)
